# version 15
# baseline (speedup 1.0000x reference)
import numpy as np
import ml_dtypes

import concourse.bass as bass
import concourse.mybir as mybir
from concourse.bass_utils import run_bass_kernel_spmd
from concourse.aot_env import allow_ici_all_to_all
from contextlib import ExitStack

F32 = mybir.dt.float32
BF16 = mybir.dt.bfloat16
AF = mybir.ActivationFunctionType
ALU = mybir.AluOpType
AX = mybir.AxisListType.X
bf16 = ml_dtypes.bfloat16

NCORES = 8
T = 2048
D = 1024
NT = T // 128

ENGINES = ("tensor", "vector", "scalar", "gpsimd", "sync")
DMA_QUEUES = ("sync", "scalar", "gpsimd")


class H:
    __slots__ = ("eng", "dma", "val")

    def __init__(self, eng, dma, val):
        self.eng, self.dma, self.val = eng, dma, val


class KB:

    NLANES = 8

    def __init__(self, nc):
        self.nc = nc
        self.steps = []
        self.ccount = {e: 0 for e in ENGINES}
        self.dcount = {q: 0 for q in DMA_QUEUES}
        self.cc_count = 0

    @staticmethod
    def _flat(deps):
        out = []
        for d in deps:
            if d is None:
                continue
            if isinstance(d, (list, tuple)):
                out.extend(x for x in d if x is not None)
            else:
                out.append(d)
        return out

    def emit(self, eng, fn, deps=()):
        deps = self._flat(deps)
        self.ccount[eng] += 1
        self.steps.append((eng, fn, deps, False, False))
        return H(eng, False, self.ccount[eng])

    def dma(self, q, out, in_, deps=(), **kw):
        deps = self._flat(deps)
        idx = self.dcount[q]
        self.dcount[q] += 1
        lane = idx % self.NLANES
        if idx >= self.NLANES:
            deps = deps + [H((q, lane), True, idx // self.NLANES)]
        self.steps.append(
            ((q, lane), lambda e, o=out, i=in_, k=kw: e.dma_start(out=o, in_=i, **k),
             deps, True, False))
        return H((q, lane), True, idx // self.NLANES + 1)

    def collective(self, kind, ins, outs, deps=()):
        deps = self._flat(deps)
        cid = self.cc_count
        self.cc_count += 1

        def fn(e):
            with allow_ici_all_to_all():
                return e.collective_compute(
                    kind, ALU.bypass, replica_groups=[list(range(NCORES))],
                    ins=[a.opt() for a in ins], outs=[a.opt() for a in outs])
        self.steps.append((("cc", cid), fn, deps, False, True))
        return H(("cc", cid), False, 1)

    def barrier(self):
        hs = [H(e, False, self.ccount[e]) for e in ENGINES if self.ccount[e]]
        for q in DMA_QUEUES:
            n = self.dcount[q]
            for lane in range(min(n, self.NLANES)):
                cnt = n // self.NLANES + (1 if lane < n % self.NLANES else 0)
                hs.append(H((q, lane), True, cnt))
        for cid in range(self.cc_count):
            hs.append(H(("cc", cid), False, 1))
        return hs

    def finalize(self, block, es):
        nc = self.nc
        sem_c = {e: es.enter_context(nc.semaphore(f"sc_{e}")) for e in ENGINES}
        sem_d = {}
        for q in DMA_QUEUES:
            for lane in range(min(self.dcount[q], self.NLANES)):
                sem_d[(q, lane)] = es.enter_context(nc.semaphore(f"sd_{q}{lane}"))
        sem_cc = {cid: es.enter_context(nc.semaphore(f"scc{cid}"))
                  for cid in range(self.cc_count)}

        def sem_tgt(h):
            if isinstance(h.eng, tuple) and h.eng[0] == "cc":
                return sem_cc[h.eng[1]], h.val, h.eng
            if h.dma:
                return sem_d[h.eng], 16 * h.val, h.eng
            return sem_c[h.eng], h.val, h.eng

        per_eng = {e: [] for e in ENGINES}
        for (eng, fn, deps, is_dma, is_cc) in self.steps:
            if is_dma:
                q, lane = eng
                per_eng[q].append((fn, deps, is_dma, False, lane))
            elif is_cc:
                per_eng["gpsimd"].append((fn, deps, False, eng[1], None))
            else:
                per_eng[eng].append((fn, deps, False, False, None))

        def make_body(name):
            def body(engobj):
                waited = {}
                for fn, deps, is_dma, cc_id, lane in per_eng[name]:
                    for d in deps:
                        if (name == "tensor" and not isinstance(d.eng, tuple)
                                and d.eng == "tensor" and not d.dma):
                            continue
                        sem, tgt, key = sem_tgt(d)
                        if waited.get(key, -1) < tgt:
                            engobj.wait_ge(sem, tgt)
                            waited[key] = tgt
                    inst = fn(engobj)
                    if cc_id is not False and cc_id is not None:
                        inst.then_inc(sem_cc[cc_id], 1)
                    elif is_dma:
                        inst.then_inc(sem_d[(name, lane)], 16)
                    else:
                        inst.then_inc(sem_c[name], 1)
            return body

        for name in ENGINES:
            getattr(block, name)(make_body(name))



class Prog:

    def __init__(self, nc, es, kb):
        self.nc, self.es, self.kb = nc, es, kb
        self.uid = 0
        self.scopes = [es]
        self.phase_dep = []

    def _n(self, base):
        self.uid += 1
        return f"{base}_{self.uid}"

    def push(self):
        st = ExitStack()
        self.scopes.append(st)

    def pop(self):
        self.scopes.pop().close()
        self.phase_dep = self.kb.barrier()

    def sbuf(self, name, shape, dt):
        return self.scopes[-1].enter_context(self.nc.sbuf_tensor(self._n(name), shape, dt))

    def psum(self, name, shape, dt=F32):
        return self.scopes[-1].enter_context(self.nc.psum_tensor(self._n(name), shape, dt))

    def emit(self, eng, fn, deps=()):
        return self.kb.emit(eng, fn, list(deps) + self.phase_dep)

    def dma(self, q, out, in_, deps=(), **kw):
        return self.kb.dma(q, out, in_, list(deps) + self.phase_dep, **kw)

    def collective(self, kind, ins, outs, deps=()):
        return self.kb.collective(kind, ins, outs, list(deps) + self.phase_dep)

    def barrier(self):
        return self.kb.barrier()

    def din(self, name, shape, dt=F32):
        return self.nc.dram_tensor(name, list(shape), dt, kind="ExternalInput").ap()

    def dout(self, name, shape, dt=F32):
        return self.nc.dram_tensor(name, list(shape), dt, kind="ExternalOutput").ap()

    def dint(self, name, shape, dt=F32):
        return self.nc.dram_tensor(name, list(shape), dt).ap()


class Ring:

    def __init__(self, bufs):
        self.bufs = bufs
        self.n = len(bufs)
        self.i = 0
        self.last_user = [[] for _ in range(self.n)]

    def next(self):
        s = self.i % self.n
        self.i += 1
        u = self.last_user[s]
        self.last_user[s] = []
        return self.bufs[s], u, s

    def used(self, slot, h):
        if isinstance(h, (list, tuple)):
            self.last_user[slot].extend(x for x in h if x is not None)
        elif h is not None:
            self.last_user[slot].append(h)


def sbuf_ring(pr, name, n, shape, dt):
    return Ring([pr.sbuf(f"{name}{i}", shape, dt) for i in range(n)])


def psum_ring(pr, name, n, shape, dt=F32):
    return Ring([pr.psum(f"{name}{i}", shape, dt) for i in range(n)])



class Consts:

    def __init__(self, pr):
        kb = pr
        self.ident = pr.sbuf("c_ident", [128, 128], BF16)
        self.identf = pr.sbuf("c_identf", [128, 128], F32)
        self.ones_col = pr.sbuf("c_ones1", [1, 128], BF16)
        self.u64 = pr.sbuf("c_u64", [64, 64], F32)
        self.u128 = pr.sbuf("c_u128", [128, 128], F32)
        self.maskb = pr.sbuf("c_maskb", [128, 128], F32)
        d_ident = pr.din("c_ident", [128, 128], BF16)
        d_identf = pr.din("c_identf", [128, 128], F32)
        d_u64 = pr.din("c_u64", [64, 64], F32)
        d_u128 = pr.din("c_u128", [128, 128], F32)
        d_maskb = pr.din("c_maskb", [128, 128], F32)
        self.blockones_ap = pr.din("c_blockones", [32, 4096], F32)
        h0 = kb.dma("sync", self.ident[:, :], d_ident)
        h0b = kb.dma("sync", self.identf[:, :], d_identf)
        h1 = kb.dma("sync", self.u64[:, :], d_u64)
        h1b = kb.dma("sync", self.u128[:, :], d_u128)
        h2 = kb.dma("sync", self.maskb[:, :], d_maskb)
        self.hones = kb.emit("vector", lambda e: e.memset(self.ones_col[:, :], 1.0))
        self.ready = [h0, h0b, h1, h1b, h2, self.hones]

    @staticmethod
    def host_inputs():
        bo = np.zeros((32, 4096), dtype=np.float32)
        for h in range(32):
            bo[h, 128 * h:128 * (h + 1)] = 1.0
        return {
            "c_ident": np.eye(128, dtype=bf16),
            "c_identf": np.eye(128, dtype=np.float32),
            "c_u64": np.triu(np.ones((64, 64), dtype=np.float32)),
            "c_u128": np.triu(np.ones((128, 128), dtype=np.float32)),
            "c_maskb": np.where(np.arange(128)[:, None] <= np.arange(128)[None, :],
                                np.float32(0.0), np.float32(-1e9)),
            "c_blockones": bo,
        }



def emit_ln(pr, C, x_in_ap, xnT, xn_dram=None, tag="ln"):
    kb = pr
    xr = sbuf_ring(pr, f"{tag}_x", 3, [128, D], F32)
    nr = sbuf_ring(pr, f"{tag}_n", 2, [128, D], BF16)
    sq = pr.sbuf(f"{tag}_sq", [128, D], F32)
    st = sbuf_ring(pr, f"{tag}_st", 2, [128, 8], F32)
    pst = psum_ring(pr, f"{tag}_pst", 2, [128, 128], BF16)
    last = []
    prev_sqred = [None]
    for t in range(NT):
        xb, w, xs = xr.next()
        hl = kb.dma("sync", xb[:, :], x_in_ap[128 * t:128 * (t + 1), :], deps=[w])
        sb, ws, ss = st.next()
        h1 = kb.emit("vector", lambda e, xb=xb, sb=sb: e.reduce_sum(sb[:, 0:1], xb[:, :], AX),
                     deps=[hl, ws])
        h2a = kb.emit("scalar", lambda e, xb=xb: e.activation(
            sq[:, :], xb[:, :], AF.Square), deps=[hl, prev_sqred[0]])
        h2 = kb.emit("vector", lambda e, sb=sb: e.reduce_sum(sb[:, 1:2], sq[:, :], AX),
                     deps=[h2a, ws])
        prev_sqred[0] = h2

        h3a = kb.emit("vector", lambda e, sb=sb: e.tensor_scalar_mul(sb[:, 2:3], sb[:, 0:1], 1.0 / D), deps=[h1])
        h3b = kb.emit("vector", lambda e, sb=sb: e.tensor_tensor(sb[:, 3:4], sb[:, 2:3], sb[:, 2:3], ALU.mult), deps=[h3a])
        h3c = kb.emit("vector", lambda e, sb=sb: e.tensor_scalar(
            sb[:, 4:5], sb[:, 1:2], 1.0 / D, sb[:, 3:4], ALU.mult, ALU.subtract), deps=[h2, h3b])
        h3 = kb.emit("vector", lambda e, sb=sb: e.tensor_scalar_add(sb[:, 5:6], sb[:, 4:5], 1e-5), deps=[h3c])
        h4 = kb.emit("scalar", lambda e, sb=sb: e.sqrt(sb[:, 6:7], sb[:, 5:6]), deps=[h3])
        h5 = kb.emit("vector", lambda e, sb=sb: e.reciprocal(sb[:, 7:8], sb[:, 6:7]), deps=[h4])
        nb, wn, ns = nr.next()
        h6 = kb.emit("vector", lambda e, xb=xb, nb=nb, sb=sb: e.tensor_scalar(
            nb[:, :], xb[:, :], sb[:, 2:3], sb[:, 7:8], ALU.subtract, ALU.mult),
            deps=[h3a, h5, wn])
        xr.used(xs, [h2, h6])
        st.used(ss, h6)
        if xn_dram is not None:
            hsp = kb.dma("scalar", xn_dram[128 * t:128 * (t + 1), :], nb[:, :], deps=[h6])
        else:
            hsp = None
        evs = []
        for k in range(D // 128):
            pb, wp, ps_ = pst.next()
            ht = kb.emit("tensor", lambda e, nb=nb, pb=pb, k=k: e.transpose(
                pb[:, :], nb[:, 128 * k:128 * (k + 1)], C.ident[:, :]),
                deps=[h6, wp] + (C.ready if t == 0 and k == 0 else []))
            he = kb.emit("scalar", lambda e, pb=pb, k=k, t=t: e.copy(
                xnT[:, T * k + 128 * t: T * k + 128 * (t + 1)], pb[:, :]), deps=[ht])
            pst.used(ps_, he)
            evs.append(he)
        nr.used(ns, evs[-1])
        last.append(evs[-1])
        if hsp is not None:
            last.append(hsp)
    return last



def emit_router(pr, C, xnT, xnT_dep, rw_dram, rb_dram, kind, tag, comb_sb, combT):
    kb = pr
    E = 8
    rw = pr.sbuf(f"{tag}_rw", [128, 8 * E], BF16)
    hrw = kb.dma("gpsimd", rw[:, :].rearrange("p (k e) -> p k e", e=E),
                 rw_dram.rearrange("(k p) e -> p k e", p=128))
    onesf = pr.sbuf(f"{tag}_onesf", [1, 128], F32)
    h_onesf = kb.emit("vector", lambda e: e.memset(onesf[:, :], 1.0))
    bb = pr.sbuf(f"{tag}_bb", [128, E], F32)
    pg = pr.psum(f"{tag}_pg", [128, E], F32)
    if rb_dram is not None:
        rb = pr.sbuf(f"{tag}_rb", [1, E], F32)
        hrb = kb.dma("sync", rb[:, :], rb_dram.unsqueeze(0))
        hb1 = kb.emit("tensor", lambda e: e.matmul(pg[:, :], onesf[:, :], rb[:, :],
                                                   start=True, stop=True),
                      deps=[hrb, h_onesf])
        hbb = kb.emit("vector", lambda e: e.tensor_copy(bb[:, :], pg[:, :]), deps=[hb1])
    else:
        hbb = kb.emit("vector", lambda e: e.memset(bb[:, :], 0.0))
    gsc = sbuf_ring(pr, f"{tag}_gs", 2, [128, 3 * E + 8], F32)
    cbf = sbuf_ring(pr, f"{tag}_cbf", 2, [128, E], BF16)
    pT = psum_ring(pr, f"{tag}_pT", 2, [E, 128], BF16)
    comb_h = []
    combT_h = []
    prev_pg = [hbb]
    for t in range(NT):
        hmm = kb.emit("tensor", lambda e, t=t: [
            e.matmul(pg[:, :], xnT[:, T * k + 128 * t:T * k + 128 * (t + 1)],
                     rw[:, E * k:E * (k + 1)], start=(k == 0), stop=(k == 7))
            for k in range(8)][-1], deps=[hrw, xnT_dep] + prev_pg)
        gb, wg, sg = gsc.next()
        G = slice(0, E)
        E1 = slice(E, 2 * E)
        MK = slice(2 * E, 3 * E)
        s0, s1, s2, s3, s4, s5 = (slice(3 * E + i, 3 * E + i + 1) for i in range(6))
        h0 = kb.emit("vector", lambda e, gb=gb: e.tensor_tensor(gb[:, G], pg[:, :], bb[:, :], ALU.add),
                     deps=[hmm, wg])
        prev_pg = [h0]
        if kind == "sigmoid_top2":
            h0 = kb.emit("scalar", lambda e, gb=gb: e.activation(gb[:, G], gb[:, G], AF.Sigmoid),
                         deps=[h0])
            prev_pg = [h0]
        hm1 = kb.emit("vector", lambda e, gb=gb: e.reduce_max(gb[:, s0], gb[:, G], AX), deps=[h0])
        he1 = kb.emit("vector", lambda e, gb=gb: e.tensor_scalar(
            gb[:, E1], gb[:, G], gb[:, s0], None, ALU.is_equal), deps=[hm1])
        hmk = kb.emit("vector", lambda e, gb=gb: e.scalar_tensor_tensor(
            gb[:, MK], gb[:, E1], -1e30, gb[:, G], ALU.mult, ALU.add), deps=[he1])
        hm2 = kb.emit("vector", lambda e, gb=gb: e.reduce_max(gb[:, s1], gb[:, MK], AX), deps=[hmk])
        he2 = kb.emit("vector", lambda e, gb=gb: e.tensor_scalar(
            gb[:, MK], gb[:, MK], gb[:, s1], None, ALU.is_equal), deps=[hm2])
        if kind == "softmax_top2":
            hd = kb.emit("vector", lambda e, gb=gb: e.tensor_tensor(
                gb[:, s2], gb[:, s1], gb[:, s0], ALU.subtract), deps=[hm2])
            hx = kb.emit("scalar", lambda e, gb=gb: e.activation(gb[:, s3], gb[:, s2], AF.Exp), deps=[hd])
            hq = kb.emit("vector", lambda e, gb=gb: e.tensor_scalar_add(gb[:, s4], gb[:, s3], 1.0), deps=[hx])
            hp1 = kb.emit("vector", lambda e, gb=gb: e.reciprocal(gb[:, s2], gb[:, s4]), deps=[hq])
            hp2 = kb.emit("vector", lambda e, gb=gb: e.tensor_tensor(gb[:, s5], gb[:, s3], gb[:, s2], ALU.mult), deps=[hp1])
        else:
            hq = kb.emit("vector", lambda e, gb=gb: e.tensor_scalar(
                gb[:, s4], gb[:, s1], 1e-6, gb[:, s0], ALU.add, ALU.add), deps=[hm2])
            hr0 = kb.emit("vector", lambda e, gb=gb: e.reciprocal(gb[:, s3], gb[:, s4]), deps=[hq])
            hp1 = kb.emit("vector", lambda e, gb=gb: e.tensor_tensor(gb[:, s2], gb[:, s0], gb[:, s3], ALU.mult), deps=[hr0])
            hp2 = kb.emit("vector", lambda e, gb=gb: e.tensor_tensor(gb[:, s5], gb[:, s1], gb[:, s3], ALU.mult), deps=[hr0])
        hc1 = kb.emit("vector", lambda e, gb=gb, t=t: e.tensor_scalar(
            comb_sb[:, E * t:E * (t + 1)], gb[:, E1], gb[:, s2], None, ALU.mult),
            deps=[he1, hp1])
        hcm = kb.emit("vector", lambda e, gb=gb, t=t: e.scalar_tensor_tensor(
            comb_sb[:, E * t:E * (t + 1)], gb[:, MK], gb[:, s5],
            comb_sb[:, E * t:E * (t + 1)], ALU.mult, ALU.add), deps=[hc1, he2, hp2])
        comb_h.append(hcm)
        cb, wc, sc = cbf.next()
        hcb = kb.emit("vector", lambda e, cb=cb, t=t: e.tensor_copy(
            cb[:, :], comb_sb[:, E * t:E * (t + 1)]), deps=[hcm, wc])
        pb, wp, sp = pT.next()
        htr = kb.emit("tensor", lambda e, cb=cb, pb=pb: e.transpose(pb[:, :], cb[:, :], C.ident[:, :]),
                      deps=[hcb, wp, C.ready])
        hct = kb.emit("scalar", lambda e, pb=pb, t=t: e.copy(combT[0:E, 128 * t:128 * (t + 1)], pb[:, :]),
                      deps=[htr])
        pT.used(sp, hct)
        cbf.used(sc, htr)
        gsc.used(sg, [hc1, hcm])
        combT_h.append(hct)
    return comb_h, combT_h


def emit_moe(pr, C, xnT, xnT_dep, cfg, x_in_ap, x_out_ap, s_dram, tag,
             dbg_g=None, bg=None):
    kb = pr
    E = 8
    comb_sb = pr.sbuf(f"{tag}_comb", [128, NT * E], F32)
    combT = pr.sbuf(f"{tag}_combT", [8, T], BF16)
    pr.push()
    comb_h, combT_h = emit_router(
        pr, C, xnT, xnT_dep, cfg["router_w"], cfg.get("router_b"),
        cfg["router_kind"], tag + "r", comb_sb, combT)
    pr.pop()
    act = cfg["act"]
    g_acc = pr.sbuf(f"{tag}_gacc", [128, NT * D], F32)
    hz = kb.emit("vector", lambda e: e.memset(g_acc[:, :], 0.0))
    g_last = [hz] * NT
    w1sb = pr.sbuf(f"{tag}_w1", [128, 8 * 2048], BF16)
    w2sb = pr.sbuf(f"{tag}_w2", [128, 16 * 1024], BF16)
    b1sb = pr.sbuf(f"{tag}_b1", [128, 16], F32)
    h_sb = pr.sbuf(f"{tag}_h", [128, 16 * 512], BF16)
    ph = psum_ring(pr, f"{tag}_ph", 3, [128, 512], F32)
    py = psum_ring(pr, f"{tag}_py", 2, [128, 1024], F32)
    w1_war = [None]
    w2_war = [None]
    b1_war = [None]
    h_war = [[]]
    for ei, ex in enumerate(cfg["experts"]):
        lw1 = [kb.dma("gpsimd", w1sb[:, 2048 * kc:2048 * (kc + 1)],
                      ex["w1"][128 * kc:128 * (kc + 1), :], deps=[w1_war[0]])
               for kc in range(8)]
        lw2 = [kb.dma("gpsimd", w2sb[:, 1024 * hc:1024 * (hc + 1)],
                      ex["w2"][128 * hc:128 * (hc + 1), :], deps=[w2_war[0]])
               for hc in range(16)]
        if ex.get("b1") is not None:
            lb1 = kb.dma("scalar", b1sb[:, :],
                         ex["b1"].rearrange("(c p) -> p c", p=128),
                         deps=[b1_war[0]], allow_slow_non_contiguous=True)
        else:
            lb1 = kb.emit("vector", lambda e: e.memset(b1sb[:, :], 0.0),
                          deps=[b1_war[0]])
        last_h_step = None
        last_y_step = None
        he = None
        for tq in range(4):
            hevs = []
            for hc in range(16):
                phb, wph, sph = ph.next()
                hmmh = kb.emit("tensor", lambda e, phb=phb, hc=hc, tq=tq: [
                    e.matmul(phb[:, :],
                             w1sb[:, 2048 * kc + 128 * hc:2048 * kc + 128 * (hc + 1)],
                             xnT[:, T * kc + 512 * tq:T * kc + 512 * (tq + 1)],
                             start=(kc == 0), stop=(kc == 7)) for kc in range(8)][-1],
                    deps=lw1 + [xnT_dep, wph])
                he = kb.emit("scalar", lambda e, phb=phb, hc=hc: e.activation(
                    h_sb[:, 512 * hc:512 * (hc + 1)], phb[:, :], act,
                    bias=b1sb[:, hc:hc + 1]),
                    deps=[hmmh, lb1] + (h_war[0] if hc == 0 else []))
                ph.used(sph, he)
                hevs.append(he)
                last_h_step = hmmh
            ysteps = []
            for tl in range(4):
                tile = 4 * tq + tl
                pyb, wpy, spy = py.next()
                hy = kb.emit("tensor", lambda e, pyb=pyb, tl=tl: [
                    e.matmul(pyb[:, 512 * nh:512 * (nh + 1)],
                             h_sb[:, 512 * hc + 128 * tl:512 * hc + 128 * (tl + 1)],
                             w2sb[:, 1024 * hc + 512 * nh:1024 * hc + 512 * (nh + 1)],
                             start=(hc == 0), stop=(hc == 15))
                    for hc in range(16) for nh in range(2)][-1],
                    deps=hevs + lw2 + [wpy])
                comb = ex["comb"]
                if comb[0] == "col":
                    eidx = comb[1]
                    hep = kb.emit("vector", lambda e, pyb=pyb, tile=tile, eidx=eidx: e.scalar_tensor_tensor(
                        g_acc[:, D * tile:D * (tile + 1)], pyb[:, :],
                        comb_sb[:, E * tile + eidx:E * tile + eidx + 1],
                        g_acc[:, D * tile:D * (tile + 1)], ALU.mult, ALU.add),
                        deps=[hy, comb_h[tile], g_last[tile]])
                else:
                    val = float(comb[1])
                    hep = kb.emit("vector", lambda e, pyb=pyb, tile=tile, val=val: e.scalar_tensor_tensor(
                        g_acc[:, D * tile:D * (tile + 1)], pyb[:, :], val,
                        g_acc[:, D * tile:D * (tile + 1)], ALU.mult, ALU.add),
                        deps=[hy, g_last[tile]])
                g_last[tile] = hep
                py.used(spy, hep)
                ysteps.append(hy)
                last_y_step = hy
            h_war[0] = ysteps
            if bg:
                bg.pump(6)
        w1_war[0] = last_h_step
        w2_war[0] = last_y_step
        b1_war[0] = he
    nbias = len(cfg["bias_items"])
    bsb = []
    for bi, item in enumerate(cfg["bias_items"]):
        if item[0] == "comb":
            b2 = pr.sbuf(f"{tag}_b2{bi}", [E, D], BF16)
            hb = kb.dma("gpsimd", b2[:, :], item[1])
            bsb.append(("comb", b2, hb))
        else:
            S = item[1].shape[0]
            b2 = pr.sbuf(f"{tag}_sb2{bi}", [S, D], BF16)
            hb = kb.dma("gpsimd", b2[:, :], item[1])
            row = pr.sbuf(f"{tag}_row{bi}", [S, 128], BF16)
            val = float(item[2])
            hrow = kb.emit("vector", lambda e, row=row, val=val: e.memset(row[:, :], val))
            bsb.append(("mean", b2, hb, row, hrow, S))
    xr = sbuf_ring(pr, f"{tag}_xr", 1, [128, D], F32)
    sr = sbuf_ring(pr, f"{tag}_sr", 1, [128, D], BF16)
    gt = sbuf_ring(pr, f"{tag}_gt", 1, [128, D], F32)
    out_h = []
    for tile in range(NT):
        pyb, wpy, spy = py.next()

        def bias_mms(e, pyb=pyb, tile=tile):
            n_items = len(bsb)
            inst = None
            for ii, item in enumerate(bsb):
                for nh in range(2):
                    if item[0] == "comb":
                        _, b2, hb = item
                        lhsT = combT[0:E, 128 * tile:128 * (tile + 1)]
                        rhs = b2[0:E, 512 * nh:512 * (nh + 1)]
                    else:
                        _, b2, hb, row, hrow, S = item
                        lhsT = row[0:S, 0:128]
                        rhs = b2[0:S, 512 * nh:512 * (nh + 1)]
                    inst = e.matmul(pyb[:, 512 * nh:512 * (nh + 1)], lhsT, rhs,
                                    start=(ii == 0), stop=(ii == n_items - 1))
            return inst
        deps_b = [combT_h[tile], wpy]
        for item in bsb:
            deps_b += [item[2]]
            if item[0] == "mean":
                deps_b += [item[4]]
        hbias = kb.emit("tensor", bias_mms, deps=deps_b)
        gb, wgt, sgt = gt.next()
        hsum = kb.emit("vector", lambda e, gb=gb, pyb=pyb, tile=tile: e.tensor_tensor(
            gb[:, :], g_acc[:, D * tile:D * (tile + 1)], pyb[:, :], ALU.add),
            deps=[hbias, g_last[tile], wgt])
        py.used(spy, hsum)
        hsil = kb.emit("scalar", lambda e, gb=gb: e.activation(gb[:, :], gb[:, :], AF.Silu),
                       deps=[hsum])
        if dbg_g is not None:
            hdg = kb.dma("sync", dbg_g[128 * tile:128 * (tile + 1), :], gb[:, :], deps=[hsil])
            gt.used(sgt, hdg)
            out_h.append(hdg)
            continue
        xb, wx, sx = xr.next()
        hx = kb.dma("scalar", xb[:, :], x_in_ap[128 * tile:128 * (tile + 1), :], deps=[wx])
        sb_, wsr, ssx = sr.next()
        hs = kb.dma("scalar", sb_[:, :], s_dram[128 * tile:128 * (tile + 1), :], deps=[wsr])
        hm = kb.emit("vector", lambda e, gb=gb, sb_=sb_: e.tensor_tensor(
            gb[:, :], gb[:, :], sb_[:, :], ALU.mult), deps=[hsil, hs])
        ho = kb.emit("vector", lambda e, gb=gb, xb=xb: e.tensor_tensor(
            gb[:, :], gb[:, :], xb[:, :], ALU.add), deps=[hm, hx])
        hw = kb.dma("sync", x_out_ap[128 * tile:128 * (tile + 1), :], gb[:, :], deps=[ho])
        xr.used(sx, ho)
        sr.used(ssx, hm)
        gt.used(sgt, hw)
        out_h.append(hw)
        if bg:
            bg.pump(2)
    return out_h




def mamba_ctx(pr, tag="mb"):
    ctx = dict(
        W_in=pr.din("L0_attn_W_in", [1024, 4384]),
        conv_w=pr.din("L0_attn_conv_w", [4, 2304]),
        conv_b=pr.din("L0_attn_conv_b", [2304]),
        A_log=pr.din("L0_attn_A_log", [32]),
        Dvec=pr.din("L0_attn_D", [32]),
        dt_bias=pr.din("L0_attn_dt_bias", [32]),
        gamma=pr.din("L0_attn_gamma", [2048]),
        W_out=pr.din("L0_attn_W_out", [2048, 1024]),
        BT=pr.sbuf(f"{tag}_BT", [128, T], BF16),
        CT=pr.sbuf(f"{tag}_CT", [128, T], BF16),
        dtT=pr.sbuf(f"{tag}_dtT", [32, T], F32))
    return ctx


def emit_mamba_ab(pr, C, xnT, xnT_dep, z_dram, x_dram, ctx, tag="mb", bg=None):
    kb = pr
    W_in, conv_w, conv_b = ctx["W_in"], ctx["conv_w"], ctx["conv_b"]
    BT, CT, dtT = ctx["BT"], ctx["CT"], ctx["dtT"]
    DS = 2048
    NH = 32
    pr.push()
    wz = pr.sbuf(f"{tag}_wz", [128, 8 * DS], BF16)
    lwz = [kb.dma("gpsimd", wz[:, DS * k:DS * (k + 1)],
                  W_in[128 * k:128 * (k + 1), 0:DS]) for k in range(8)]
    zt = sbuf_ring(pr, f"{tag}_zt", 2, [128, DS], BF16)
    pz = psum_ring(pr, f"{tag}_pz", 4, [128, 512], F32)
    for t in range(NT):
        zb, wzt, szt = zt.next()
        evs = []
        for c4 in range(4):
            pzb, wpz, spz = pz.next()
            hm = kb.emit("tensor", lambda e, pzb=pzb, t=t, c4=c4: [
                e.matmul(pzb[:, :], xnT[:, T * k + 128 * t:T * k + 128 * (t + 1)],
                         wz[:, DS * k + 512 * c4:DS * k + 512 * (c4 + 1)],
                         start=(k == 0), stop=(k == 7)) for k in range(8)][-1],
                deps=lwz + [xnT_dep, wpz])
            he = kb.emit("scalar", lambda e, pzb=pzb, zb=zb, c4=c4: e.copy(
                zb[:, 512 * c4:512 * (c4 + 1)], pzb[:, :]), deps=[hm, wzt])
            pz.used(spz, he)
            evs.append(he)
        hd = kb.dma("scalar", z_dram[128 * t:128 * (t + 1), :], zb[:, :], deps=evs)
        zt.used(szt, hd)
        if bg:
            bg.pump(4)
    pr.pop()

    pr.push()
    x_tok = pr.sbuf(f"{tag}_xtok", [128, NT * DS], BF16)
    wst = sbuf_ring(pr, f"{tag}_ws", 4, [128, 1024], BF16)
    cxr = sbuf_ring(pr, f"{tag}_cx", 2, [128, T + 3], F32)
    cw = pr.sbuf(f"{tag}_cw", [128, 18 * 4], F32)
    cb = pr.sbuf(f"{tag}_cb", [128, 18], F32)
    hcw = [kb.dma("scalar", cw[:, 18 * k:18 * (k + 1)],
                  conv_w[k].rearrange("(f p) -> p f", p=128),
                  allow_slow_non_contiguous=True) for k in range(4)]
    hcb = kb.dma("scalar", cb[:, :], conv_b.rearrange("(f p) -> p f", p=128),
                 allow_slow_non_contiguous=True)
    hz3a = kb.emit("vector", lambda e: e.memset(cxr.bufs[0][:, 0:3], 0.0))
    hz3 = kb.emit("vector", lambda e: e.memset(cxr.bufs[1][:, 0:3], 0.0))
    pxb = psum_ring(pr, f"{tag}_pxb", 4, [128, 512], F32)
    ptr = psum_ring(pr, f"{tag}_ptr", 2, [128, 128], BF16)
    cvar = sbuf_ring(pr, f"{tag}_cva", 2, [128, T], F32)
    sxbr = sbuf_ring(pr, f"{tag}_sxb", 2, [128, T], BF16)
    prev_cx = [hz3]
    xtile_h = [[] for _ in range(NT)]
    bc_ready = None

    def mm_part(fc):
        wb, wwar, wslot = wst.next()
        lw = kb.dma("gpsimd", wb[:, :].rearrange("p (k c) -> p k c", c=128),
                    W_in[:, DS + 128 * fc:DS + 128 * (fc + 1)]
                    .rearrange("(k p) c -> p k c", p=128), deps=[wwar])
        evs = []
        hm = None
        cx, wcx, scx = cxr.next()
        for tq in range(4):
            pb, wp, sp = pxb.next()
            hm = kb.emit("tensor", lambda e, pb=pb, tq=tq, wb=wb: [
                e.matmul(pb[:, :], wb[:, 128 * k:128 * (k + 1)],
                         xnT[:, T * k + 512 * tq:T * k + 512 * (tq + 1)],
                         start=(k == 0), stop=(k == 7)) for k in range(8)][-1],
                deps=[lw, xnT_dep, wp])
            he = kb.emit("scalar", lambda e, pb=pb, tq=tq, cx=cx: e.copy(
                cx[:, 3 + 512 * tq:3 + 512 * (tq + 1)], pb[:, :]),
                deps=[hm] + (wcx if tq == 0 else []))
            pxb.used(sp, he)
            evs.append(he)
        wst.used(wslot, hm)
        return (fc, cx, scx, evs)

    def conv_part(stb):
        nonlocal bc_ready
        fc, cx, scx, evs = stb
        cva, wcv, scv = cvar.next()
        h1 = kb.emit("vector", lambda e, fc=fc, cva=cva, cx=cx: e.tensor_scalar(
            cva[:, :], cx[:, 0:T], cw[:, fc:fc + 1], cb[:, fc:fc + 1],
            ALU.mult, ALU.add), deps=evs + hcw + [hcb, hz3a, hz3] + wcv)
        for k in range(1, 4):
            h1 = kb.emit("vector", lambda e, fc=fc, k=k, cva=cva, cx=cx: e.scalar_tensor_tensor(
                cva[:, :], cx[:, k:k + T], cw[:, 18 * k + fc:18 * k + fc + 1],
                cva[:, :], ALU.mult, ALU.add), deps=[h1])
        cxr.used(scx, h1)
        sxb, wsx, ssx = sxbr.next()
        hs = kb.emit("scalar", lambda e, cva=cva, sxb=sxb: e.activation(
            sxb[:, :], cva[:, :], AF.Silu), deps=[h1] + wsx)
        cvar.used(scv, hs)
        if fc < 16:
            tevs = []
            for t in range(NT):
                pb, wp, sp = ptr.next()
                ht = kb.emit("tensor", lambda e, pb=pb, t=t, sxb=sxb: e.transpose(
                    pb[:, :], sxb[:, 128 * t:128 * (t + 1)], C.ident[:, :]),
                    deps=[hs, wp, C.ready])
                hev = kb.emit("scalar", lambda e, pb=pb, t=t, fc=fc: e.copy(
                    x_tok[:, DS * t + 128 * fc:DS * t + 128 * (fc + 1)], pb[:, :]),
                    deps=[ht])
                ptr.used(sp, hev)
                xtile_h[t].append(hev)
                tevs.append(hev)
            sxbr.used(ssx, tevs[-1])
        else:
            dst = BT if fc == 16 else CT
            hcp = kb.emit("vector", lambda e, dst=dst, sxb=sxb: e.tensor_copy(dst[:, :], sxb[:, :]),
                          deps=[hs])
            sxbr.used(ssx, hcp)
            bc_ready = hcp

    pend_b = None
    for fc in range(18):
        stb = mm_part(fc)
        if pend_b is not None:
            conv_part(pend_b)
        pend_b = stb
        if bg:
            bg.pump(4)
    conv_part(pend_b)
    for t in range(NT):
        kb.dma("scalar", x_dram[128 * t:128 * (t + 1), :],
               x_tok[:, DS * t:DS * (t + 1)], deps=[xtile_h[t][-1]])
    pdt = pr.psum(f"{tag}_pdt", [32, 512], F32)
    wdt = pr.sbuf(f"{tag}_wdt", [128, 8 * 32], BF16)
    ldt = kb.dma("gpsimd", wdt[:, :].rearrange("p (k c) -> p k c", c=32),
                 W_in[:, 4352:4384].rearrange("(k p) c -> p k c", p=128))
    prev_dt = []
    for tq in range(4):
        hmdt = kb.emit("tensor", lambda e, tq=tq: [
            e.matmul(pdt[:, :], wdt[:, 32 * k:32 * (k + 1)],
                     xnT[:, T * k + 512 * tq:T * k + 512 * (tq + 1)],
                     start=(k == 0), stop=(k == 7)) for k in range(8)][-1],
            deps=[ldt, xnT_dep] + prev_dt)
        hdt = kb.emit("vector", lambda e, tq=tq: e.tensor_copy(
            dtT[:, 512 * tq:512 * (tq + 1)], pdt[:, :]), deps=[hmdt])
        prev_dt = [hdt]
        if bg:
            bg.pump(2)
    dt_ready = prev_dt[0]
    pr.pop()

    ctx["bc_ready"] = bc_ready
    ctx["dt_ready"] = dt_ready


def emit_mamba_c(pr, C, s_dram, z_dram, x_dram, ctx, tag="mb", bg=None):
    kb = pr
    A_log, Dvec, dt_bias = ctx["A_log"], ctx["Dvec"], ctx["dt_bias"]
    gamma, W_out = ctx["gamma"], ctx["W_out"]
    BT, CT, dtT = ctx["BT"], ctx["CT"], ctx["dtT"]
    bc_ready, dt_ready = ctx["bc_ready"], ctx["dt_ready"]
    DS = 2048
    NH = 32
    pr.push()
    wout_sb = pr.sbuf(f"{tag}_wo", [128, 16 * 1024], BF16)
    lwo = [kb.dma("gpsimd", wout_sb[:, 1024 * kc:1024 * (kc + 1)],
                  W_out[128 * kc:128 * (kc + 1), :]) for kc in range(16)]
    sv = pr.sbuf(f"{tag}_sv", [32, 8], F32)
    hdb = kb.dma("sync", sv[0:32, 0:1], dt_bias.unsqueeze(1))
    hal = kb.dma("sync", sv[0:32, 1:2], A_log.unsqueeze(1))
    hea = kb.emit("scalar", lambda e: e.activation(sv[0:32, 3:4], sv[0:32, 1:2], AF.Exp),
                  deps=[hal])
    bbT = pr.sbuf(f"{tag}_bbT", [32, T], F32)
    osb = pr.sbuf(f"{tag}_osb", [64, T], F32)
    ebtok = pr.sbuf(f"{tag}_ebtok", [128, NT * 32], F32)
    lamall = pr.sbuf(f"{tag}_lamall", [128, NT * 32], F32)
    w2fall = pr.sbuf(f"{tag}_w2fa", [128, NT * 32], F32)
    lrow = pr.sbuf(f"{tag}_lrow", [1, NT * 32], F32)
    psA = pr.psum(f"{tag}_psA", [128, 2048], F32)
    psB = pr.psum(f"{tag}_psB", [128, 1024], F32)
    psT0 = pr.psum(f"{tag}_psT0", [128, 128], BF16)
    psT1 = pr.psum(f"{tag}_psT1", [128, 128], BF16)
    onesf = pr.sbuf(f"{tag}_onesf", [1, 128], F32)
    honf = kb.emit("vector", lambda e: e.memset(onesf[:, :], 1.0))
    onec = pr.sbuf(f"{tag}_onec", [128, 1], F32)
    honc = kb.emit("vector", lambda e: e.memset(onec[:, :], 1.0))
    pr.push()
    deltaT = pr.sbuf(f"{tag}_delta", [32, T], F32)
    ldmbT = pr.sbuf(f"{tag}_ldmb", [32, T], F32)
    aT = pr.sbuf(f"{tag}_aT", [32, T], F32)
    atok = pr.sbuf(f"{tag}_atok", [128, NT * 32], F32)
    u1 = pr.sbuf(f"{tag}_u1", [32, T], F32)
    u2 = pr.sbuf(f"{tag}_u2", [32, T], F32)
    q1 = kb.emit("vector", lambda e: e.tensor_scalar(
        u1[:, :], dtT[:, :], sv[0:32, 0:1], None, ALU.add), deps=[dt_ready, hdb])
    q2 = kb.emit("scalar", lambda e: e.activation(u2[:, :], u1[:, :], AF.Abs), deps=[q1])
    q3 = kb.emit("scalar", lambda e: e.activation(u2[:, :], u2[:, :], AF.Exp, scale=-1.0), deps=[q2])
    q4 = kb.emit("vector", lambda e: e.tensor_scalar_add(u2[:, :], u2[:, :], 1.0), deps=[q3])
    q5 = kb.emit("scalar", lambda e: e.activation(u2[:, :], u2[:, :], AF.Ln), deps=[q4])
    q6 = kb.emit("vector", lambda e: e.tensor_scalar_max(u1[:, :], u1[:, :], 0.0), deps=[q1, q2])
    q7 = kb.emit("vector", lambda e: e.tensor_add(deltaT[:, :], u1[:, :], u2[:, :]), deps=[q5, q6])
    q8 = kb.emit("scalar", lambda e: e.activation(u1[:, :], deltaT[:, :], AF.Ln), deps=[q7])
    q9 = kb.emit("vector", lambda e: e.tensor_scalar(
        aT[:, :], deltaT[:, :], sv[0:32, 3:4], -1.0, ALU.mult, ALU.mult), deps=[q7, hea])
    ones32 = pr.sbuf(f"{tag}_o32", [32, 1], F32)
    ho32 = kb.emit("vector", lambda e: e.memset(ones32[:, :], 1.0))
    hbb = None
    for c in range(NT):
        hbb = kb.emit("vector", lambda e, c=c: e.tensor_tensor_scan(
            bbT[:, 128 * c:128 * (c + 1)], ones32[:, :].broadcast_to([32, 128]),
            aT[:, 128 * c:128 * (c + 1)], 0.0, ALU.mult, ALU.add),
            deps=[q9, ho32])
    qA = kb.emit("vector", lambda e: e.tensor_sub(ldmbT[:, :], u1[:, :], bbT[:, :]), deps=[q8, hbb])
    hos1 = kb.emit("vector", lambda e: e.tensor_copy(osb[0:32, :], ldmbT[:, :]), deps=[qA])
    hos2 = kb.emit("vector", lambda e: e.memset(osb[32:64, :], 1.0))

    prep_evs = []
    prev_slot = [None, None]
    prev_lam = None
    for c in range(NT):
        cs = slice(128 * c, 128 * (c + 1))
        ts = slice(32 * c, 32 * (c + 1))
        for j, (srcT, dst, dep0) in enumerate(
                ((bbT, ebtok, hbb), (ldmbT, w2fall, qA), (aT, atok, q9))):
            bank = (c * 3 + j) % 2
            sl = slice(512 * bank, 512 * bank + 32)
            ht = kb.emit("tensor", lambda e, srcT=srcT, cs=cs, sl=sl: e.transpose(
                psA[:, sl], srcT[:, cs], C.identf[0:32, 0:32]),
                deps=[dep0, C.ready, prev_slot[bank]])
            hv2 = kb.emit("vector", lambda e, dst=dst, sl=sl, ts=ts: e.tensor_copy(
                dst[:, ts], psA[:, sl]), deps=[ht])
            prev_slot[bank] = hv2
            prep_evs.append(hv2)
        hl = kb.emit("tensor", lambda e, ts=ts, c=c: e.matmul(
            psB[0:1, 32 * (c % 8):32 * (c % 8) + 32], onec[:, :], atok[:, ts],
            start=True, stop=True), deps=[prep_evs[-1], honc, prev_lam])
        hr = kb.emit("vector", lambda e, ts=ts, c=c: e.tensor_copy(
            lrow[0:1, ts], psB[0:1, 32 * (c % 8):32 * (c % 8) + 32]), deps=[hl])
        prev_lam = hr
        prep_evs.append(hr)
        if bg:
            bg.pump(2)
    pr.pop()
    hlb = None
    lb_evs = []
    for i in range(NT * 32 // 512):
        hmb = kb.emit("tensor", lambda e, i=i: e.matmul(
            psA[:, 512:1024], onesf[:, :], lrow[0:1, 512 * i:512 * (i + 1)],
            start=True, stop=True), deps=prep_evs + [honf, hlb])
        hlb = kb.emit("vector", lambda e, i=i: e.tensor_copy(
            lamall[:, 512 * i:512 * (i + 1)], psA[:, 512:1024]), deps=[hmb])
        lb_evs.append(hlb)
    hw2a = kb.emit("vector", lambda e: e.tensor_add(w2fall[:, :], w2fall[:, :], lamall[:, :]),
                   deps=lb_evs + prep_evs)
    hw2b = kb.emit("scalar", lambda e: e.activation(w2fall[:, :], w2fall[:, :], AF.Exp), deps=[hw2a])
    hlam = kb.emit("scalar", lambda e: e.activation(lamall[:, :], lamall[:, :], AF.Exp),
                   deps=[hw2a] + lb_evs)
    hebt = kb.emit("scalar", lambda e: e.activation(ebtok[:, :], ebtok[:, :], AF.Exp), deps=prep_evs)

    grow = pr.sbuf(f"{tag}_grow", [1, DS], F32)
    hgr = kb.dma("sync", grow[:, :], gamma.unsqueeze(0))
    Drow = pr.sbuf(f"{tag}_Drow", [1, 32], F32)
    hDrw = kb.dma("sync", Drow[:, :], Dvec.unsqueeze(0))
    gbc = pr.sbuf(f"{tag}_gbc", [128, DS], BF16)
    Dbc = pr.sbuf(f"{tag}_Dbc", [128, 32], F32)
    hg1 = kb.emit("tensor", lambda e: [
        e.matmul(psA[:, 512 * i:512 * (i + 1)], onesf[:, :], grow[:, 512 * i:512 * (i + 1)],
                 start=True, stop=True) for i in range(4)][-1],
        deps=[hgr, honf, hlb] + prep_evs)
    hg2 = kb.emit("vector", lambda e: e.tensor_copy(gbc[:, :], psA[:, :]), deps=[hg1])
    hg3 = kb.emit("tensor", lambda e: e.matmul(psA[:, 0:32], onesf[:, :], Drow[:, :],
                                               start=True, stop=True), deps=[hDrw, honf, hg2])
    hg4 = kb.emit("vector", lambda e: e.tensor_copy(Dbc[:, :], psA[:, 0:32]), deps=[hg3])

    blockones = C.blockones_ap
    rhs_t = pr.sbuf(f"{tag}_rhs", [64, 4096], F32)
    hbo = kb.dma("sync", rhs_t[0:32, :], blockones)
    bones3 = pr.sbuf(f"{tag}_bo3", [32, 4096], BF16)
    hbo2 = kb.dma("gpsimd", bones3[:, :], blockones)

    hstf = pr.sbuf(f"{tag}_hst", [128, DS], F32)
    hstb = pr.sbuf(f"{tag}_hstbf", [128, DS], BF16)
    hh0 = kb.emit("vector", lambda e: e.memset(hstf[:, :], 0.0))
    hh1 = kb.emit("vector", lambda e: e.memset(hstb[:, :], 0.0))

    atbf = pr.sbuf(f"{tag}_atbf", [128, 128], BF16)
    btok = pr.sbuf(f"{tag}_btok", [128, 128], BF16)
    bwa = pr.sbuf(f"{tag}_bwa", [128, 4096], BF16)
    wsc = pr.sbuf(f"{tag}_wsc", [128, 1024], F32)
    watbf = pr.sbuf(f"{tag}_watbf", [128, 4096], BF16)
    ysum = pr.sbuf(f"{tag}_ysum", [128, DS], F32)
    yscr = pr.sbuf(f"{tag}_yscr", [128, DS], BF16)
    ssmT = pr.sbuf(f"{tag}_ssmT", [128, 16 * 128], BF16)
    sbf = pr.sbuf(f"{tag}_sbf", [128, 1024], BF16)
    zld = sbuf_ring(pr, f"{tag}_zld", 1, [128, DS], BF16)
    xld = sbuf_ring(pr, f"{tag}_xld", 2, [128, DS], BF16)
    rstat = pr.sbuf(f"{tag}_rstat", [128, 4], F32)

    pv = {"psA": [hg4], "psB": [prev_lam], "rhs": [hbo], "wsc": [],
          "watbf": [], "atbf": [], "btok": [], "bwa": [], "ysum": [], "yscr": [],
          "ssmbf": [], "ssmT": [], "sbf": [], "zsil": [], "rstat": [],
          "y2": [], "hstb": [hh1], "hstf": [hh0], "psT0": None, "psT1": None}

    for c in range(NT):
        cs = slice(128 * c, 128 * (c + 1))
        ts = slice(32 * c, 32 * (c + 1))
        xb, wxl, sxl = xld.next()
        h_xl = kb.dma("gpsimd", xb[:, :], x_dram[cs, :], deps=[wxl])
        h_at = kb.emit("tensor", lambda e, cs=cs: e.matmul(
            psB[:, 0:128], BT[:, cs], CT[:, cs], start=True, stop=True),
            deps=[bc_ready] + pv["psB"])
        h_ae = kb.emit("scalar", lambda e: e.copy(atbf[:, :], psB[:, 0:128]),
                       deps=[h_at] + pv["atbf"])
        h_bd = kb.emit("gpsimd", lambda e, cs=cs: e.tensor_tensor(
            rhs_t[32:64, :].rearrange("p (h t) -> p h t", t=128),
            bbT[:, cs].unsqueeze(1).broadcast_to([32, 32, 128]),
            bones3[:, :].rearrange("p (h t) -> p h t", t=128), ALU.mult),
            deps=[hbb, hbo2] + pv["rhs"])
        wat_evs = []
        h_wq_prev = h_ae
        last_wq = None
        for qq in range(4):
            qsl = slice(1024 * qq, 1024 * (qq + 1))
            h_wq = kb.emit("tensor", lambda e, cs=cs, qq=qq: [
                e.matmul(psB[:, 512 * i:512 * (i + 1)],
                         osb[0:64, cs],
                         rhs_t[0:64, 1024 * qq + 512 * i:1024 * qq + 512 * (i + 1)],
                         start=True, stop=True) for i in range(2)][-1],
                deps=[hos1, hos2, h_bd, h_wq_prev])
            last_wq = h_wq
            h_wm = kb.emit("vector", lambda e: e.tensor_tensor(
                wsc[:, :].rearrange("p (h t) -> p h t", t=128),
                psB[:, :].rearrange("p (h t) -> p h t", t=128),
                C.maskb[:, :].unsqueeze(1).broadcast_to([128, 8, 128]), ALU.add),
                deps=[h_wq] + pv["wsc"])
            h_we = kb.emit("scalar", lambda e: e.activation(wsc[:, :], wsc[:, :], AF.Exp),
                           deps=[h_wm])
            h_wt = kb.emit("vector", lambda e, qsl=qsl: e.tensor_tensor(
                watbf[:, qsl].rearrange("p (h t) -> p h t", t=128),
                wsc[:, :].rearrange("p (h t) -> p h t", t=128),
                atbf[:, :].unsqueeze(1).broadcast_to([128, 8, 128]), ALU.mult),
                deps=[h_we, h_ae] + (pv["watbf"] if qq == 0 else []))
            pv["wsc"] = [h_wt]
            h_wq_prev = h_wm
            wat_evs.append(h_wt)
        h_wm_last = h_wq_prev
        h_y1 = kb.emit("tensor", lambda e, xb=xb: [
            e.matmul(psA[:, 64 * h:64 * (h + 1)], watbf[:, 128 * h:128 * (h + 1)],
                     xb[:, 64 * h:64 * (h + 1)], start=True, stop=True)
            for h in range(NH)][-1],
            deps=wat_evs + [h_xl] + pv["psA"])
        yc_evs = []
        y2_mms = []
        for hh in range(2):
            hsl = slice(1024 * hh, 1024 * (hh + 1))
            h_y2 = kb.emit("tensor", lambda e, cs=cs, hsl=hsl: [
                e.matmul(psB[:, 512 * i:512 * (i + 1)], CT[:, cs],
                         hstb[:, hsl][:, 512 * i:512 * (i + 1)], start=True, stop=True)
                for i in range(2)][-1],
                deps=[bc_ready, h_wm_last] + pv["hstb"] + pv["psB"])
            y2_mms.append(h_y2)
            h_yc = kb.emit("vector", lambda e, ts=ts, hh=hh, hsl=hsl: e.tensor_tensor(
                ysum[:, hsl].rearrange("p (h q) -> p h q", q=64),
                psB[:, :].rearrange("p (h q) -> p h q", q=64),
                ebtok[:, ts][:, 16 * hh:16 * (hh + 1)].unsqueeze(2).broadcast_to([128, 16, 64]),
                ALU.mult), deps=[h_y2, hebt] + pv["ysum"])
            pv["psB"] = [h_yc]
            h_ya = kb.emit("vector", lambda e, hsl=hsl: e.tensor_tensor(
                ysum[:, hsl], ysum[:, hsl], psA[:, hsl], ALU.add), deps=[h_yc, h_y1])
            yc_evs.append(h_ya)
        pv["ysum"] = []
        pv["y2"] = y2_mms
        h_sk1 = kb.emit("vector", lambda e, xb=xb: e.tensor_tensor(
            yscr[:, :].rearrange("p (h q) -> p h q", q=64),
            xb[:, :].rearrange("p (h q) -> p h q", q=64),
            Dbc[:, :].unsqueeze(2).broadcast_to([128, 32, 64]), ALU.mult),
            deps=[h_xl, hg4] + pv["yscr"])
        h_sk2 = kb.emit("vector", lambda e: e.tensor_add(ysum[:, :], ysum[:, :], yscr[:, :]),
                        deps=[h_sk1] + yc_evs)
        h_r1 = kb.emit("scalar", lambda e: e.activation(yscr[:, :], ysum[:, :], AF.Square),
                       deps=[h_sk2])
        h_r2 = kb.emit("vector", lambda e: e.reduce_sum(rstat[:, 0:1], yscr[:, :], AX),
                       deps=[h_r1] + pv["rstat"])
        pv["yscr"] = [h_r2]
        h_r3 = kb.emit("vector", lambda e: e.tensor_scalar(
            rstat[:, 1:2], rstat[:, 0:1], 1.0 / DS, 1e-5, ALU.mult, ALU.add), deps=[h_r2])
        h_r4 = kb.emit("scalar", lambda e: e.sqrt(rstat[:, 2:3], rstat[:, 1:2]), deps=[h_r3])
        h_r5 = kb.emit("vector", lambda e: e.reciprocal(rstat[:, 3:4], rstat[:, 2:3]), deps=[h_r4])
        zb, wz2, sz2 = zld.next()
        h_zl = kb.dma("sync", zb[:, :], z_dram[cs, :], deps=[wz2])
        h_zs = kb.emit("scalar", lambda e, zb=zb: e.activation(zb[:, :], zb[:, :], AF.Silu),
                       deps=[h_zl] + pv["zsil"])
        h_n1 = kb.emit("vector", lambda e: e.tensor_scalar(
            ysum[:, :], ysum[:, :], rstat[:, 3:4], None, ALU.mult), deps=[h_r5, h_sk2])
        pv["rstat"] = [h_n1]
        h_n2 = kb.emit("vector", lambda e: e.tensor_tensor(ysum[:, :], ysum[:, :], gbc[:, :], ALU.mult),
                       deps=[h_n1, hg2])
        h_n3 = kb.emit("vector", lambda e, zb=zb: e.tensor_tensor(yscr[:, :], ysum[:, :], zb[:, :], ALU.mult),
                       deps=[h_n2, h_zs, h_r2] + pv["ssmbf"])
        pv["ysum"] = [h_n3]
        pv["zsil"] = [h_n3]
        zld.used(sz2, h_n3)
        tprev = [pv["psT0"], pv["psT1"]]
        psTs = [psT0, psT1]
        tevs = []
        for kcb in range(16):
            slot = kcb % 2
            pTt = psTs[slot]
            h_tt = kb.emit("tensor", lambda e, kcb=kcb, pTt=pTt: e.transpose(
                pTt[:, :], yscr[:, 128 * kcb:128 * (kcb + 1)], C.ident[:, :]),
                deps=[h_n3, C.ready, tprev[slot]])
            h_te = kb.emit("scalar", lambda e, kcb=kcb, pTt=pTt: e.copy(
                ssmT[:, 128 * kcb:128 * (kcb + 1)], pTt[:, :]),
                deps=[h_tt] + (pv["ssmT"] if kcb < 2 else []))
            tprev[slot] = h_te
            tevs.append(h_te)
        pv["ssmbf"] = [tevs[-1]]
        pv["yscr"] = [tevs[-1]]
        h_o1 = kb.emit("tensor", lambda e: [
            e.matmul(psB[:, 512 * nh:512 * (nh + 1)],
                     ssmT[:, 128 * kc:128 * (kc + 1)],
                     wout_sb[:, 1024 * kc + 512 * nh:1024 * kc + 512 * (nh + 1)],
                     start=(kc == 0), stop=(kc == 15))
            for nh in range(2) for kc in range(16)][-1],
            deps=tevs + lwo + pv["psB"])
        h_oe = kb.emit("scalar", lambda e: e.copy(sbf[:, :], psB[:, :]),
                       deps=[h_o1] + pv["sbf"])
        h_sd = kb.dma("sync", s_dram[cs, :], sbf[:, :], deps=[h_oe])
        pv["ssmT"] = [h_o1]
        pv["sbf"] = [h_sd]
        pv["psB"] = [h_oe]
        h_bt = kb.emit("tensor", lambda e, cs=cs: e.transpose(
            psT0[:, :], BT[:, cs], C.ident[:, :]),
            deps=[bc_ready, tprev[0]])
        h_be = kb.emit("scalar", lambda e: e.copy(btok[:, :], psT0[:, :]),
                       deps=[h_bt] + pv["btok"])
        pv["psT0"] = h_be
        pv["psT1"] = tprev[1]
        h_bw = kb.emit("vector", lambda e, ts=ts: e.tensor_tensor(
            bwa[:, :].rearrange("p (h n) -> p h n", n=128),
            btok[:, :].unsqueeze(1).broadcast_to([128, 32, 128]),
            w2fall[:, ts].unsqueeze(2).broadcast_to([128, 32, 128]), ALU.mult),
            deps=[h_be, hw2b] + pv["bwa"])
        h_dh = kb.emit("tensor", lambda e, xb=xb: [
            e.matmul(psA[:, 64 * h:64 * (h + 1)], bwa[:, 128 * h:128 * (h + 1)],
                     xb[:, 64 * h:64 * (h + 1)], start=True, stop=True)
            for h in range(NH)][-1],
            deps=[h_bw, h_xl] + yc_evs)
        xld.used(sxl, [h_dh, h_sk1, h_y1])
        pv["bwa"] = [h_dh]
        pv["btok"] = [h_bw]
        h_dc = kb.emit("vector", lambda e, ts=ts: e.tensor_tensor(
            hstf[:, :].rearrange("p (h q) -> p h q", q=64),
            hstf[:, :].rearrange("p (h q) -> p h q", q=64),
            lamall[:, ts].unsqueeze(2).broadcast_to([128, 32, 64]), ALU.mult),
            deps=[hlam] + pv["hstf"])
        h_da = kb.emit("vector", lambda e: e.tensor_add(hstf[:, :], hstf[:, :], psA[:, :]),
                       deps=[h_dc, h_dh])
        h_cb = kb.emit("vector", lambda e: e.tensor_copy(hstb[:, :], hstf[:, :]),
                       deps=[h_da] + y2_mms)
        pv["hstf"] = [h_cb]
        pv["hstb"] = [h_cb]
        pv["psA"] = [h_da]
        pv["atbf"] = [wat_evs[-1]]
        pv["watbf"] = [h_y1]
        pv["rhs"] = [last_wq]
        if bg:
            bg.pump(2)
    pr.pop()
    return None


def emit_mamba(pr, C, xnT, xnT_dep, s_dram, z_dram, x_dram, tag="mb", bg=None):
    ctx = mamba_ctx(pr, tag)
    emit_mamba_ab(pr, C, xnT, xnT_dep, z_dram, x_dram, ctx, tag, bg=bg)
    emit_mamba_c(pr, C, s_dram, z_dram, x_dram, ctx, tag, bg=bg)




def emit_gla(pr, C, xnT, xnT_dep, W, dims, s_dram, tag):
    kb = pr
    DK, DV, NHD = dims["DK"], dims["DV"], dims["HEADS"]
    HDK, HDV = DK // NHD, DV // NHD
    OWC = DK // 128
    OWH = HDK // 128
    VC = DV // 128
    scale = float(HDK) ** -0.5
    gnorm = dims["GATE_NORM"]
    CH = 128
    TB = 512
    NTB = T // TB
    CPB = TB // CH

    t1T = pr.sbuf(f"{tag}_t1T", [16, T], BF16)
    pr.push()
    wg1 = pr.sbuf(f"{tag}_wg1", [128, 8 * 16], BF16)
    lg1 = kb.dma("gpsimd", wg1[:, :].rearrange("p (k c) -> p k c", c=16),
                 W["Wgk1"].rearrange("(k p) c -> p k c", p=128))
    pg1 = pr.psum(f"{tag}_pg1", [16, 512], F32)
    prev_t1 = []
    for tq in range(4):
        hm = kb.emit("tensor", lambda e, tq=tq: [
            e.matmul(pg1[:, :], wg1[:, 16 * k:16 * (k + 1)],
                     xnT[:, T * k + 512 * tq:T * k + 512 * (tq + 1)],
                     start=(k == 0), stop=(k == 7)) for k in range(8)][-1],
            deps=[lg1, xnT_dep] + prev_t1)
        ht = kb.emit("vector", lambda e, tq=tq: e.tensor_copy(
            t1T[:, 512 * tq:512 * (tq + 1)], pg1[:, :]), deps=[hm])
        prev_t1 = [ht]
    t1_ready = prev_t1[0]
    pr.pop()
    pr.push()
    qT = pr.sbuf(f"{tag}_qT", [128, OWC * TB], BF16)
    kT = pr.sbuf(f"{tag}_kT", [128, OWC * TB], BF16)
    wqk = sbuf_ring(pr, f"{tag}_wqk", 2, [128, 8 * 128], BF16)
    wg2 = pr.sbuf(f"{tag}_wg2", [16, DK], BF16)
    lg2 = kb.dma("gpsimd", wg2[:, :], W["Wgk2"])
    bg2 = pr.sbuf(f"{tag}_bg2", [128, OWC], F32)
    lbg = kb.dma("sync", bg2[:, :], W["bgk2"].rearrange("(c p) -> p c", p=128),
                 allow_slow_non_contiguous=True)
    onesf = pr.sbuf(f"{tag}_onesf", [1, 128], F32)
    honf = kb.emit("vector", lambda e: e.memset(onesf[:, :], 1.0))
    ones_c = pr.sbuf(f"{tag}_onc", [128, 1], F32)
    honc = kb.emit("vector", lambda e: e.memset(ones_c[:, :], 1.0))
    wo_sb = pr.sbuf(f"{tag}_wo", [128, VC * 1024], BF16)
    lwo = [kb.dma("gpsimd", wo_sb[:, 1024 * vc:1024 * (vc + 1)],
                  W["Wo"][128 * vc:128 * (vc + 1), :]) for vc in range(VC)]
    rw_row = pr.sbuf(f"{tag}_rwrow", [1, HDV], F32)
    lrw = kb.dma("sync", rw_row[:, :], W["rms_w"].unsqueeze(0))
    rw_bc = pr.sbuf(f"{tag}_rwbc", [128, HDV], BF16)
    pPr = psum_ring(pr, f"{tag}_pPr", 2, [128, 512], F32)
    rw_evs = []
    for i in range((HDV + 511) // 512):
        w_ = min(512, HDV - 512 * i)
        pb_, wpp, spp = pPr.next()
        h1_ = kb.emit("tensor", lambda e, i=i, w_=w_, pb_=pb_: e.matmul(
            pb_[:, 0:w_], onesf[:, :], rw_row[:, 512 * i:512 * i + w_],
            start=True, stop=True), deps=[lrw, honf] + wpp)
        h2_ = kb.emit("vector", lambda e, i=i, w_=w_, pb_=pb_: e.tensor_copy(
            rw_bc[:, 512 * i:512 * i + w_], pb_[:, 0:w_]), deps=[h1_])
        pPr.used(spp, h2_)
        rw_evs.append(h2_)
    rw_ready = rw_evs[-1]

    S_sb = pr.sbuf(f"{tag}_S", [128, OWC * HDV], F32)
    S_bf = pr.sbuf(f"{tag}_Sbf", [128, OWC * HDV], BF16)
    hS0 = kb.emit("vector", lambda e: e.memset(S_sb[:, :], 0.0))
    hS0b = kb.emit("vector", lambda e: e.memset(S_bf[:, :], 0.0))

    vtb = pr.sbuf(f"{tag}_vtb", [128, 4 * DV], BF16)
    gtb = pr.sbuf(f"{tag}_gtb", [128, 4 * DV], BF16)
    gkr = pr.sbuf(f"{tag}_gkr", [128, OWC * TB], F32)
    wv = sbuf_ring(pr, f"{tag}_wv", 2, [128, 8 * 256], BF16)
    qgr = sbuf_ring(pr, f"{tag}_qg", 2, [128, OWC * 128], BF16)
    kdr = sbuf_ring(pr, f"{tag}_kd", 1, [128, OWC * 128], BF16)
    kgr = sbuf_ring(pr, f"{tag}_kg", 1, [128, OWC * 128], BF16)
    efr = sbuf_ring(pr, f"{tag}_efac", 2, [128, max(OWC * 128, TB)], F32)
    lmr = sbuf_ring(pr, f"{tag}_lamc", 2, [128, OWC], F32)
    atr = sbuf_ring(pr, f"{tag}_atm", 1, [128, 128 * NHD], BF16)
    oger = sbuf_ring(pr, f"{tag}_oge", 1, [128, DV], BF16)
    ogTr = sbuf_ring(pr, f"{tag}_ogT", 1, [128, VC * 128], BF16)
    kgtr = sbuf_ring(pr, f"{tag}_kgt", 2, [128, OWC * 128], BF16)
    sstat = pr.sbuf(f"{tag}_sstat", [128, 6 * NHD], F32)
    sagg = pr.sbuf(f"{tag}_sagg", [128, 2 * NHD], F32)
    srst = pr.sbuf(f"{tag}_srst", [128, 2 * NHD], F32)
    sbf = pr.sbuf(f"{tag}_sbf", [128, 1024], BF16)
    pA = pr.psum(f"{tag}_pA", [128, 128], F32)
    pOr = psum_ring(pr, f"{tag}_pO", 2, [128, HDV], F32)
    pW = pr.psum(f"{tag}_pW", [128, 512], F32)
    pT0 = pr.psum(f"{tag}_pT0", [128, 128], BF16)
    pT1 = pr.psum(f"{tag}_pT1", [128, 128], BF16)

    pv = {"vtb": [], "gtb": [], "gkr": [], "qkg": [], "efac": [], "lamc": [],
          "qkbuf": [], "vhalf": [],
          "atm": [], "oge": [], "osq": [], "ogb": [], "ogT": [], "kgt": [],
          "sstat": [], "sagg": [], "srst": [], "sbf": [], "pA": [],
          "pW": [], "pT0": None, "pT1": None, "S": [hS0], "Sbf": [hS0b]}

    for tb in range(NTB):
        t0 = TB * tb
        tqi = tb
        qk_evs = []
        for (dst, wname, key) in ((qT, "Wq", "qT"), (kT, "Wk", "kT")):
            for oc in range(OWC):
                wb, wwar, ws = wqk.next()
                lw = kb.dma("gpsimd", wb[:, :].rearrange("p (k c) -> p k c", c=128),
                            W[wname][:, 128 * oc:128 * (oc + 1)]
                            .rearrange("(k p) c -> p k c", p=128), deps=[wwar])
                pb_, wpp, spp = pPr.next()
                hm = kb.emit("tensor", lambda e, wb=wb, tqi=tqi, pb_=pb_: [
                    e.matmul(pb_[:, :], wb[:, 128 * k:128 * (k + 1)],
                             xnT[:, T * k + 512 * tqi:T * k + 512 * (tqi + 1)],
                             start=(k == 0), stop=(k == 7)) for k in range(8)][-1],
                    deps=[lw, xnT_dep] + wpp)
                he = kb.emit("scalar", lambda e, dst=dst, oc=oc, pb_=pb_: e.copy(
                    dst[:, TB * oc:TB * (oc + 1)], pb_[:, :]),
                    deps=[hm] + (pv["qkbuf"] if oc == 0 and dst is qT else []))
                wqk.used(ws, hm)
                pPr.used(spp, he)
                qk_evs.append(he)
        qk_done = qk_evs[-1]
        vread, gread = [], []
        for (dst, wname, rdkey) in ((vtb, "Wv", vread), (gtb, "Wg", gread)):
            evs = []
            for ncol in range(DV // 256):
                wb, wwar, ws = wv.next()
                lw = kb.dma("gpsimd", wb[:, :].rearrange("p (k c) -> p k c", c=256),
                            W[wname][:, 256 * ncol:256 * (ncol + 1)]
                            .rearrange("(k p) c -> p k c", p=128), deps=[wwar])
                hm_all = []
                for tt in range(4):
                    pb_, wpp, spp = pPr.next()
                    hm = kb.emit("tensor", lambda e, wb=wb, tt=tt, t0=t0, pb_=pb_: [
                        e.matmul(pb_[:, 0:256],
                                 xnT[:, T * k + t0 + 128 * tt:T * k + t0 + 128 * (tt + 1)],
                                 wb[:, 256 * k:256 * (k + 1)], start=(k == 0), stop=(k == 7))
                        for k in range(8)][-1],
                        deps=[lw, xnT_dep] + wpp)
                    he = kb.emit("scalar", lambda e, dst=dst, tt=tt, ncol=ncol, pb_=pb_: e.copy(
                        dst[:, DV * tt + 256 * ncol:DV * tt + 256 * (ncol + 1)], pb_[:, 0:256]),
                        deps=[hm] + (pv["vtb" if dst is vtb else "gtb"]
                                     if tt == 0 and ncol == 0 else []))
                    pPr.used(spp, he)
                    hm_all.append(hm)
                    evs.append(he)
                wv.used(ws, hm_all)
            if dst is vtb:
                v_ready = evs[-1]
            else:
                g_ready = evs[-1]
        hgs0 = kb.emit("scalar", lambda e: e.activation(gtb[:, :], gtb[:, :], AF.Silu),
                       deps=[g_ready])
        hgs = kb.emit("vector", lambda e: e.tensor_tensor(
            gtb[:, :].rearrange("p (b v) -> p b v", v=HDV),
            gtb[:, :].rearrange("p (b v) -> p b v", v=HDV),
            rw_bc[:, 0:HDV].unsqueeze(1).broadcast_to([128, 4 * NHD, HDV]), ALU.mult),
            deps=[hgs0, rw_ready])
        pv["gtb"] = []
        pv["vtb"] = []
        gk_evs = []
        for oc in range(OWC):
            pb_, wpp, spp = pPr.next()
            hm = kb.emit("tensor", lambda e, oc=oc, t0=t0, pb_=pb_: e.matmul(
                pb_[:, :], wg2[0:16, 128 * oc:128 * (oc + 1)],
                t1T[:, t0:t0 + TB], start=True, stop=True),
                deps=[lg2, t1_ready] + wpp)
            he = kb.emit("vector", lambda e, oc=oc, pb_=pb_: e.tensor_scalar(
                gkr[:, TB * oc:TB * (oc + 1)], pb_[:, :], bg2[:, oc:oc + 1], None,
                ALU.add), deps=[hm, lbg] + (pv["gkr"] if oc == 0 else []))
            pPr.used(spp, he)
            gk_evs.append(he)
        lg_evs = []
        for oc in range(OWC):
            sl = slice(TB * oc, TB * (oc + 1))
            hA = kb.emit("scalar", lambda e, sl=sl: e.activation(
                efac[:, 0:TB], gkr[:, sl], AF.Abs),
                deps=[gk_evs[oc]] + lg_evs[-1:] + pv["efac"])
            hB = kb.emit("scalar", lambda e: e.activation(
                efac[:, 0:TB], efac[:, 0:TB], AF.Exp, scale=-1.0), deps=[hA])
            hC = kb.emit("vector", lambda e: e.tensor_scalar_add(
                efac[:, 0:TB], efac[:, 0:TB], 1.0), deps=[hB])
            hD_ = kb.emit("scalar", lambda e: e.activation(
                efac[:, 0:TB], efac[:, 0:TB], AF.Ln), deps=[hC])
            hE = kb.emit("vector", lambda e, sl=sl: e.tensor_scalar_min(
                gkr[:, sl], gkr[:, sl], 0.0), deps=[hA])
            hF = kb.emit("vector", lambda e, sl=sl: e.tensor_sub(
                gkr[:, sl], gkr[:, sl], efac[:, 0:TB]), deps=[hD_, hE])
            hG = kb.emit("vector", lambda e, sl=sl: e.tensor_scalar_mul(
                gkr[:, sl], gkr[:, sl], 1.0 / gnorm), deps=[hF])
            lg_evs.append(hG)
        pv["efac"] = [lg_evs[-1]]
        sc_evs = []
        for oc in range(OWC):
            for ci in range(CPB):
                sl = slice(TB * oc + CH * ci, TB * oc + CH * (ci + 1))
                hSc = kb.emit("vector", lambda e, sl=sl: e.tensor_tensor_scan(
                    gkr[:, sl], ones_c[:, :].broadcast_to([128, CH]), gkr[:, sl],
                    0.0, ALU.mult, ALU.add), deps=[lg_evs[oc], honc])
                sc_evs.append(hSc)
        bc_ready = sc_evs[-1]
        pv["gkr"] = []

        for ci in range(CPB):
            c = CPB * tb + ci
            tt = ci
            qg, wqg, sqg = qgr.next()
            kd, wkd, skd = kdr.next()
            kg, wkg, skg = kgr.next()
            efac, wef, sef = efr.next()
            lamc, wlm, slm = lmr.next()
            atm, wat, sat = atr.next()
            oge, wog, sog = oger.next()
            osq = oge
            ogT, wot, sot = ogTr.next()
            kgt, wkt, skt = kgtr.next()
            gk3 = gkr[:, :].rearrange("p (o t) -> p o t", o=OWC)[:, :, CH * ci:CH * (ci + 1)]
            qT3 = qT[:, :].rearrange("p (o t) -> p o t", o=OWC)[:, :, CH * ci:CH * (ci + 1)]
            kT3 = kT[:, :].rearrange("p (o t) -> p o t", o=OWC)[:, :, CH * ci:CH * (ci + 1)]
            ef3 = efac[:, 0:OWC * CH].rearrange("p (o t) -> p o t", t=CH)
            qg3 = qg[:, :].rearrange("p (o t) -> p o t", t=CH)
            kd3 = kd[:, :].rearrange("p (o t) -> p o t", t=CH)
            kg3 = kg[:, :].rearrange("p (o t) -> p o t", t=CH)
            h1 = kb.emit("scalar", lambda e, gk3=gk3, ef3=ef3: e.activation(
                ef3, gk3, AF.Exp), deps=[bc_ready] + wef)
            h2 = kb.emit("vector", lambda e, ef3=ef3, qT3=qT3, qg3=qg3: e.scalar_tensor_tensor(
                qg3, ef3, scale, qT3, ALU.mult, ALU.mult), deps=[h1, qk_done] + wqg)
            h3 = kb.emit("scalar", lambda e, gk3=gk3, ef3=ef3: e.activation(
                ef3, gk3, AF.Exp, scale=-1.0), deps=[h2])
            h4 = kb.emit("vector", lambda e, ef3=ef3, kT3=kT3, kd3=kd3: e.tensor_tensor(
                kd3, ef3, kT3, ALU.mult), deps=[h3, qk_done] + wkd)
            h5s = []
            for oc in range(OWC):
                sl = slice(TB * oc + CH * ci, TB * oc + CH * (ci + 1))
                lastc = TB * oc + CH * ci + CH - 1
                h5 = kb.emit("scalar", lambda e, sl=sl, lastc=lastc, oc=oc, efac=efac: e.activation(
                    efac[:, CH * oc:CH * (oc + 1)], gkr[:, sl], AF.Exp, scale=-1.0,
                    bias=gkr[:, lastc:lastc + 1]), deps=[h4])
                h5s.append(h5)
            h6 = kb.emit("vector", lambda e, kT3=kT3, kg3=kg3, efac=efac: e.tensor_tensor(
                kg3, efac[:, 0:OWC * CH].rearrange("p (o t) -> p o t", t=CH),
                kT3, ALU.mult), deps=h5s + [qk_done] + wkg)
            hlam = kb.emit("scalar", lambda e, ci=ci, lamc=lamc: e.activation(
                lamc[:, :], gkr[:, CH * ci + CH - 1:CH * ci + CH + TB * (OWC - 1):TB] if OWC > 1
                else gkr[:, CH * ci + CH - 1:CH * ci + CH], AF.Exp),
                deps=[bc_ready] + wlm)
            och = []
            for h in range(NHD):
                hA1 = kb.emit("tensor", lambda e, h=h, qg=qg, kd=kd: [
                    e.matmul(pA[:, :], kd[:, CH * (OWH * h + j):CH * (OWH * h + j + 1)],
                             qg[:, CH * (OWH * h + j):CH * (OWH * h + j + 1)],
                             start=(j == 0), stop=(j == OWH - 1)) for j in range(OWH)][-1],
                    deps=[h2, h4] + pv["pA"])
                hA2 = kb.emit("vector", lambda e, h=h, atm=atm: e.tensor_tensor(
                    atm[:, CH * h:CH * (h + 1)], pA[:, :], C.u128[:, :], ALU.mult),
                    deps=[hA1, C.ready] + (wat if h == 0 else []))
                pv["pA"] = [hA2]
                och.append(hA2)
            hO1s = []
            ev_evs = []
            st_evs = []
            for h in range(NHD):
                pO, wpo, spo = pOr.next()

                def omm(e, h=h, tt=tt, pO=pO, atm=atm, qg=qg):
                    i1 = e.matmul(pO[:, :], atm[:, CH * h:CH * (h + 1)],
                                  vtb[:, DV * tt + HDV * h:DV * tt + HDV * (h + 1)],
                                  start=True, stop=False)
                    for j in range(OWH):
                        oc = OWH * h + j
                        i1 = e.matmul(pO[:, :], qg[:, CH * oc:CH * (oc + 1)],
                                      S_bf[:, HDV * oc:HDV * (oc + 1)],
                                      start=False, stop=(j == OWH - 1))
                    return i1
                hO1 = kb.emit("tensor", omm, deps=[och[h], h2, v_ready] + pv["Sbf"] + wpo)
                hO1s.append(hO1)
                hOe = kb.emit("scalar", lambda e, h=h, pO=pO, oge=oge: e.copy(
                    oge[:, HDV * h:HDV * (h + 1)], pO[:, :]),
                    deps=[hO1] + (wog if h == 0 else []))
                ev_evs.append(hOe)
                hst = kb.emit("vector", lambda e, h=h, oge=oge: e.bn_stats(
                    sstat[:, 6 * h:6 * (h + 1)], oge[:, HDV * h:HDV * (h + 1)]),
                    deps=[hOe] + (pv["sstat"] if h == 0 else []))
                hag = kb.emit("vector", lambda e, h=h: e.bn_aggr(
                    sagg[:, 2 * h:2 * (h + 1)], sstat[:, 6 * h:6 * (h + 1)]),
                    deps=[hst] + (pv["sagg"] if h == 0 else []))
                st_evs.append(hag)
                pOr.used(spo, [hOe])
            hr1 = kb.emit("vector", lambda e: e.tensor_tensor(
                srst[:, 0:NHD], sagg[:, 0:2 * NHD:2], sagg[:, 0:2 * NHD:2], ALU.mult),
                deps=st_evs + pv["srst"])
            hr2 = kb.emit("vector", lambda e: e.tensor_tensor(
                srst[:, NHD:2 * NHD], srst[:, 0:NHD], sagg[:, 1:2 * NHD:2], ALU.add),
                deps=[hr1] + st_evs)
            hr3 = kb.emit("vector", lambda e: e.tensor_scalar_add(
                srst[:, NHD:2 * NHD], srst[:, NHD:2 * NHD], 1e-5), deps=[hr2])
            hr4 = kb.emit("scalar", lambda e: e.activation(
                srst[:, NHD:2 * NHD], srst[:, NHD:2 * NHD], AF.Ln), deps=[hr3])
            hr5 = kb.emit("scalar", lambda e: e.activation(
                srst[:, NHD:2 * NHD], srst[:, NHD:2 * NHD], AF.Exp, scale=-0.5), deps=[hr4])
            pv["sstat"] = [hag]
            pv["sagg"] = [hr2]
            pv["srst"] = [hr5]
            hN = kb.emit("vector", lambda e, osq=osq, oge=oge: e.tensor_tensor(
                osq[:, :].rearrange("p (h v) -> p h v", v=HDV),
                oge[:, :].rearrange("p (h v) -> p h v", v=HDV),
                srst[:, NHD:2 * NHD].unsqueeze(2).broadcast_to([128, NHD, HDV]), ALU.mult),
                deps=[hr5] + ev_evs)
            hG1 = kb.emit("vector", lambda e, tt=tt, osq=osq: e.tensor_tensor(
                osq[:, :], osq[:, :], gtb[:, DV * tt:DV * (tt + 1)],
                ALU.mult), deps=[hN, hgs])
            tprev = [pv["pT0"], pv["pT1"]]
            tevs = []
            for vcb in range(VC):
                slot = vcb % 2
                pTt = pT0 if slot == 0 else pT1
                hT1 = kb.emit("tensor", lambda e, vcb=vcb, pTt=pTt, osq=osq: e.transpose(
                    pTt[:, :], osq[:, 128 * vcb:128 * (vcb + 1)], C.ident[:, :]),
                    deps=[hG1, C.ready, tprev[slot]])
                hT2 = kb.emit("scalar", lambda e, vcb=vcb, pTt=pTt, ogT=ogT: e.copy(
                    ogT[:, 128 * vcb:128 * (vcb + 1)], pTt[:, :]),
                    deps=[hT1] + (wot if vcb < 2 else []))
                tprev[slot] = hT2
                tevs.append(hT2)
            pv["pT0"], pv["pT1"] = tprev
            oger.used(sog, [tevs[-1], hG1])
            hW = kb.emit("tensor", lambda e, ogT=ogT: [
                e.matmul(pW[:, :], ogT[:, 128 * vcb:128 * (vcb + 1)],
                         wo_sb[:, 1024 * vcb:1024 * vcb + 512],
                         start=(vcb == 0), stop=(vcb == VC - 1))
                for vcb in range(VC)][-1],
                deps=tevs + lwo + pv["pW"])
            hWe = kb.emit("scalar", lambda e: e.copy(sbf[:, 0:512], pW[:, :]),
                          deps=[hW] + pv["sbf"])
            hWb = kb.emit("tensor", lambda e, ogT=ogT: [
                e.matmul(pW[:, :], ogT[:, 128 * vcb:128 * (vcb + 1)],
                         wo_sb[:, 1024 * vcb + 512:1024 * vcb + 1024],
                         start=(vcb == 0), stop=(vcb == VC - 1))
                for vcb in range(VC)][-1],
                deps=tevs + lwo + [hWe])
            hWe2 = kb.emit("scalar", lambda e: e.copy(sbf[:, 512:1024], pW[:, :]),
                           deps=[hWb])
            hWd = kb.dma("sync", s_dram[CH * c:CH * (c + 1), :], sbf[:, :], deps=[hWe, hWe2])
            ogTr.used(sot, hWb)
            pv["sbf"] = [hWd]
            pv["pW"] = [hWe2]
            ktevs = []
            tprev2 = [pv["pT0"], pv["pT1"]]
            for oc in range(OWC):
                slot = oc % 2
                pTt = pT0 if slot == 0 else pT1
                hK1 = kb.emit("tensor", lambda e, oc=oc, pTt=pTt, kg=kg: e.transpose(
                    pTt[:, :], kg[:, CH * oc:CH * (oc + 1)], C.ident[:, :]),
                    deps=[h6, C.ready, tprev2[slot]])
                hK2 = kb.emit("scalar", lambda e, oc=oc, pTt=pTt, kgt=kgt: e.copy(
                    kgt[:, 128 * oc:128 * (oc + 1)], pTt[:, :]),
                    deps=[hK1] + (wkt if oc < 2 else []))
                tprev2[slot] = hK2
                ktevs.append(hK2)
            pv["pT0"], pv["pT1"] = tprev2
            hSd = kb.emit("vector", lambda e, lamc=lamc: e.tensor_tensor(
                S_sb[:, :].rearrange("p (o v) -> p o v", o=OWC),
                S_sb[:, :].rearrange("p (o v) -> p o v", o=OWC),
                lamc[:, :].unsqueeze(2).broadcast_to([128, OWC, HDV]), ALU.mult),
                deps=[hlam] + pv["S"])
            lmr.used(slm, hSd)
            sdec = [hSd]
            hD3s = []
            for oc in range(OWC):
                h = oc // OWH
                pb_, wpp, spp = pPr.next()
                hD3 = kb.emit("tensor", lambda e, oc=oc, tt=tt, h=h, kgt=kgt, pb_=pb_: e.matmul(
                    pb_[:, 0:HDV], kgt[:, 128 * oc:128 * (oc + 1)],
                    vtb[:, DV * tt + HDV * h:DV * tt + HDV * (h + 1)],
                    start=True, stop=True),
                    deps=[ktevs[oc], v_ready] + wpp)
                hD3s.append(hD3)
                hD4 = kb.emit("vector", lambda e, oc=oc, pb_=pb_: e.tensor_add(
                    S_sb[:, HDV * oc:HDV * (oc + 1)], S_sb[:, HDV * oc:HDV * (oc + 1)],
                    pb_[:, 0:HDV]), deps=[hSd, hD3])
                pPr.used(spp, hD4)
                sdec.append(hD4)
            h_cbS = kb.emit("vector", lambda e: e.tensor_copy(S_bf[:, :], S_sb[:, :]),
                            deps=sdec + hO1s[-1:])
            pv["S"] = [h_cbS]
            pv["Sbf"] = [h_cbS]
            kgtr.used(skt, hD3s)
            qgr.used(sqg, [hD3s[-1], h2, hO1s[-1]])
            kdr.used(skd, [hA1, h4])
            kgr.used(skg, [hD3s[-1], h6])
            efr.used(sef, [h2, h4, h6])
            atr.used(sat, hO1s)
        pv["vtb"] = [hD3s[-1], hO1s[-1]]
        pv["gtb"] = [hG1]
        pv["qkbuf"] = [hD3s[-1], h6, h2, h4]
    pr.pop()
    return None



def emit_quantize(pr, C, w_dram, KC, HC, wq_sb, tag):
    kb = pr
    CW = min(HC, 1024)
    NCW = HC // CW
    pr.push()
    ring = sbuf_ring(pr, f"{tag}_ring", 6, [128, CW], F32)
    gcol = pr.sbuf(f"{tag}_gcol", [128, KC * NCW], F32)
    absr = sbuf_ring(pr, f"{tag}_abs", 2, [128, CW], F32)
    stt = pr.sbuf(f"{tag}_stt", [128, 2], F32)
    assert KC * NCW <= 128
    onec = pr.sbuf(f"{tag}_onec", [128, 1], F32)
    onesf = pr.sbuf(f"{tag}_onesf", [1, 128], F32)
    pg = pr.psum(f"{tag}_pg", [128, 8], F32)
    h1 = kb.emit("vector", lambda e: e.memset(onec[:, :], 1.0))
    h2 = kb.emit("vector", lambda e: e.memset(onesf[:, :], 1.0))
    red_evs = []
    for kc in range(KC):
        for j in range(NCW):
            wb, war, ws = ring.next()
            hl = kb.dma(("sync", "scalar", "gpsimd")[(kc * NCW + j) % 3], wb[:, :],
                        w_dram[128 * kc:128 * (kc + 1), CW * j:CW * (j + 1)], deps=[war])
            hr = kb.emit("vector", lambda e, wb=wb, kc=kc, j=j: e.reduce_sum(
                gcol[:, NCW * kc + j:NCW * kc + j + 1], wb[:, :], AX,
                apply_absolute_value=True), deps=[hl])
            ring.used(ws, hr)
            red_evs.append(hr)
    NKC = KC * NCW
    hm1 = kb.emit("tensor", lambda e: e.matmul(pg[0:NKC, 0:1], gcol[:, :], onec[:, :],
                                               start=True, stop=True),
                  deps=red_evs + [h1])
    hm2 = kb.emit("vector", lambda e: e.tensor_copy(stt[0:NKC, 0:1], pg[0:NKC, 0:1]), deps=[hm1])
    hm3 = kb.emit("tensor", lambda e: e.matmul(pg[0:1, 1:2], stt[0:NKC, 0:1], onec[0:NKC, :],
                                               start=True, stop=True), deps=[hm2])
    hm4 = kb.emit("vector", lambda e: e.tensor_scalar(
        stt[0:1, 1:2], pg[0:1, 1:2], 1.0 / (128 * KC * HC), 1e-6, ALU.mult, ALU.add),
        deps=[hm3])
    hm5 = kb.emit("vector", lambda e: e.tensor_scalar_mul(stt[0:1, 1:2], stt[0:1, 1:2], 0.5), deps=[hm4])
    hm6 = kb.emit("tensor", lambda e: e.matmul(pg[:, 2:3], onesf[:, :], stt[0:1, 1:2],
                                               start=True, stop=True), deps=[hm5, h2])
    thr = pr.sbuf(f"{tag}_thr", [128, 1], F32)
    hm7 = kb.emit("vector", lambda e: e.tensor_copy(thr[:, :], pg[:, 2:3]), deps=[hm6])
    q_evs = []
    for kc in range(KC):
      for j in range(NCW):
        wb, war, ws = ring.next()
        hl = kb.dma(("sync", "scalar", "gpsimd")[(kc * NCW + j) % 3], wb[:, :],
                    w_dram[128 * kc:128 * (kc + 1), CW * j:CW * (j + 1)], deps=[war])
        ab, awar, asl = absr.next()
        ha = kb.emit("scalar", lambda e, wb=wb, ab=ab: e.activation(ab[:, :], wb[:, :], AF.Abs),
                     deps=[hl] + awar)
        hg = kb.emit("vector", lambda e, ab=ab: e.tensor_scalar(
            ab[:, :], ab[:, :], thr[:, :], None, ALU.is_ge), deps=[ha, hm7])
        hs1 = kb.emit("gpsimd", lambda e, wb=wb: e.tensor_single_scalar(
            wb[:, :], wb[:, :], 0.0, ALU.is_ge), deps=[hl, ha])
        hs2 = kb.emit("gpsimd", lambda e, wb=wb: e.tensor_scalar(
            wb[:, :], wb[:, :], 2.0, -1.0, ALU.mult, ALU.add), deps=[hs1])
        hq = kb.emit("vector", lambda e, wb=wb, ab=ab, kc=kc, j=j: e.tensor_tensor(
            wq_sb[:, HC * kc + CW * j:HC * kc + CW * (j + 1)], ab[:, :], wb[:, :],
            ALU.mult), deps=[hg, hs2])
        ring.used(ws, [hq])
        absr.used(asl, hq)
        q_evs.append(hq)
    pr.pop()
    return q_evs[-1]


class QBG:

    CW = 512
    CW2 = 256

    def __init__(self, pr):
        self.pr, self.kb = pr, pr.kb
        kb = pr.kb
        wfc = pr.din("L1_ffn_W_fcT", [1024, 4096])
        wproj = pr.din("L1_ffn_W_projT", [4096, 1024])
        self.wq_dram = pr.dint("qbg_wq", [8, 128, 4096], BF16)
        self.wp_dram = pr.dint("qbg_wp", [32, 128, 1024], BF16)
        self.init_dep = []
        self.ring = self.outr = self.absr = None
        CW = self.CW
        self.gcol = pr.sbuf("qbg_gcol", [128, 128], F32)
        self.thr = pr.sbuf("qbg_thr", [128, 2], F32)
        self.onec = pr.sbuf("qbg_onec", [128, 1], F32)
        self.onesf = pr.sbuf("qbg_onesf", [1, 128], F32)
        self.misc = pr.sbuf("qbg_misc", [128, 4], F32)
        self.h_ones = [kb.emit("vector", lambda e: e.memset(self.onec[:, :], 1.0)),
                       kb.emit("vector", lambda e: e.memset(self.onesf[:, :], 1.0))]
        items = []
        for kc in range(8):
            for j in range(8):
                items.append((0, wfc[128 * kc:128 * (kc + 1), CW * j:CW * (j + 1)],
                              kc * 8 + j, None, None, CW))
        for kc in range(32):
            for j in range(2):
                items.append((1, wproj[128 * kc:128 * (kc + 1), CW * j:CW * (j + 1)],
                              64 + kc * 2 + j, None, None, CW))
        C2 = self.CW2
        for kc in range(8):
            for j in range(16):
                items.append((0, wfc[128 * kc:128 * (kc + 1), C2 * j:C2 * (j + 1)],
                              None, self.wq_dram[kc, :, C2 * j:C2 * (j + 1)],
                              (0, kc, j), C2))
        for kc in range(32):
            for j in range(4):
                items.append((1, wproj[128 * kc:128 * (kc + 1), C2 * j:C2 * (j + 1)],
                              None, self.wp_dram[kc, :, C2 * j:C2 * (j + 1)],
                              (1, kc, j), C2))
        self.items = items
        self.gate = 128
        self.di_cap = 128
        self.ci = 0
        self.di = 0
        self.pend = {}
        self.p1_evs = [[], []]
        self.out_h = {}
        self.thresh_done = None

    def open_arena(self, p2=False):
        pr = self.pr
        self.init_dep = list(pr.phase_dep)
        w = self.CW2 if p2 else self.CW
        self.LEAD = 6 if p2 else 4
        self.ring = sbuf_ring(pr, "qbg_in", 8 if p2 else 6, [128, w], F32)
        if p2:
            self.outr = sbuf_ring(pr, "qbg_out", 1, [128, w], BF16)
            self.absr = sbuf_ring(pr, "qbg_abs", 1, [128, w], F32)
            self.di_cap = len(self.items)

    def _issue_dma(self):
        m, src, gi, dst, key, w = self.items[self.di]
        wb, war, slot = self.ring.next()
        hl = self.kb.dma("sync", wb[:, 0:w], src, deps=war + self.init_dep)
        self.pend[self.di] = (wb, slot, hl)
        self.di += 1

    def _emit_compute(self, i):
        kb = self.kb
        m, src, gi, dst, key, w = self.items[i]
        wb, slot, hl = self.pend.pop(i)
        if dst is None:
            hr = kb.emit("vector", lambda e, wb=wb, gi=gi, w=w: e.reduce_sum(
                self.gcol[:, gi:gi + 1], wb[:, 0:w], AX,
                apply_absolute_value=True), deps=[hl])
            self.ring.used(slot, hr)
            self.p1_evs[m].append(hr)
            return
        ab, awar, aslot = self.absr.next()
        ha = kb.emit("scalar", lambda e, wb=wb, ab=ab, w=w: e.activation(
            ab[:, 0:w], wb[:, 0:w], AF.Abs), deps=[hl] + awar)
        hg = kb.emit("vector", lambda e, ab=ab, m=m, w=w: e.tensor_scalar(
            ab[:, 0:w], ab[:, 0:w], self.thr[:, m:m + 1], None, ALU.is_ge),
            deps=[ha] + self.thresh_done)
        hs1 = kb.emit("vector", lambda e, wb=wb, w=w: e.tensor_single_scalar(
            wb[:, 0:w], wb[:, 0:w], 0.0, ALU.is_ge), deps=[hl, ha])
        hs2 = kb.emit("vector", lambda e, wb=wb, w=w: e.tensor_scalar(
            wb[:, 0:w], wb[:, 0:w], 2.0, -1.0, ALU.mult, ALU.add), deps=[hs1])
        ob, owar, oslot = self.outr.next()
        hq = kb.emit("vector", lambda e, ob=ob, ab=ab, wb=wb, w=w: e.tensor_tensor(
            ob[:, 0:w], ab[:, 0:w], wb[:, 0:w], ALU.mult), deps=[hg, hs2] + owar)
        self.ring.used(slot, hq)
        self.absr.used(aslot, hq)
        hw = kb.dma("sync", dst, ob[:, 0:w], deps=[hq])
        self.outr.used(oslot, hw)
        self.out_h[key] = hw

    def pump(self, n=2):
        if self.ring is None:
            return
        for _ in range(n):
            while self.di < self.di_cap and self.di < self.ci + self.LEAD:
                self._issue_dma()
            if self.ci >= len(self.items):
                return
            if self.ci >= self.gate and (self.thresh_done is None
                                         or self.absr is None):
                return
            self._emit_compute(self.ci)
            self.ci += 1
        while self.di < self.di_cap and self.di < self.ci + self.LEAD:
            self._issue_dma()

    def drain_p1(self):
        while self.ci < self.gate:
            self.pump(8)

    def emit_thresh(self):
        kb, pr = self.kb, self.pr
        pg = pr.psum("qbg_pg", [128, 8], F32)
        phase = list(pr.phase_dep)
        N = 1024 * 4096
        hs = []
        for m in (0, 1):
            h1 = kb.emit("vector", lambda e, m=m: e.reduce_sum(
                self.misc[:, m:m + 1], self.gcol[:, 64 * m:64 * m + 64], AX),
                deps=self.p1_evs[m])
            h2 = kb.emit("tensor", lambda e, m=m: e.matmul(
                pg[0:1, m:m + 1], self.misc[:, m:m + 1], self.onec[:, :],
                start=True, stop=True), deps=[h1] + self.h_ones + phase)
            h3 = kb.emit("vector", lambda e, m=m: e.tensor_scalar(
                self.misc[0:1, 2 + m:3 + m], pg[0:1, m:m + 1],
                0.5 / N, 0.5e-6, ALU.mult, ALU.add), deps=[h2])
            h4 = kb.emit("tensor", lambda e, m=m: e.matmul(
                pg[:, 4 + m:5 + m], self.onesf[:, :], self.misc[0:1, 2 + m:3 + m],
                start=True, stop=True), deps=[h3] + self.h_ones + phase)
            h5 = kb.emit("vector", lambda e, m=m: e.tensor_copy(
                self.thr[:, m:m + 1], pg[:, 4 + m:5 + m]), deps=[h4])
            hs.append(h5)
        self.thresh_done = hs

    def drain(self):
        while True:
            before = (self.ci, self.di)
            self.pump(1000)
            if (self.ci, self.di) == before:
                break
        assert self.ci == len(self.items), "QBG not drained (thresh missing?)"


def emit_bitffn(pr, C, xnT, xnT_dep, W, x_in_ap, x_out_ap, s_dram, tag, qbg=None):
    kb = pr
    H = 4096
    wq_sb = pr.sbuf(f"{tag}_wq", [128, 8 * H], BF16)
    wp_sb = pr.sbuf(f"{tag}_wp", [128, 32 * 1024], BF16)
    if qbg is None:
        hq1 = emit_quantize(pr, C, W["W_fcT"], 8, H, wq_sb, tag + "q1")
        hq2 = emit_quantize(pr, C, W["W_projT"], 32, 1024, wq_sb=wp_sb, tag=tag + "q2")
        wq_ready = [[hq1] for _ in range(8)]
        wp_ready = [hq2]
    else:
        wq_ready = []
        for hc in range(8):
            wq_ready.append([pr.dma(
                "sync", wq_sb[:, H * kc + 512 * hc:H * kc + 512 * (hc + 1)],
                qbg.wq_dram[kc, :, 512 * hc:512 * (hc + 1)],
                deps=[qbg.out_h[(0, kc, 2 * hc)], qbg.out_h[(0, kc, 2 * hc + 1)]])
                for kc in range(8)])
        wp_ready = [pr.dma(
            "sync", wp_sb[:, 1024 * kc2:1024 * (kc2 + 1)], qbg.wp_dram[kc2],
            deps=[qbg.out_h[(1, kc2, j)] for j in range(4)])
            for kc2 in range(32)]
    pr.push()
    lnwc = pr.sbuf(f"{tag}_lnwc", [128, 32], F32)
    lnbc = pr.sbuf(f"{tag}_lnbc", [128, 32], F32)
    hlw = kb.dma("gpsimd", lnwc[:, :], W["ln_w"].rearrange("(c p) -> p c", p=128),
                 allow_slow_non_contiguous=True)
    hlb = kb.dma("gpsimd", lnbc[:, :], W["ln_b"].rearrange("(c p) -> p c", p=128),
                 allow_slow_non_contiguous=True)
    ln_ready = [hlw, hlb]
    hbr = sbuf_ring(pr, f"{tag}_hbuf", 2, [128, H], BF16)
    lnr = sbuf_ring(pr, f"{tag}_lnhT", 2, [128, 32 * 128], BF16)
    stat = pr.sbuf(f"{tag}_stat", [128, 8], F32)
    ph = psum_ring(pr, f"{tag}_ph", 2, [128, 512], F32)
    por = psum_ring(pr, f"{tag}_po", 2, [128, 1024], F32)
    pt0 = pr.psum(f"{tag}_pt0", [128, 128], BF16)
    pt1 = pr.psum(f"{tag}_pt1", [128, 128], BF16)
    xr = sbuf_ring(pr, f"{tag}_xr", 1, [128, D], F32)
    sr = sbuf_ring(pr, f"{tag}_sr", 1, [128, D], BF16)
    ot = sbuf_ring(pr, f"{tag}_ot", 1, [128, D], F32)
    pv = {"stat": [], "pt0": None, "pt1": None}
    out_h = []

    def h_phase(tt, hb, hb_war):
        hevs = []
        for hc in range(8):
            pb, wp_, sp = ph.next()
            hm = kb.emit("tensor", lambda e, pb=pb, hc=hc, tt=tt: [
                e.matmul(pb[:, :], xnT[:, T * kc + 128 * tt:T * kc + 128 * (tt + 1)],
                         wq_sb[:, H * kc + 512 * hc:H * kc + 512 * (hc + 1)],
                         start=(kc == 0), stop=(kc == 7)) for kc in range(8)][-1],
                deps=wq_ready[hc] + [xnT_dep, wp_])
            he = kb.emit("scalar", lambda e, pb=pb, hc=hc, hb=hb: e.activation(
                hb[:, 512 * hc:512 * (hc + 1)], pb[:, :], AF.Gelu),
                deps=[hm] + (hb_war if hc == 0 else []))
            ph.used(sp, he)
            hevs.append(he)
        return hevs

    def stats(tt, hb, hevs):
        lnhT, wln, sln = lnr.next()
        h1 = kb.emit("vector", lambda e, hb=hb: e.reduce_sum(stat[:, 0:1], hb[:, :], AX),
                     deps=hevs + pv["stat"])
        h2 = kb.emit("scalar", lambda e, hb=hb, lnhT=lnhT: e.activation(
            lnhT[:, :], hb[:, :], AF.Square), deps=hevs + wln)
        h3 = kb.emit("vector", lambda e, lnhT=lnhT: e.reduce_sum(stat[:, 1:2], lnhT[:, :], AX),
                     deps=[h2] + wln)
        h4 = kb.emit("vector", lambda e: e.tensor_scalar_mul(stat[:, 2:3], stat[:, 0:1], 1.0 / H), deps=[h1])
        h5 = kb.emit("vector", lambda e: e.tensor_tensor(stat[:, 3:4], stat[:, 2:3], stat[:, 2:3], ALU.mult), deps=[h4])
        h6 = kb.emit("vector", lambda e: e.tensor_scalar(
            stat[:, 4:5], stat[:, 1:2], 1.0 / H, stat[:, 3:4], ALU.mult, ALU.subtract), deps=[h3, h5])
        h7 = kb.emit("vector", lambda e: e.tensor_scalar_add(stat[:, 5:6], stat[:, 4:5], 1e-5), deps=[h6])
        h8 = kb.emit("scalar", lambda e: e.sqrt(stat[:, 6:7], stat[:, 5:6]), deps=[h7])
        h9 = kb.emit("vector", lambda e: e.reciprocal(stat[:, 7:8], stat[:, 6:7]), deps=[h8])
        hA = kb.emit("vector", lambda e, hb=hb: e.tensor_scalar(
            hb[:, :], hb[:, :], stat[:, 2:3], stat[:, 7:8], ALU.subtract, ALU.mult),
            deps=[h9, h2, h4])
        pv["stat"] = [hA]
        return (lnhT, sln, hA, h2)

    def rest(tt, hb, hbs, st):
        lnhT, sln, hA, h2 = st
        tprev = [pv["pt0"], pv["pt1"]]
        tevs = []
        tT1 = None
        for kc2 in range(32):
            slot = kc2 % 2
            ptx = pt0 if slot == 0 else pt1
            hT1 = kb.emit("tensor", lambda e, kc2=kc2, ptx=ptx, hb=hb: e.transpose(
                ptx[:, :], hb[:, 128 * kc2:128 * (kc2 + 1)], C.ident[:, :]),
                deps=[hA, C.ready, tprev[slot]])
            hT2 = kb.emit("vector", lambda e, kc2=kc2, ptx=ptx, lnhT=lnhT: e.tensor_scalar(
                lnhT[:, 128 * kc2:128 * (kc2 + 1)], ptx[:, :],
                lnwc[:, kc2:kc2 + 1], lnbc[:, kc2:kc2 + 1], ALU.mult, ALU.add),
                deps=[hT1] + ln_ready)
            tprev[slot] = hT2
            tT1 = hT1
            tevs.append(hT2)
        pv["pt0"], pv["pt1"] = tprev
        hbr.used(hbs, [hA, h2, tT1])
        po, wpo, spo = por.next()
        hO = kb.emit("tensor", lambda e, lnhT=lnhT, po=po: [
            e.matmul(po[:, 512 * nh:512 * (nh + 1)], lnhT[:, 128 * kc2:128 * (kc2 + 1)],
                     wp_sb[:, 1024 * kc2 + 512 * nh:1024 * kc2 + 512 * (nh + 1)],
                     start=(kc2 == 0), stop=(kc2 == 31))
            for nh in range(2) for kc2 in range(32)][-1],
            deps=tevs + wp_ready + wpo)
        lnr.used(sln, hO)
        ob, wo_, so = ot.next()
        hsil = kb.emit("scalar", lambda e, ob=ob, po=po: e.activation(ob[:, :], po[:, :], AF.Silu),
                       deps=[hO] + wo_)
        por.used(spo, hsil)
        xb, wx, sx = xr.next()
        hx = kb.dma("scalar", xb[:, :], x_in_ap[128 * tt:128 * (tt + 1), :], deps=[wx])
        sb_, wsr, ssx = sr.next()
        hs = kb.dma("gpsimd", sb_[:, :], s_dram[128 * tt:128 * (tt + 1), :], deps=[wsr])
        hm1 = kb.emit("vector", lambda e, ob=ob, sb_=sb_: e.tensor_tensor(
            ob[:, :], ob[:, :], sb_[:, :], ALU.mult), deps=[hsil, hs])
        hm2 = kb.emit("vector", lambda e, ob=ob, xb=xb: e.tensor_tensor(
            ob[:, :], ob[:, :], xb[:, :], ALU.add), deps=[hm1, hx])
        hw = kb.dma("gpsimd", x_out_ap[128 * tt:128 * (tt + 1), :], ob[:, :], deps=[hm2])
        xr.used(sx, hm2)
        sr.used(ssx, hm1)
        ot.used(so, hw)
        out_h.append(hw)

    prev = None
    for tt in range(NT):
        st = stats(*prev[:2], prev[3]) if prev is not None else None
        hb, hbw, hbs = hbr.next()
        hevs = h_phase(tt, hb, hbw)
        if prev is not None:
            rest(prev[0], prev[1], prev[2], st)
        prev = (tt, hb, hbs, hevs)
    st = stats(*prev[:2], prev[3])
    rest(prev[0], prev[1], prev[2], st)
    pr.pop()
    return out_h


def emit_layer1(pr, C, xnT, x_in, x_out, qbg=None):
    pr.push()
    ln_done = emit_ln(pr, C, x_in, xnT, tag="ln1")
    pr.pop()
    s_dram = pr.dint("sp_s1", [T, D], BF16)
    W = {k: pr.din("L1_attn_" + k,
                   dict(Wq=[1024, 1024], Wk=[1024, 1024], Wv=[1024, 2048],
                        Wg=[1024, 2048], Wgk1=[1024, 16], Wgk2=[16, 1024],
                        bgk2=[1024], Wo=[2048, 1024], rms_w=[512])[k])
         for k in ("Wq", "Wk", "Wv", "Wg", "Wgk1", "Wgk2", "bgk2", "Wo", "rms_w")}
    pr.push()
    emit_gla(pr, C, xnT, ln_done, W,
             dict(DK=1024, DV=2048, HEADS=4, GATE_NORM=16.0), s_dram, "g1")
    pr.pop()
    WF = dict(ln_w=pr.din("L1_ffn_ln_w", [4096]),
              ln_b=pr.din("L1_ffn_ln_b", [4096]))
    if qbg is None:
        WF["W_fcT"] = pr.din("L1_ffn_W_fcT", [1024, 4096])
        WF["W_projT"] = pr.din("L1_ffn_W_projT", [4096, 1024])
    pr.push()
    hs = emit_bitffn(pr, C, xnT, ln_done, WF, x_in, x_out, s_dram, "bf1", qbg=qbg)
    pr.pop()
    return hs


def l0_moe_cfg(pr):
    w1 = pr.din("L0_ffn_w1", [8, 1024, 2048])
    b1 = pr.din("L0_ffn_b1", [8, 2048])
    w2 = pr.din("L0_ffn_w2", [8, 2048, 1024])
    b2 = pr.din("L0_ffn_b2", [8, 1024])
    sw1 = pr.din("L0_ffn_sw1", [2, 1024, 2048])
    sb1 = pr.din("L0_ffn_sb1", [2, 2048])
    sw2 = pr.din("L0_ffn_sw2", [2, 2048, 1024])
    sb2 = pr.din("L0_ffn_sb2", [2, 1024])
    experts = [dict(w1=w1[e], w2=w2[e], b1=b1[e], comb=("col", e)) for e in range(8)]
    experts += [dict(w1=sw1[s], w2=sw2[s], b1=sb1[s], comb=("const", 0.5)) for s in range(2)]
    return dict(router_w=pr.din("L0_ffn_router_w", [1024, 8]),
                router_b=pr.din("L0_ffn_router_b", [8]),
                router_kind="softmax_top2", act=AF.Gelu, experts=experts,
                bias_items=[("comb", b2), ("mean", sb2, 0.5)])


def l2_moe_cfg(pr):
    ew1 = pr.din("L2_ffn_ew1", [8, 1024, 4096])
    eb1 = pr.din("L2_ffn_eb1", [8, 4096])
    ew2 = pr.din("L2_ffn_ew2", [8, 4096, 1024])
    eb2 = pr.din("L2_ffn_eb2", [8, 1024])
    sw1 = pr.din("L2_ffn_sw1", [1024, 4096])
    sb1 = pr.din("L2_ffn_sb1", [4096])
    sw2 = pr.din("L2_ffn_sw2", [4096, 1024])
    sb2 = pr.din("L2_ffn_sb2", [1024])
    experts = []
    for e in range(8):
        for hh in range(2):
            experts.append(dict(w1=ew1[e][:, 2048 * hh:2048 * (hh + 1)],
                                w2=ew2[e][2048 * hh:2048 * (hh + 1), :],
                                b1=eb1[e][2048 * hh:2048 * (hh + 1)],
                                comb=("col", e)))
    for hh in range(2):
        experts.append(dict(w1=sw1[:, 2048 * hh:2048 * (hh + 1)],
                            w2=sw2[2048 * hh:2048 * (hh + 1), :],
                            b1=sb1[2048 * hh:2048 * (hh + 1)],
                            comb=("const", 1.0)))
    return dict(router_w=pr.din("L2_ffn_router_wT", [1024, 8]),
                router_b=None,
                router_kind="sigmoid_top2", act=AF.Relu, experts=experts,
                bias_items=[("comb", eb2), ("mean", sb2.unsqueeze(0), 1.0)])


def emit_layer0(pr, C, xnT, x_in, x_out, bg=None):
    pr.push()
    ln_done = emit_ln(pr, C, x_in, xnT, tag="ln0")
    pr.pop()
    s_dram = pr.dint("sp_s0", [T, D], BF16)
    z_dram = pr.dint("sp_z0", [T, 2048], BF16)
    xsp_dram = pr.dint("sp_xc0", [T, 2048], BF16)
    pr.push()
    ctx = mamba_ctx(pr)
    pr.push()
    if bg:
        bg.open_arena(p2=False)
    emit_mamba_ab(pr, C, xnT, ln_done, z_dram, xsp_dram, ctx, bg=bg)
    if bg:
        bg.drain_p1()
    pr.pop()
    if bg:
        pr.push()
        bg.emit_thresh()
        pr.pop()
    emit_mamba_c(pr, C, s_dram, z_dram, xsp_dram, ctx)
    pr.pop()
    cfg = l0_moe_cfg(pr)
    pr.push()
    if bg:
        bg.open_arena(p2=True)
    hs = emit_moe(pr, C, xnT, ln_done, cfg, x_in, x_out, s_dram, "m0", bg=bg)
    if bg:
        bg.drain()
    pr.pop()
    return hs


def emit_rms_inplace(pr, C, xnT, tag):
    kb = pr
    pr.push()
    sq = pr.sbuf(f"{tag}_sq", [128, T], BF16)
    onec = pr.sbuf(f"{tag}_onec", [128, 1], BF16)
    h0 = kb.emit("vector", lambda e: e.memset(onec[:, :], 1.0))
    nwc = pr.sbuf(f"{tag}_nwc", [128, 8], F32)
    hnw = kb.dma("sync", nwc[:, :], pr.din("L2_ffn_norm_weight", [1024])
                 .rearrange("(c p) -> p c", p=128), allow_slow_non_contiguous=True)
    rrow = pr.sbuf(f"{tag}_rrow", [1, T], F32)
    rbc = pr.sbuf(f"{tag}_rbc", [128, T], BF16)
    onesf = pr.sbuf(f"{tag}_onesf", [1, 128], F32)
    h1 = kb.emit("vector", lambda e: e.memset(onesf[:, :], 1.0))
    pv = pr.psum(f"{tag}_pv", [1, T], F32)
    pb = pr.psum(f"{tag}_pb", [128, 512], F32)
    sq_evs = []
    mm_evs = []
    for kc in range(8):
        hs = kb.emit("scalar", lambda e, kc=kc: e.activation(
            sq[:, :], xnT[:, T * kc:T * (kc + 1)], AF.Square), deps=mm_evs[-1:])
        hm = kb.emit("tensor", lambda e, kc=kc: [
            e.matmul(pv[0:1, 512 * i:512 * (i + 1)], onec[:, :],
                     sq[:, 512 * i:512 * (i + 1)], start=(kc == 0), stop=(kc == 7))
            for i in range(4)][-1], deps=[hs, h0])
        sq_evs.append(hs)
        mm_evs.append(hm)
    h2 = kb.emit("vector", lambda e: e.tensor_scalar(
        rrow[0:1, :], pv[0:1, :], 1.0 / 1024.0, 1e-8, ALU.mult, ALU.add),
        deps=[mm_evs[-1]])
    h3 = kb.emit("scalar", lambda e: e.sqrt(rrow[0:1, :], rrow[0:1, :]), deps=[h2])
    h4 = kb.emit("vector", lambda e: e.reciprocal(rrow[0:1, :], rrow[0:1, :]), deps=[h3])
    bc_evs = []
    for i in range(4):
        hm = kb.emit("tensor", lambda e, i=i: e.matmul(
            pb[:, :], onesf[:, :], rrow[0:1, 512 * i:512 * (i + 1)],
            start=True, stop=True), deps=[h4, h1] + bc_evs[-1:])
        hv = kb.emit("vector", lambda e, i=i: e.tensor_copy(
            rbc[:, 512 * i:512 * (i + 1)], pb[:, :]), deps=[hm])
        bc_evs.append(hv)
    done = []
    for kc in range(8):
        ha = kb.emit("vector", lambda e, kc=kc: e.tensor_tensor(
            xnT[:, T * kc:T * (kc + 1)], xnT[:, T * kc:T * (kc + 1)], rbc[:, :],
            ALU.mult), deps=bc_evs + [sq_evs[-1]])
        hb = kb.emit("vector", lambda e, kc=kc: e.tensor_scalar(
            xnT[:, T * kc:T * (kc + 1)], xnT[:, T * kc:T * (kc + 1)],
            nwc[:, kc:kc + 1], None, ALU.mult), deps=[ha, hnw])
        done.append(hb)
    pr.pop()
    return done


def emit_layer2(pr, C, xnT, x_in, x_out):
    pr.push()
    ln_done = emit_ln(pr, C, x_in, xnT, tag="ln2")
    pr.pop()
    s_dram = pr.dint("sp_s2", [T, D], BF16)
    W = {k: pr.din("L2_attn_" + k,
                   dict(Wq=[1024, 512], Wk=[1024, 512], Wv=[1024, 1024],
                        Wg=[1024, 1024], Wgk1=[1024, 16], Wgk2=[16, 512],
                        bgk2=[512], Wo=[1024, 1024], gw=[256])[k])
         for k in ("Wq", "Wk", "Wv", "Wg", "Wgk1", "Wgk2", "bgk2", "Wo", "gw")}
    W["rms_w"] = W.pop("gw")
    pr.push()
    emit_gla(pr, C, xnT, ln_done, W,
             dict(DK=512, DV=1024, HEADS=4, GATE_NORM=16.0), s_dram, "g2")
    pr.pop()
    rms_done = emit_rms_inplace(pr, C, xnT, "r2")
    cfg = l2_moe_cfg(pr)
    pr.push()
    hs = emit_moe(pr, C, xnT, rms_done, cfg, x_in, x_out, s_dram, "m2")
    pr.pop()
    return hs




def emit_attn3(pr, C, xnT, xnT_dep, W, s_dram, tag="a3"):
    kb = pr
    NH, HD = 16, 64
    cos_d = pr.din("c_rope_cos", [T, D])
    sin_d = pr.din("c_rope_sin", [T, D])
    qTr = pr.sbuf(f"{tag}_qTr", [128, 8 * T], BF16)
    kTr = pr.sbuf(f"{tag}_kTr", [128, 8 * T], BF16)
    vtok = pr.sbuf(f"{tag}_vtok", [128, NT * D], BF16)
    pr.push()
    wsb = sbuf_ring(pr, f"{tag}_w", 2, [128, 8 * 512], BF16)
    pp = psum_ring(pr, f"{tag}_pp", 4, [128, 512], F32)
    ptr = psum_ring(pr, f"{tag}_ptr", 2, [128, 128], BF16)
    qf = pr.sbuf(f"{tag}_qf", [128, D], F32)
    rot = pr.sbuf(f"{tag}_rot", [128, D], F32)
    qb = pr.sbuf(f"{tag}_qb", [128, D], BF16)
    cs = sbuf_ring(pr, f"{tag}_cs", 2, [128, D], F32)
    sn = sbuf_ring(pr, f"{tag}_sn", 2, [128, D], F32)
    pv_qf = []
    pv_rot = []
    pv_qb = []
    tr_prev = [None, None]
    for (wname, dstT, do_rope) in (("WqT", qTr, True), ("WkT", kTr, True),
                                   ("WvT", vtok, False)):
        wslots = []
        for ncol in range(2):
            wb, war, ws = wsb.next()
            lw = kb.dma("gpsimd", wb[:, :].rearrange("p (k c) -> p k c", c=512),
                        W[wname][:, 512 * ncol:512 * (ncol + 1)]
                        .rearrange("(k p) c -> p k c", p=128), deps=[war])
            wslots.append((wb, lw, ws))
        wlast = None
        for tt in range(NT):
            hevs = []
            for ncol in range(2):
                wb, lw, ws = wslots[ncol]
                pb, wp, sp = pp.next()
                hm = kb.emit("tensor", lambda e, pb=pb, wb=wb, tt=tt: [
                    e.matmul(pb[:, :], xnT[:, T * k + 128 * tt:T * k + 128 * (tt + 1)],
                             wb[:, 512 * k:512 * (k + 1)], start=(k == 0), stop=(k == 7))
                    for k in range(8)][-1], deps=[lw, xnT_dep] + wp)
                wlast = hm
                if not do_rope:
                    he = kb.emit("scalar", lambda e, pb=pb, tt=tt, ncol=ncol: e.copy(
                        vtok[:, D * tt + 512 * ncol:D * tt + 512 * (ncol + 1)], pb[:, :]),
                        deps=[hm])
                else:
                    he = kb.emit("vector", lambda e, pb=pb, ncol=ncol: e.tensor_copy(
                        qf[:, 512 * ncol:512 * (ncol + 1)], pb[:, :]),
                        deps=[hm] + (pv_qf if ncol == 0 else []))
                pp.used(sp, he)
                hevs.append(he)
            if not do_rope:
                continue
            hc, wc_, sc_ = cs.next()
            hl1 = kb.dma("sync", hc[:, :], cos_d[128 * tt:128 * (tt + 1), :], deps=wc_)
            hsn, wsn, ssn = sn.next()
            hl2 = kb.dma("sync", hsn[:, :], sin_d[128 * tt:128 * (tt + 1), :], deps=wsn)
            h_r1 = kb.emit("vector", lambda e: e.tensor_scalar_mul(
                rot[:, 0:D:2], qf[:, 1:D:2], -1.0), deps=hevs + pv_rot)
            h_r2 = kb.emit("vector", lambda e: e.tensor_copy(
                rot[:, 1:D:2], qf[:, 0:D:2]), deps=hevs + pv_rot)
            h_r3 = kb.emit("vector", lambda e, hc=hc: e.tensor_tensor(
                qf[:, :], qf[:, :], hc[:, :], ALU.mult), deps=[hl1, h_r1, h_r2])
            h_r4 = kb.emit("vector", lambda e, hsn=hsn: e.tensor_tensor(
                rot[:, :], rot[:, :], hsn[:, :], ALU.mult), deps=[hl2, h_r1, h_r2])
            h_r5 = kb.emit("vector", lambda e: e.tensor_add(qf[:, :], qf[:, :], rot[:, :]),
                           deps=[h_r3, h_r4])
            cs.used(sc_, h_r3)
            sn.used(ssn, h_r4)
            pv_rot = [h_r5]
            h_cb = kb.emit("vector", lambda e: e.tensor_copy(qb[:, :], qf[:, :]),
                           deps=[h_r5] + pv_qb)
            pv_qf = [h_cb]
            tevs = []
            for kc in range(8):
                pb2, wp2, sp2 = ptr.next()
                hT1 = kb.emit("tensor", lambda e, kc=kc, pb2=pb2: e.transpose(
                    pb2[:, :], qb[:, 128 * kc:128 * (kc + 1)], C.ident[:, :]),
                    deps=[h_cb, C.ready] + wp2)
                hT2 = kb.emit("scalar", lambda e, kc=kc, pb2=pb2, dstT=dstT, tt=tt: e.copy(
                    dstT[:, T * kc + 128 * tt:T * kc + 128 * (tt + 1)], pb2[:, :]),
                    deps=[hT1])
                ptr.used(sp2, hT2)
                tevs.append(hT2)
            pv_qb = [tevs[-1]]
        wsb.used(wslots[0][2], wlast)
        wsb.used(wslots[1][2], wlast)
    pr.pop()
    qk_ready = kb.barrier()

    pr.push()
    PTb = pr.sbuf(f"{tag}_PT", [128, T], BF16)
    o_all = pr.sbuf(f"{tag}_oall", [128, D], BF16)
    sumc = pr.sbuf(f"{tag}_sum", [128, 2], F32)
    onec_bf = pr.sbuf(f"{tag}_onecb", [128, 1], BF16)
    h_ob = kb.emit("vector", lambda e: e.memset(onec_bf[:, :], 1.0))
    mm3 = pr.sbuf(f"{tag}_m3", [128, 1], F32)
    h_m3 = kb.emit("vector", lambda e: e.memset(mm3[:, :], -3.0))
    sbf = pr.sbuf(f"{tag}_sbf", [128, D], BF16)
    oT = pr.sbuf(f"{tag}_oT", [128, 8 * 128], BF16)
    wo_sb = pr.sbuf(f"{tag}_wo", [128, 8 * D], BF16)
    lwo = [kb.dma("gpsimd", wo_sb[:, D * kc:D * (kc + 1)],
                  W["WoT"][128 * kc:128 * (kc + 1), :]) for kc in range(8)]
    pS = psum_ring(pr, f"{tag}_pS", 3, [128, 128], F32)
    pSum = pr.psum(f"{tag}_pSum", [128, 8], F32)
    pOh = pr.psum(f"{tag}_pOh", [128, 64], F32)
    pT0 = pr.psum(f"{tag}_pT0", [128, 128], BF16)
    pT1 = pr.psum(f"{tag}_pT1", [128, 128], BF16)
    pW = pr.psum(f"{tag}_pW", [128, 512], F32)
    pv = {"PT": [], "oall": [], "sum": [], "pSum": [], "pOh": [], "pW": [],
          "pT0": None, "pT1": None, "oT": [], "sbf": []}
    for qt in range(NT):
        for h in range(NH):
            kc_h = h // 2
            pof = 64 * (h % 2)
            exp_evs = []
            for kt in range(qt + 1):
                pb, wp, sp = pS.next()
                hm = kb.emit("tensor", lambda e, pb=pb, h=h, qt=qt, kt=kt, kc_h=kc_h, pof=pof: e.matmul(
                    pb[:, :], kTr[pof:pof + 64, T * kc_h + 128 * kt:T * kc_h + 128 * (kt + 1)],
                    qTr[pof:pof + 64, T * kc_h + 128 * qt:T * kc_h + 128 * (qt + 1)],
                    start=True, stop=True), deps=qk_ready + wp)
                if kt == qt:
                    hmask = kb.emit("vector", lambda e, pb=pb: e.tensor_tensor(
                        pb[:, :], pb[:, :], C.maskb[:, :], ALU.add), deps=[hm, C.ready])
                else:
                    hmask = hm
                hx = kb.emit("scalar", lambda e, pb=pb, kt=kt: e.activation(
                    PTb[:, 128 * kt:128 * (kt + 1)], pb[:, :], AF.Exp,
                    bias=mm3[:, :], scale=0.125), deps=[hmask, h_m3] + (pv["PT"] if kt == 0 else []))
                pS.used(sp, hx)
                exp_evs.append(hx)
            hsum = kb.emit("tensor", lambda e, qt=qt: [
                e.matmul(pSum[:, 0:1], PTb[:, 128 * kt:128 * (kt + 1)], onec_bf[:, :],
                         start=(kt == 0), stop=(kt == qt)) for kt in range(qt + 1)][-1],
                deps=exp_evs + [h_ob] + pv["pSum"])
            ho = kb.emit("tensor", lambda e, qt=qt, h=h: [
                e.matmul(pOh[:, :], PTb[:, 128 * kt:128 * (kt + 1)],
                         vtok[:, D * kt + HD * h:D * kt + HD * (h + 1)],
                         start=(kt == 0), stop=(kt == qt)) for kt in range(qt + 1)][-1],
                deps=exp_evs + qk_ready + pv["pOh"])
            hrc = kb.emit("vector", lambda e: e.reciprocal(sumc[:, 0:1], pSum[:, 0:1]),
                          deps=[hsum] + pv["sum"])
            hon = kb.emit("vector", lambda e, h=h: e.tensor_scalar(
                o_all[:, HD * h:HD * (h + 1)], pOh[:, :], sumc[:, 0:1], None, ALU.mult),
                deps=[hrc, ho] + (pv["oall"] if h == 0 else []))
            pv["pSum"] = [hrc]
            pv["pOh"] = [hon]
            pv["sum"] = [hon]
            pv["PT"] = [hsum, ho]
            last_hon = hon
        tprev = [pv["pT0"], pv["pT1"]]
        tevs = []
        for kc in range(8):
            slot = kc % 2
            ptx = pT0 if slot == 0 else pT1
            hT1 = kb.emit("tensor", lambda e, kc=kc, ptx=ptx: e.transpose(
                ptx[:, :], o_all[:, 128 * kc:128 * (kc + 1)], C.ident[:, :]),
                deps=[last_hon, C.ready, tprev[slot]])
            hT2 = kb.emit("scalar", lambda e, kc=kc, ptx=ptx: e.copy(
                oT[:, 128 * kc:128 * (kc + 1)], ptx[:, :]),
                deps=[hT1] + (pv["oT"] if kc < 2 else []))
            tprev[slot] = hT2
            tevs.append(hT2)
        pv["pT0"], pv["pT1"] = tprev
        pv["oall"] = [tevs[-1]]
        hW = kb.emit("tensor", lambda e: [
            e.matmul(pW[:, :], oT[:, 128 * kc:128 * (kc + 1)],
                     wo_sb[:, D * kc + 512 * nh:D * kc + 512 * (nh + 1)],
                     start=(kc == 0), stop=(kc == 7))
            for nh in range(1) for kc in range(8)][-1],
            deps=tevs + lwo + pv["pW"])
        hW2 = kb.emit("tensor", lambda e: [
            e.matmul(pW[:, :], oT[:, 128 * kc:128 * (kc + 1)],
                     wo_sb[:, D * kc + 512:D * kc + 1024],
                     start=(kc == 0), stop=(kc == 7)) for kc in range(8)][-1],
            deps=[]) if False else None
        hWe = kb.emit("scalar", lambda e: e.copy(sbf[:, 0:512], pW[:, :]),
                      deps=[hW] + pv["sbf"])
        hWb = kb.emit("tensor", lambda e: [
            e.matmul(pW[:, :], oT[:, 128 * kc:128 * (kc + 1)],
                     wo_sb[:, D * kc + 512:D * kc + 1024],
                     start=(kc == 0), stop=(kc == 7)) for kc in range(8)][-1],
            deps=tevs + lwo + [hWe])
        hWe2 = kb.emit("scalar", lambda e: e.copy(sbf[:, 512:1024], pW[:, :]),
                       deps=[hWb])
        hSd = kb.dma("sync", s_dram[128 * qt:128 * (qt + 1), :], sbf[:, :], deps=[hWe, hWe2])
        pv["oT"] = [hWb]
        pv["sbf"] = [hSd]
        pv["pW"] = [hWe2]
    pr.pop()
    return None



def emit_lstm(pr, C, xnT, xnT_dep, W, a2a_out, cc_in, cc_out, h_a2a,
              x_in_ap, x_out_ap, s_dram, tag="l3f"):
    kb = pr
    NR = T // NCORES
    xsT = pr.sbuf(f"{tag}_xsT", [128, 8 * T], BF16)
    pr.push()
    xr = sbuf_ring(pr, f"{tag}_xr", 2, [128, D], BF16)
    ptr = psum_ring(pr, f"{tag}_ptr", 2, [128, 128], BF16)
    for tt in range(NT):
        xb, war, ws = xr.next()
        hl = kb.dma("sync", xb[:, :], a2a_out[128 * tt:128 * (tt + 1), :],
                    deps=[h_a2a] + war)
        tevs = []
        for kc in range(8):
            pb2, wp2, sp2 = ptr.next()
            hT1 = kb.emit("tensor", lambda e, kc=kc, pb2=pb2, xb=xb: e.transpose(
                pb2[:, :], xb[:, 128 * kc:128 * (kc + 1)], C.ident[:, :]),
                deps=[hl, C.ready] + wp2)
            hT2 = kb.emit("scalar", lambda e, kc=kc, pb2=pb2, tt=tt: e.copy(
                xsT[:, T * kc + 128 * tt:T * kc + 128 * (tt + 1)], pb2[:, :]),
                deps=[hT1])
            ptr.used(sp2, hT2)
            tevs.append(hT2)
        xr.used(ws, tevs[-1])
    pr.pop()
    xs_ready = kb.barrier()

    pr.push()
    wnames = ("wfT", "ufT", "wiT", "uiT", "wcT", "ucT")
    wsb = {}
    lws = []
    for wn in wnames:
        wsb[wn] = pr.sbuf(f"{tag}_{wn}", [128, 8 * 1024], BF16)
        lws += [kb.dma("gpsimd", wsb[wn][:, 1024 * kc:1024 * (kc + 1)],
                       W[wn][128 * kc:128 * (kc + 1), :]) for kc in range(8)]
    bcols = pr.sbuf(f"{tag}_bc", [128, 8 * 3], F32)
    lb = [kb.dma("sync", bcols[:, 8 * i:8 * (i + 1)],
                 W[bn].rearrange("(c p) -> p c", p=128),
                 allow_slow_non_contiguous=True)
          for i, bn in enumerate(("bf", "bi", "bc_"))]
    icc = pr.sbuf(f"{tag}_icc", [128, 8], F32)
    lic = kb.dma("sync", icc[:, :], W["in_c"].rearrange("(c p) -> p c", p=128),
                 allow_slow_non_contiguous=True)
    cT = pr.sbuf(f"{tag}_cT", [128, 8 * NR], F32)
    cTb = pr.sbuf(f"{tag}_cTb", [128, 8 * NR], BF16)
    fb = pr.sbuf(f"{tag}_fb", [128, 8 * NR], F32)
    ib = pr.sbuf(f"{tag}_ib", [128, 8 * NR], F32)
    gb = pr.sbuf(f"{tag}_gb", [128, 8 * NR], F32)
    pg = psum_ring(pr, f"{tag}_pg", 3, [128, NR], F32)
    init_evs = []
    for kc in range(8):
        h0 = kb.emit("vector", lambda e, kc=kc: e.memset(cT[:, NR * kc:NR * (kc + 1)], 0.0))
        h1 = kb.emit("vector", lambda e, kc=kc: e.tensor_scalar_add(
            cT[:, NR * kc:NR * (kc + 1)], cT[:, NR * kc:NR * (kc + 1)],
            icc[:, kc:kc + 1]), deps=[h0, lic])
        init_evs.append(h1)
    hcb = kb.emit("vector", lambda e: e.tensor_copy(cTb[:, :], cT[:, :]), deps=init_evs)
    pv = {"cT": init_evs, "cTb": [hcb], "fb": [], "ib": [], "gb": []}
    for t in range(NCORES):
        gate_evs = {"fb": [], "ib": [], "gb": []}
        for (gname, wname, uname, brow, actf) in (
                ("fb", "wfT", "ufT", 0, AF.Sigmoid), ("ib", "wiT", "uiT", 1, AF.Sigmoid),
                ("gb", "wcT", "ucT", 2, AF.Tanh)):
            dst = {"fb": fb, "ib": ib, "gb": gb}[gname]
            for oc in range(8):
                pb, wp, sp = pg.next()

                def gmm(e, pb=pb, wname=wname, uname=uname, oc=oc, t=t):
                    for kc in range(8):
                        e.matmul(pb[:, :],
                                 wsb[wname][:, 1024 * kc + 128 * oc:1024 * kc + 128 * (oc + 1)],
                                 xsT[:, T * kc + NR * t:T * kc + NR * (t + 1)],
                                 start=(kc == 0), stop=False)
                    for kc in range(8):
                        i1 = e.matmul(pb[:, :],
                                      wsb[uname][:, 1024 * kc + 128 * oc:1024 * kc + 128 * (oc + 1)],
                                      cTb[:, NR * kc:NR * (kc + 1)],
                                      start=False, stop=(kc == 7))
                    return i1
                hm = kb.emit("tensor", gmm, deps=lws + xs_ready + wp + pv["cTb"])
                he = kb.emit("scalar", lambda e, pb=pb, dst=dst, oc=oc, brow=brow, actf=actf: e.activation(
                    dst[:, NR * oc:NR * (oc + 1)], pb[:, :], actf,
                    bias=bcols[:, 8 * brow + oc:8 * brow + oc + 1]),
                    deps=[hm] + lb + (pv[gname] if oc == 0 else []))
                pg.used(sp, he)
                gate_evs[gname].append(he)
        hc1 = kb.emit("vector", lambda e: e.tensor_tensor(cT[:, :], fb[:, :], cT[:, :], ALU.mult),
                      deps=gate_evs["fb"] + pv["cT"])
        hc2 = kb.emit("vector", lambda e: e.tensor_tensor(ib[:, :], ib[:, :], gb[:, :], ALU.mult),
                      deps=gate_evs["ib"] + gate_evs["gb"])
        hc3 = kb.emit("vector", lambda e: e.tensor_add(cT[:, :], cT[:, :], ib[:, :]),
                      deps=[hc1, hc2])
        hcb = kb.emit("vector", lambda e: e.tensor_copy(cTb[:, :], cT[:, :]),
                      deps=[hc3] + [gate_evs["fb"][-1]])
        pv["cT"] = [hc3]
        pv["cTb"] = [hcb]
        pv["fb"] = [hc1]
        pv["ib"] = [hc3]
        pv["gb"] = [hc2]
    cfin_evs = []
    for kc in range(8):
        hd = kb.dma("sync", cc_in[128 * kc:128 * (kc + 1), :],
                    cT[:, NR * kc:NR * (kc + 1)], deps=pv["cT"])
        cfin_evs.append(hd)
    h_ag = kb.collective("AllGather", [cc_in], [cc_out], deps=cfin_evs)
    pr.pop()

    pr.push()
    woT = pr.sbuf(f"{tag}_woT", [128, 8 * 1024], BF16)
    uoT = pr.sbuf(f"{tag}_uoT", [128, 8 * 1024], BF16)
    lwo = [kb.dma("gpsimd", woT[:, 1024 * kc:1024 * (kc + 1)],
                  W["woT"][128 * kc:128 * (kc + 1), :]) for kc in range(8)]
    luo = [kb.dma("gpsimd", uoT[:, 1024 * kc:1024 * (kc + 1)],
                  W["uoT"][128 * kc:128 * (kc + 1), :]) for kc in range(8)]
    borow = pr.sbuf(f"{tag}_borow", [1, D], BF16)
    lbo = kb.dma("gpsimd", borow[:, :], W["bo"].unsqueeze(0))
    bo_bc = pr.sbuf(f"{tag}_bobc", [128, D], F32)
    pb0 = pr.psum(f"{tag}_pb0", [128, 512], F32)
    bc_evs = []
    for i in range(2):
        hm = kb.emit("tensor", lambda e, i=i: e.matmul(
            pb0[:, :], C.ones_col[:, :], borow[0:1, 512 * i:512 * (i + 1)],
            start=True, stop=True), deps=[lbo, C.ready] + bc_evs[-1:])
        hv = kb.emit("vector", lambda e, i=i: e.tensor_copy(
            bo_bc[:, 512 * i:512 * (i + 1)], pb0[:, :]), deps=[hm])
        bc_evs.append(hv)
    ccr = sbuf_ring(pr, f"{tag}_ccr", 2, [128, 8 * 128], BF16)
    po = psum_ring(pr, f"{tag}_po", 2, [128, 512], F32)
    gt = sbuf_ring(pr, f"{tag}_gt", 2, [128, D], F32)
    xr2 = sbuf_ring(pr, f"{tag}_xr2", 2, [128, D], F32)
    sr2 = sbuf_ring(pr, f"{tag}_sr2", 2, [128, D], BF16)
    out_h = []
    for tt in range(NT):
        cb, wcc, scc = ccr.next()
        lcc = [kb.dma("gpsimd", cb[:, 128 * kc:128 * (kc + 1)],
                      cc_out[1024 * (tt // 2) + 128 * kc:1024 * (tt // 2) + 128 * (kc + 1),
                             128 * (tt % 2):128 * (tt % 2 + 1)],
                      deps=[h_ag] + wcc) for kc in range(8)]
        gbuf, wg, sg = gt.next()
        evs = []
        for nh in range(2):
            pb, wp, sp = po.next()
            def fmm(e, pb=pb, cb=cb, tt=tt, nh=nh):
                for kc in range(8):
                    e.matmul(pb[:, :], xnT[:, T * kc + 128 * tt:T * kc + 128 * (tt + 1)],
                             woT[:, 1024 * kc + 512 * nh:1024 * kc + 512 * (nh + 1)],
                             start=(kc == 0), stop=False)
                for kc in range(8):
                    i1 = e.matmul(pb[:, :], cb[:, 128 * kc:128 * (kc + 1)],
                                  uoT[:, 1024 * kc + 512 * nh:1024 * kc + 512 * (nh + 1)],
                                  start=False, stop=(kc == 7))
                return i1
            hm = kb.emit("tensor", fmm, deps=lwo + luo + lcc + [xnT_dep] + wp)
            hv = kb.emit("vector", lambda e, pb=pb, gbuf=gbuf, nh=nh: e.tensor_tensor(
                gbuf[:, 512 * nh:512 * (nh + 1)], pb[:, :], bo_bc[:, 512 * nh:512 * (nh + 1)],
                ALU.add), deps=[hm] + bc_evs + (wg if nh == 0 else []))
            po.used(sp, hv)
            evs.append(hv)
        ht = kb.emit("scalar", lambda e, gbuf=gbuf: e.activation(gbuf[:, :], gbuf[:, :], AF.Tanh),
                     deps=evs)
        hsl = kb.emit("scalar", lambda e, gbuf=gbuf: e.activation(gbuf[:, :], gbuf[:, :], AF.Silu),
                      deps=[ht])
        ccr.used(scc, hm)
        xb, wx, sx = xr2.next()
        hx = kb.dma("sync", xb[:, :], x_in_ap[128 * tt:128 * (tt + 1), :], deps=[wx])
        sb_, wsr, ssx = sr2.next()
        hs = kb.dma("gpsimd", sb_[:, :], s_dram[128 * tt:128 * (tt + 1), :], deps=[wsr])
        hm1 = kb.emit("vector", lambda e, gbuf=gbuf, sb_=sb_: e.tensor_tensor(
            gbuf[:, :], gbuf[:, :], sb_[:, :], ALU.mult), deps=[hsl, hs])
        hm2 = kb.emit("vector", lambda e, gbuf=gbuf, xb=xb: e.tensor_tensor(
            gbuf[:, :], gbuf[:, :], xb[:, :], ALU.add), deps=[hm1, hx])
        hw = kb.dma("sync", x_out_ap[128 * tt:128 * (tt + 1), :], gbuf[:, :], deps=[hm2])
        gt.used(sg, hw)
        xr2.used(sx, hm2)
        sr2.used(ssx, hm1)
        out_h.append(hw)
    pr.pop()
    return out_h


def emit_layer3(pr, C, xnT, x_in, x_out):
    a2a_in = pr.dint("cc_a2a_in", [T, D], BF16)
    a2a_out = pr.dint("cc_a2a_out", [T, D], BF16)
    cc_in = pr.dint("cc_ag_in", [D, T // NCORES], F32)
    cc_out = pr.dint("cc_ag_out", [D * NCORES, T // NCORES], F32)
    pr.push()
    ln_done = emit_ln(pr, C, x_in, xnT, xn_dram=a2a_in, tag="ln3")
    pr.pop()
    h_a2a = pr.collective("AllToAll", [a2a_in], [a2a_out], deps=ln_done)
    s_dram = pr.dint("sp_s3", [T, D], BF16)
    W = {}
    for nm in ("Wq", "Wk", "Wv", "Wo"):
        W[nm + "T"] = pr.din(f"L3_attn_{nm}T", [1024, 1024])
    pr.push()
    emit_attn3(pr, C, xnT, ln_done, W, s_dram)
    pr.pop()
    WF = {}
    for nm in ("wfT", "ufT", "wiT", "uiT", "wcT", "ucT", "woT", "uoT"):
        WF[nm] = pr.din(f"L3_ffn_{nm}", [1024, 1024])
    for nm in ("bf", "bi", "bo", "in_c"):
        WF[nm] = pr.din(f"L3_ffn_{nm}", [1024])
    WF["bc_"] = pr.din("L3_ffn_bc", [1024])
    hs = emit_lstm(pr, C, xnT, ln_done, WF, a2a_out, cc_in, cc_out, h_a2a,
                   x_in, x_out, s_dram)
    return hs


def build_program(layers, dbg_outputs=()):
    nc = bass.Bass("TRN2", target_bir_lowering=False)
    es = ExitStack()
    with es:
        kb = KB(nc)
        pr = Prog(nc, es, kb)
        x_in = pr.din("x_in", [T, D], F32)
        x_out = pr.dout("x_out", [T, D], F32)
        C = Consts(pr)
        xnT = pr.sbuf("xnT", [128, 8 * T], BF16)

        if layers == ["gla1"]:
            pr.push()
            ln_done = emit_ln(pr, C, x_in, xnT, tag="ln1")
            pr.pop()
            s_dram = pr.dout("dbg_s", [T, D], BF16)
            W = {k: pr.din("L1_attn_" + k,
                           dict(Wq=[1024, 1024], Wk=[1024, 1024], Wv=[1024, 2048],
                                Wg=[1024, 2048], Wgk1=[1024, 16], Wgk2=[16, 1024],
                                bgk2=[1024], Wo=[2048, 1024], rms_w=[512])[k])
                 for k in ("Wq", "Wk", "Wv", "Wg", "Wgk1", "Wgk2", "bgk2", "Wo", "rms_w")}
            pr.push()
            emit_gla(pr, C, xnT, ln_done, W,
                     dict(DK=1024, DV=2048, HEADS=4, GATE_NORM=16.0), s_dram, "g1")
            pr.pop()
            xr0 = sbuf_ring(pr, "cp0", 2, [128, D], F32)
            for t in range(NT):
                xb, w, s = xr0.next()
                hl = pr.dma("sync", xb[:, :], x_in[128 * t:128 * (t + 1), :], deps=[w])
                hs = pr.dma("gpsimd", x_out[128 * t:128 * (t + 1), :], xb[:, :], deps=[hl])
                xr0.used(s, hs)
            layers = None
        if layers == [0, 1, 2, 3]:
            xa = pr.dint("x_mid_a", [T, D], F32)
            xb2 = pr.dint("x_mid_b", [T, D], F32)
            emit_layer0(pr, C, xnT, x_in, xa)
            emit_layer1(pr, C, xnT, xa, xb2)
            emit_layer2(pr, C, xnT, xb2, xa)
            emit_layer3(pr, C, xnT, xa, x_out)
            layers = None
        if layers == [0]:
            emit_layer0(pr, C, xnT, x_in, x_out)
            layers = None
        if layers == [1]:
            emit_layer1(pr, C, xnT, x_in, x_out)
            layers = None
        if layers == [2]:
            emit_layer2(pr, C, xnT, x_in, x_out)
            layers = None
        if layers == [3]:
            emit_layer3(pr, C, xnT, x_in, x_out)
            layers = None
        if layers == ["mamba0"]:
            pr.push()
            ln_done = emit_ln(pr, C, x_in, xnT)
            pr.pop()
            s_dram = pr.dout("dbg_s", [T, D], BF16)
            z_dram = pr.dint("sp_z", [T, 2048], BF16)
            xsp_dram = pr.dint("sp_x", [T, 2048], BF16)
            pr.push()
            emit_mamba(pr, C, xnT, ln_done, s_dram, z_dram, xsp_dram)
            pr.pop()
            xr0 = sbuf_ring(pr, "cp0", 2, [128, D], F32)
            for t in range(NT):
                xb, w, s = xr0.next()
                hl = pr.dma("sync", xb[:, :], x_in[128 * t:128 * (t + 1), :], deps=[w])
                hs = pr.dma("gpsimd", x_out[128 * t:128 * (t + 1), :], xb[:, :], deps=[hl])
                xr0.used(s, hs)
            layers = None
        if layers == ["moe0"]:
            pr.push()
            ln_done = emit_ln(pr, C, x_in, xnT)
            pr.pop()
            cfg = l0_moe_cfg(pr)
            dbg = pr.dout("dbg_g", [T, D], F32)
            pr.push()
            emit_moe(pr, C, xnT, ln_done, cfg, x_in, x_out, None, "m0", dbg_g=dbg)
            pr.pop()
            xr0 = sbuf_ring(pr, "cp0", 2, [128, D], F32)
            for t in range(NT):
                xb, w, s = xr0.next()
                hl = pr.dma("sync", xb[:, :], x_in[128 * t:128 * (t + 1), :], deps=[w])
                hs = pr.dma("gpsimd", x_out[128 * t:128 * (t + 1), :], xb[:, :], deps=[hl])
                xr0.used(s, hs)
            layers = None
        if layers == []:
            ln_done = emit_ln(pr, C, x_in, xnT)
            xr = sbuf_ring(pr, "cp", 2, [128, D], F32)
            for t in range(NT):
                xb, w, s = xr.next()
                hl = pr.dma("sync", xb[:, :], x_in[128 * t:128 * (t + 1), :], deps=[w])
                hs = pr.dma("gpsimd", x_out[128 * t:128 * (t + 1), :], xb[:, :], deps=[hl])
                xr.used(s, hs)
            if "xnT" in dbg_outputs:
                d = pr.dout("dbg_xnT", [128, 8 * T], BF16)
                pr.dma("sync", d, xnT[:, :], deps=ln_done)

        kb.emit("sync", lambda e: e.nop(), deps=kb.barrier())

        block = es.enter_context(nc.Block())
        kb.finalize(block, es)
    return nc


def _rope_tables():
    hd = 64
    inv = 1.0 / (10000.0 ** (np.arange(0, hd, 2, dtype=np.float32) / hd))
    pos = np.arange(T, dtype=np.float32)
    emb = np.repeat(pos[:, None] * inv[None, :], 2, axis=-1)
    cos = np.tile(np.cos(emb), (1, 16)).astype(np.float32)
    sin = np.tile(np.sin(emb), (1, 16)).astype(np.float32)
    return cos, sin


def _layer_inputs(L, inputs):
    f32 = lambda a: np.ascontiguousarray(a, dtype=np.float32)
    m = dict(Consts.host_inputs())
    if L == 0:
        for k in inputs:
            if k.startswith("L0_") and k != "L0_attn_hidden_states":
                m[k] = f32(inputs[k])
    elif L == 1:
        for k in inputs:
            if k.startswith("L1_attn_"):
                m[k] = f32(inputs[k])
        m["L1_ffn_W_fcT"] = f32(inputs["L1_ffn_W_fc"].T)
        m["L1_ffn_W_projT"] = f32(inputs["L1_ffn_W_proj"].T)
        m["L1_ffn_ln_w"] = f32(inputs["L1_ffn_ln_w"])
        m["L1_ffn_ln_b"] = f32(inputs["L1_ffn_ln_b"])
    elif L == 2:
        for k in inputs:
            if k.startswith("L2_attn_") or k.startswith("L2_ffn_"):
                m[k] = f32(inputs[k])
        m["L2_ffn_router_wT"] = f32(inputs["L2_ffn_router_w"].T)
        del m["L2_ffn_router_w"]
    else:
        cos, sin = _rope_tables()
        m["c_rope_cos"] = cos
        m["c_rope_sin"] = sin
        for nm in ("Wq", "Wk", "Wv", "Wo"):
            m[f"L3_attn_{nm}T"] = f32(inputs[f"L3_attn_{nm}"].T)
        for nm in ("wf", "uf", "wi", "ui", "wc", "uc", "wo", "uo"):
            m[f"L3_ffn_{nm}T"] = f32(inputs[f"L3_ffn_{nm}"].T)
        for nm in ("bf", "bi", "bc", "bo", "in_c"):
            m[f"L3_ffn_{nm}"] = f32(inputs[f"L3_ffn_{nm}"])
    return m


_prog_cache = {}


def _get_program(key):
    if key not in _prog_cache:
        _prog_cache[key] = build_program(list(key))
    return _prog_cache[key]


FUSED = True


def kernel(**inputs):
    x = np.ascontiguousarray(inputs["L0_attn_hidden_states"], dtype=np.float32)
    if FUSED:
        nc = _get_program((0, 1, 2, 3))
        wmap = {}
        for L in range(4):
            wmap.update(_layer_inputs(L, inputs))
        in_maps = []
        for b in range(NCORES):
            m = dict(wmap)
            m["x_in"] = x[b]
            in_maps.append(m)
        res = run_bass_kernel_spmd(nc, in_maps, core_ids=list(range(NCORES)))
        return np.stack([res.results[b]["x_out"] for b in range(NCORES)], axis=0)
    for L in range(4):
        nc = _get_program((L,))
        wmap = _layer_inputs(L, inputs)
        in_maps = []
        for b in range(NCORES):
            m = dict(wmap)
            m["x_in"] = x[b]
            in_maps.append(m)
        res = run_bass_kernel_spmd(nc, in_maps, core_ids=list(range(NCORES)))
        x = np.stack([res.results[b]["x_out"] for b in range(NCORES)], axis=0)
    return x


# version 16
# speedup vs baseline: 1.0549x; 1.0549x over previous
import numpy as np
import ml_dtypes

import concourse.bass as bass
import concourse.mybir as mybir
from concourse.bass_utils import run_bass_kernel_spmd
from concourse.aot_env import allow_ici_all_to_all
from contextlib import ExitStack

F32 = mybir.dt.float32
BF16 = mybir.dt.bfloat16
AF = mybir.ActivationFunctionType
ALU = mybir.AluOpType
AX = mybir.AxisListType.X
bf16 = ml_dtypes.bfloat16

NCORES = 8
T = 2048
D = 1024
NT = T // 128

ENGINES = ("tensor", "vector", "scalar", "gpsimd", "sync")
DMA_QUEUES = ("sync", "scalar", "gpsimd")


class H:
    __slots__ = ("eng", "dma", "val")

    def __init__(self, eng, dma, val):
        self.eng, self.dma, self.val = eng, dma, val


class KB:

    NLANES = 8

    def __init__(self, nc):
        self.nc = nc
        self.steps = []
        self.ccount = {e: 0 for e in ENGINES}
        self.dcount = {q: 0 for q in DMA_QUEUES}
        self.cc_count = 0

    @staticmethod
    def _flat(deps):
        out = []
        for d in deps:
            if d is None:
                continue
            if isinstance(d, (list, tuple)):
                out.extend(x for x in d if x is not None)
            else:
                out.append(d)
        return out

    def emit(self, eng, fn, deps=()):
        deps = self._flat(deps)
        self.ccount[eng] += 1
        self.steps.append((eng, fn, deps, False, False))
        return H(eng, False, self.ccount[eng])

    def dma(self, q, out, in_, deps=(), **kw):
        deps = self._flat(deps)
        idx = self.dcount[q]
        self.dcount[q] += 1
        lane = idx % self.NLANES
        if idx >= self.NLANES:
            deps = deps + [H((q, lane), True, idx // self.NLANES)]
        self.steps.append(
            ((q, lane), lambda e, o=out, i=in_, k=kw: e.dma_start(out=o, in_=i, **k),
             deps, True, False))
        return H((q, lane), True, idx // self.NLANES + 1)

    def collective(self, kind, ins, outs, deps=()):
        deps = self._flat(deps)
        cid = self.cc_count
        self.cc_count += 1

        def fn(e):
            with allow_ici_all_to_all():
                return e.collective_compute(
                    kind, ALU.bypass, replica_groups=[list(range(NCORES))],
                    ins=[a.opt() for a in ins], outs=[a.opt() for a in outs])
        self.steps.append((("cc", cid), fn, deps, False, True))
        return H(("cc", cid), False, 1)

    def barrier(self):
        hs = [H(e, False, self.ccount[e]) for e in ENGINES if self.ccount[e]]
        for q in DMA_QUEUES:
            n = self.dcount[q]
            for lane in range(min(n, self.NLANES)):
                cnt = n // self.NLANES + (1 if lane < n % self.NLANES else 0)
                hs.append(H((q, lane), True, cnt))
        for cid in range(self.cc_count):
            hs.append(H(("cc", cid), False, 1))
        return hs

    def finalize(self, block, es):
        nc = self.nc
        sem_c = {e: es.enter_context(nc.semaphore(f"sc_{e}")) for e in ENGINES}
        sem_d = {}
        for q in DMA_QUEUES:
            for lane in range(min(self.dcount[q], self.NLANES)):
                sem_d[(q, lane)] = es.enter_context(nc.semaphore(f"sd_{q}{lane}"))
        sem_cc = {cid: es.enter_context(nc.semaphore(f"scc{cid}"))
                  for cid in range(self.cc_count)}

        def sem_tgt(h):
            if isinstance(h.eng, tuple) and h.eng[0] == "cc":
                return sem_cc[h.eng[1]], h.val, h.eng
            if h.dma:
                return sem_d[h.eng], 16 * h.val, h.eng
            return sem_c[h.eng], h.val, h.eng

        per_eng = {e: [] for e in ENGINES}
        for (eng, fn, deps, is_dma, is_cc) in self.steps:
            if is_dma:
                q, lane = eng
                per_eng[q].append((fn, deps, is_dma, False, lane))
            elif is_cc:
                per_eng["gpsimd"].append((fn, deps, False, eng[1], None))
            else:
                per_eng[eng].append((fn, deps, False, False, None))

        def make_body(name):
            def body(engobj):
                waited = {}
                for fn, deps, is_dma, cc_id, lane in per_eng[name]:
                    for d in deps:
                        if (name == "tensor" and not isinstance(d.eng, tuple)
                                and d.eng == "tensor" and not d.dma):
                            continue
                        sem, tgt, key = sem_tgt(d)
                        if waited.get(key, -1) < tgt:
                            engobj.wait_ge(sem, tgt)
                            waited[key] = tgt
                    inst = fn(engobj)
                    if cc_id is not False and cc_id is not None:
                        inst.then_inc(sem_cc[cc_id], 1)
                    elif is_dma:
                        inst.then_inc(sem_d[(name, lane)], 16)
                    else:
                        inst.then_inc(sem_c[name], 1)
            return body

        for name in ENGINES:
            getattr(block, name)(make_body(name))



class Prog:

    def __init__(self, nc, es, kb):
        self.nc, self.es, self.kb = nc, es, kb
        self.uid = 0
        self.scopes = [es]
        self.phase_dep = []

    def _n(self, base):
        self.uid += 1
        return f"{base}_{self.uid}"

    def push(self):
        st = ExitStack()
        self.scopes.append(st)

    def pop(self):
        self.scopes.pop().close()
        self.phase_dep = self.kb.barrier()

    def sbuf(self, name, shape, dt):
        return self.scopes[-1].enter_context(self.nc.sbuf_tensor(self._n(name), shape, dt))

    def psum(self, name, shape, dt=F32):
        return self.scopes[-1].enter_context(self.nc.psum_tensor(self._n(name), shape, dt))

    def emit(self, eng, fn, deps=()):
        return self.kb.emit(eng, fn, list(deps) + self.phase_dep)

    def dma(self, q, out, in_, deps=(), **kw):
        return self.kb.dma(q, out, in_, list(deps) + self.phase_dep, **kw)

    def collective(self, kind, ins, outs, deps=()):
        return self.kb.collective(kind, ins, outs, list(deps) + self.phase_dep)

    def barrier(self):
        return self.kb.barrier()

    def din(self, name, shape, dt=F32):
        return self.nc.dram_tensor(name, list(shape), dt, kind="ExternalInput").ap()

    def dout(self, name, shape, dt=F32):
        return self.nc.dram_tensor(name, list(shape), dt, kind="ExternalOutput").ap()

    def dint(self, name, shape, dt=F32):
        return self.nc.dram_tensor(name, list(shape), dt).ap()


class Ring:

    def __init__(self, bufs):
        self.bufs = bufs
        self.n = len(bufs)
        self.i = 0
        self.last_user = [[] for _ in range(self.n)]

    def next(self):
        s = self.i % self.n
        self.i += 1
        u = self.last_user[s]
        self.last_user[s] = []
        return self.bufs[s], u, s

    def used(self, slot, h):
        if isinstance(h, (list, tuple)):
            self.last_user[slot].extend(x for x in h if x is not None)
        elif h is not None:
            self.last_user[slot].append(h)


def sbuf_ring(pr, name, n, shape, dt):
    return Ring([pr.sbuf(f"{name}{i}", shape, dt) for i in range(n)])


def psum_ring(pr, name, n, shape, dt=F32):
    return Ring([pr.psum(f"{name}{i}", shape, dt) for i in range(n)])



class Consts:

    def __init__(self, pr):
        kb = pr
        self.ident = pr.sbuf("c_ident", [128, 128], BF16)
        self.identf = pr.sbuf("c_identf", [128, 128], F32)
        self.ones_col = pr.sbuf("c_ones1", [1, 128], BF16)
        self.u64 = pr.sbuf("c_u64", [64, 64], F32)
        self.u128 = pr.sbuf("c_u128", [128, 128], F32)
        self.maskb = pr.sbuf("c_maskb", [128, 128], F32)
        d_ident = pr.din("c_ident", [128, 128], BF16)
        d_identf = pr.din("c_identf", [128, 128], F32)
        d_u64 = pr.din("c_u64", [64, 64], F32)
        d_u128 = pr.din("c_u128", [128, 128], F32)
        d_maskb = pr.din("c_maskb", [128, 128], F32)
        self.blockones_ap = pr.din("c_blockones", [32, 4096], F32)
        h0 = kb.dma("sync", self.ident[:, :], d_ident)
        h0b = kb.dma("sync", self.identf[:, :], d_identf)
        h1 = kb.dma("sync", self.u64[:, :], d_u64)
        h1b = kb.dma("sync", self.u128[:, :], d_u128)
        h2 = kb.dma("sync", self.maskb[:, :], d_maskb)
        self.hones = kb.emit("vector", lambda e: e.memset(self.ones_col[:, :], 1.0))
        self.ready = [h0, h0b, h1, h1b, h2, self.hones]

    @staticmethod
    def host_inputs():
        bo = np.zeros((32, 4096), dtype=np.float32)
        for h in range(32):
            bo[h, 128 * h:128 * (h + 1)] = 1.0
        return {
            "c_ident": np.eye(128, dtype=bf16),
            "c_identf": np.eye(128, dtype=np.float32),
            "c_u64": np.triu(np.ones((64, 64), dtype=np.float32)),
            "c_u128": np.triu(np.ones((128, 128), dtype=np.float32)),
            "c_maskb": np.where(np.arange(128)[:, None] <= np.arange(128)[None, :],
                                np.float32(0.0), np.float32(-1e9)),
            "c_blockones": bo,
        }



def emit_ln(pr, C, x_in_ap, xnT, xn_dram=None, tag="ln"):
    kb = pr
    xr = sbuf_ring(pr, f"{tag}_x", 3, [128, D], F32)
    nr = sbuf_ring(pr, f"{tag}_n", 2, [128, D], BF16)
    sq = pr.sbuf(f"{tag}_sq", [128, D], F32)
    st = sbuf_ring(pr, f"{tag}_st", 2, [128, 8], F32)
    pst = psum_ring(pr, f"{tag}_pst", 2, [128, 128], BF16)
    last = []
    prev_sqred = [None]
    for t in range(NT):
        xb, w, xs = xr.next()
        hl = kb.dma("sync", xb[:, :], x_in_ap[128 * t:128 * (t + 1), :], deps=[w])
        sb, ws, ss = st.next()
        h1 = kb.emit("vector", lambda e, xb=xb, sb=sb: e.reduce_sum(sb[:, 0:1], xb[:, :], AX),
                     deps=[hl, ws])
        h2a = kb.emit("scalar", lambda e, xb=xb: e.activation(
            sq[:, :], xb[:, :], AF.Square), deps=[hl, prev_sqred[0]])
        h2 = kb.emit("vector", lambda e, sb=sb: e.reduce_sum(sb[:, 1:2], sq[:, :], AX),
                     deps=[h2a, ws])
        prev_sqred[0] = h2

        h3a = kb.emit("vector", lambda e, sb=sb: e.tensor_scalar_mul(sb[:, 2:3], sb[:, 0:1], 1.0 / D), deps=[h1])
        h3b = kb.emit("vector", lambda e, sb=sb: e.tensor_tensor(sb[:, 3:4], sb[:, 2:3], sb[:, 2:3], ALU.mult), deps=[h3a])
        h3c = kb.emit("vector", lambda e, sb=sb: e.tensor_scalar(
            sb[:, 4:5], sb[:, 1:2], 1.0 / D, sb[:, 3:4], ALU.mult, ALU.subtract), deps=[h2, h3b])
        h3 = kb.emit("vector", lambda e, sb=sb: e.tensor_scalar_add(sb[:, 5:6], sb[:, 4:5], 1e-5), deps=[h3c])
        h4 = kb.emit("scalar", lambda e, sb=sb: e.sqrt(sb[:, 6:7], sb[:, 5:6]), deps=[h3])
        h5 = kb.emit("vector", lambda e, sb=sb: e.reciprocal(sb[:, 7:8], sb[:, 6:7]), deps=[h4])
        nb, wn, ns = nr.next()
        h6 = kb.emit("vector", lambda e, xb=xb, nb=nb, sb=sb: e.tensor_scalar(
            nb[:, :], xb[:, :], sb[:, 2:3], sb[:, 7:8], ALU.subtract, ALU.mult),
            deps=[h3a, h5, wn])
        xr.used(xs, [h2, h6])
        st.used(ss, h6)
        if xn_dram is not None:
            hsp = kb.dma("scalar", xn_dram[128 * t:128 * (t + 1), :], nb[:, :], deps=[h6])
        else:
            hsp = None
        evs = []
        for k in range(D // 128):
            pb, wp, ps_ = pst.next()
            ht = kb.emit("tensor", lambda e, nb=nb, pb=pb, k=k: e.transpose(
                pb[:, :], nb[:, 128 * k:128 * (k + 1)], C.ident[:, :]),
                deps=[h6, wp] + (C.ready if t == 0 and k == 0 else []))
            he = kb.emit("scalar", lambda e, pb=pb, k=k, t=t: e.copy(
                xnT[:, T * k + 128 * t: T * k + 128 * (t + 1)], pb[:, :]), deps=[ht])
            pst.used(ps_, he)
            evs.append(he)
        nr.used(ns, evs[-1])
        last.append(evs[-1])
        if hsp is not None:
            last.append(hsp)
    return last



def emit_router(pr, C, xnT, xnT_dep, rw_dram, rb_dram, kind, tag, comb_sb, combT):
    kb = pr
    E = 8
    rw = pr.sbuf(f"{tag}_rw", [128, 8 * E], BF16)
    hrw = kb.dma("gpsimd", rw[:, :].rearrange("p (k e) -> p k e", e=E),
                 rw_dram.rearrange("(k p) e -> p k e", p=128))
    onesf = pr.sbuf(f"{tag}_onesf", [1, 128], F32)
    h_onesf = kb.emit("vector", lambda e: e.memset(onesf[:, :], 1.0))
    bb = pr.sbuf(f"{tag}_bb", [128, E], F32)
    pg = pr.psum(f"{tag}_pg", [128, E], F32)
    if rb_dram is not None:
        rb = pr.sbuf(f"{tag}_rb", [1, E], F32)
        hrb = kb.dma("sync", rb[:, :], rb_dram.unsqueeze(0))
        hb1 = kb.emit("tensor", lambda e: e.matmul(pg[:, :], onesf[:, :], rb[:, :],
                                                   start=True, stop=True),
                      deps=[hrb, h_onesf])
        hbb = kb.emit("vector", lambda e: e.tensor_copy(bb[:, :], pg[:, :]), deps=[hb1])
    else:
        hbb = kb.emit("vector", lambda e: e.memset(bb[:, :], 0.0))
    gsc = sbuf_ring(pr, f"{tag}_gs", 2, [128, 3 * E + 8], F32)
    cbf = sbuf_ring(pr, f"{tag}_cbf", 2, [128, E], BF16)
    pT = psum_ring(pr, f"{tag}_pT", 2, [E, 128], BF16)
    comb_h = []
    combT_h = []
    prev_pg = [hbb]
    for t in range(NT):
        hmm = kb.emit("tensor", lambda e, t=t: [
            e.matmul(pg[:, :], xnT[:, T * k + 128 * t:T * k + 128 * (t + 1)],
                     rw[:, E * k:E * (k + 1)], start=(k == 0), stop=(k == 7))
            for k in range(8)][-1], deps=[hrw, xnT_dep] + prev_pg)
        gb, wg, sg = gsc.next()
        G = slice(0, E)
        E1 = slice(E, 2 * E)
        MK = slice(2 * E, 3 * E)
        s0, s1, s2, s3, s4, s5 = (slice(3 * E + i, 3 * E + i + 1) for i in range(6))
        h0 = kb.emit("vector", lambda e, gb=gb: e.tensor_tensor(gb[:, G], pg[:, :], bb[:, :], ALU.add),
                     deps=[hmm, wg])
        prev_pg = [h0]
        if kind == "sigmoid_top2":
            h0 = kb.emit("scalar", lambda e, gb=gb: e.activation(gb[:, G], gb[:, G], AF.Sigmoid),
                         deps=[h0])
            prev_pg = [h0]
        hm1 = kb.emit("vector", lambda e, gb=gb: e.reduce_max(gb[:, s0], gb[:, G], AX), deps=[h0])
        he1 = kb.emit("vector", lambda e, gb=gb: e.tensor_scalar(
            gb[:, E1], gb[:, G], gb[:, s0], None, ALU.is_equal), deps=[hm1])
        hmk = kb.emit("vector", lambda e, gb=gb: e.scalar_tensor_tensor(
            gb[:, MK], gb[:, E1], -1e30, gb[:, G], ALU.mult, ALU.add), deps=[he1])
        hm2 = kb.emit("vector", lambda e, gb=gb: e.reduce_max(gb[:, s1], gb[:, MK], AX), deps=[hmk])
        he2 = kb.emit("vector", lambda e, gb=gb: e.tensor_scalar(
            gb[:, MK], gb[:, MK], gb[:, s1], None, ALU.is_equal), deps=[hm2])
        if kind == "softmax_top2":
            hd = kb.emit("vector", lambda e, gb=gb: e.tensor_tensor(
                gb[:, s2], gb[:, s1], gb[:, s0], ALU.subtract), deps=[hm2])
            hx = kb.emit("scalar", lambda e, gb=gb: e.activation(gb[:, s3], gb[:, s2], AF.Exp), deps=[hd])
            hq = kb.emit("vector", lambda e, gb=gb: e.tensor_scalar_add(gb[:, s4], gb[:, s3], 1.0), deps=[hx])
            hp1 = kb.emit("vector", lambda e, gb=gb: e.reciprocal(gb[:, s2], gb[:, s4]), deps=[hq])
            hp2 = kb.emit("vector", lambda e, gb=gb: e.tensor_tensor(gb[:, s5], gb[:, s3], gb[:, s2], ALU.mult), deps=[hp1])
        else:
            hq = kb.emit("vector", lambda e, gb=gb: e.tensor_scalar(
                gb[:, s4], gb[:, s1], 1e-6, gb[:, s0], ALU.add, ALU.add), deps=[hm2])
            hr0 = kb.emit("vector", lambda e, gb=gb: e.reciprocal(gb[:, s3], gb[:, s4]), deps=[hq])
            hp1 = kb.emit("vector", lambda e, gb=gb: e.tensor_tensor(gb[:, s2], gb[:, s0], gb[:, s3], ALU.mult), deps=[hr0])
            hp2 = kb.emit("vector", lambda e, gb=gb: e.tensor_tensor(gb[:, s5], gb[:, s1], gb[:, s3], ALU.mult), deps=[hr0])
        hc1 = kb.emit("vector", lambda e, gb=gb, t=t: e.tensor_scalar(
            comb_sb[:, E * t:E * (t + 1)], gb[:, E1], gb[:, s2], None, ALU.mult),
            deps=[he1, hp1])
        hcm = kb.emit("vector", lambda e, gb=gb, t=t: e.scalar_tensor_tensor(
            comb_sb[:, E * t:E * (t + 1)], gb[:, MK], gb[:, s5],
            comb_sb[:, E * t:E * (t + 1)], ALU.mult, ALU.add), deps=[hc1, he2, hp2])
        comb_h.append(hcm)
        cb, wc, sc = cbf.next()
        hcb = kb.emit("vector", lambda e, cb=cb, t=t: e.tensor_copy(
            cb[:, :], comb_sb[:, E * t:E * (t + 1)]), deps=[hcm, wc])
        pb, wp, sp = pT.next()
        htr = kb.emit("tensor", lambda e, cb=cb, pb=pb: e.transpose(pb[:, :], cb[:, :], C.ident[:, :]),
                      deps=[hcb, wp, C.ready])
        hct = kb.emit("scalar", lambda e, pb=pb, t=t: e.copy(combT[0:E, 128 * t:128 * (t + 1)], pb[:, :]),
                      deps=[htr])
        pT.used(sp, hct)
        cbf.used(sc, htr)
        gsc.used(sg, [hc1, hcm])
        combT_h.append(hct)
    return comb_h, combT_h


def emit_moe(pr, C, xnT, xnT_dep, cfg, x_in_ap, x_out_ap, s_dram, tag,
             dbg_g=None, bg=None):
    kb = pr
    E = 8
    comb_sb = pr.sbuf(f"{tag}_comb", [128, NT * E], F32)
    combT = pr.sbuf(f"{tag}_combT", [8, T], BF16)
    pr.push()
    comb_h, combT_h = emit_router(
        pr, C, xnT, xnT_dep, cfg["router_w"], cfg.get("router_b"),
        cfg["router_kind"], tag + "r", comb_sb, combT)
    pr.pop()
    act = cfg["act"]
    g_acc = pr.sbuf(f"{tag}_gacc", [128, NT * D], F32)
    hz = kb.emit("vector", lambda e: e.memset(g_acc[:, :], 0.0))
    g_last = [hz] * NT
    w1sb = pr.sbuf(f"{tag}_w1", [128, 8 * 2048], BF16)
    w2sb = pr.sbuf(f"{tag}_w2", [128, 16 * 1024], BF16)
    b1sb = pr.sbuf(f"{tag}_b1", [128, 16], F32)
    h_sb = pr.sbuf(f"{tag}_h", [128, 16 * 512], BF16)
    ph = psum_ring(pr, f"{tag}_ph", 3, [128, 512], F32)
    py = psum_ring(pr, f"{tag}_py", 2, [128, 1024], F32)
    w1_war = [None]
    w2_war = [None]
    b1_war = [None]
    h_war = [[]]
    for ei, ex in enumerate(cfg["experts"]):
        lw1 = [kb.dma("gpsimd", w1sb[:, 2048 * kc:2048 * (kc + 1)],
                      ex["w1"][128 * kc:128 * (kc + 1), :], deps=[w1_war[0]])
               for kc in range(8)]
        lw2 = [kb.dma("gpsimd", w2sb[:, 1024 * hc:1024 * (hc + 1)],
                      ex["w2"][128 * hc:128 * (hc + 1), :], deps=[w2_war[0]])
               for hc in range(16)]
        if ex.get("b1") is not None:
            lb1 = kb.dma("scalar", b1sb[:, :],
                         ex["b1"].rearrange("(c p) -> p c", p=128),
                         deps=[b1_war[0]], allow_slow_non_contiguous=True)
        else:
            lb1 = kb.emit("vector", lambda e: e.memset(b1sb[:, :], 0.0),
                          deps=[b1_war[0]])
        last_h_step = None
        last_y_step = None
        he = None
        for tq in range(4):
            hevs = []
            for hc in range(16):
                phb, wph, sph = ph.next()
                hmmh = kb.emit("tensor", lambda e, phb=phb, hc=hc, tq=tq: [
                    e.matmul(phb[:, :],
                             w1sb[:, 2048 * kc + 128 * hc:2048 * kc + 128 * (hc + 1)],
                             xnT[:, T * kc + 512 * tq:T * kc + 512 * (tq + 1)],
                             start=(kc == 0), stop=(kc == 7)) for kc in range(8)][-1],
                    deps=lw1 + [xnT_dep, wph])
                he = kb.emit("scalar", lambda e, phb=phb, hc=hc: e.activation(
                    h_sb[:, 512 * hc:512 * (hc + 1)], phb[:, :], act,
                    bias=b1sb[:, hc:hc + 1]),
                    deps=[hmmh, lb1] + (h_war[0] if hc == 0 else []))
                ph.used(sph, he)
                hevs.append(he)
                last_h_step = hmmh
            ysteps = []
            for tl in range(4):
                tile = 4 * tq + tl
                pyb, wpy, spy = py.next()
                hy = kb.emit("tensor", lambda e, pyb=pyb, tl=tl: [
                    e.matmul(pyb[:, 512 * nh:512 * (nh + 1)],
                             h_sb[:, 512 * hc + 128 * tl:512 * hc + 128 * (tl + 1)],
                             w2sb[:, 1024 * hc + 512 * nh:1024 * hc + 512 * (nh + 1)],
                             start=(hc == 0), stop=(hc == 15))
                    for hc in range(16) for nh in range(2)][-1],
                    deps=hevs + lw2 + [wpy])
                comb = ex["comb"]
                if comb[0] == "col":
                    eidx = comb[1]
                    hep = kb.emit("vector", lambda e, pyb=pyb, tile=tile, eidx=eidx: e.scalar_tensor_tensor(
                        g_acc[:, D * tile:D * (tile + 1)], pyb[:, :],
                        comb_sb[:, E * tile + eidx:E * tile + eidx + 1],
                        g_acc[:, D * tile:D * (tile + 1)], ALU.mult, ALU.add),
                        deps=[hy, comb_h[tile], g_last[tile]])
                else:
                    val = float(comb[1])
                    hep = kb.emit("vector", lambda e, pyb=pyb, tile=tile, val=val: e.scalar_tensor_tensor(
                        g_acc[:, D * tile:D * (tile + 1)], pyb[:, :], val,
                        g_acc[:, D * tile:D * (tile + 1)], ALU.mult, ALU.add),
                        deps=[hy, g_last[tile]])
                g_last[tile] = hep
                py.used(spy, hep)
                ysteps.append(hy)
                last_y_step = hy
            h_war[0] = ysteps
            if bg:
                bg.pump(6)
        w1_war[0] = last_h_step
        w2_war[0] = last_y_step
        b1_war[0] = he
    nbias = len(cfg["bias_items"])
    bsb = []
    for bi, item in enumerate(cfg["bias_items"]):
        if item[0] == "comb":
            b2 = pr.sbuf(f"{tag}_b2{bi}", [E, D], BF16)
            hb = kb.dma("gpsimd", b2[:, :], item[1])
            bsb.append(("comb", b2, hb))
        else:
            S = item[1].shape[0]
            b2 = pr.sbuf(f"{tag}_sb2{bi}", [S, D], BF16)
            hb = kb.dma("gpsimd", b2[:, :], item[1])
            row = pr.sbuf(f"{tag}_row{bi}", [S, 128], BF16)
            val = float(item[2])
            hrow = kb.emit("vector", lambda e, row=row, val=val: e.memset(row[:, :], val))
            bsb.append(("mean", b2, hb, row, hrow, S))
    xr = sbuf_ring(pr, f"{tag}_xr", 1, [128, D], F32)
    sr = sbuf_ring(pr, f"{tag}_sr", 1, [128, D], BF16)
    gt = sbuf_ring(pr, f"{tag}_gt", 1, [128, D], F32)
    out_h = []
    for tile in range(NT):
        pyb, wpy, spy = py.next()

        def bias_mms(e, pyb=pyb, tile=tile):
            n_items = len(bsb)
            inst = None
            for ii, item in enumerate(bsb):
                for nh in range(2):
                    if item[0] == "comb":
                        _, b2, hb = item
                        lhsT = combT[0:E, 128 * tile:128 * (tile + 1)]
                        rhs = b2[0:E, 512 * nh:512 * (nh + 1)]
                    else:
                        _, b2, hb, row, hrow, S = item
                        lhsT = row[0:S, 0:128]
                        rhs = b2[0:S, 512 * nh:512 * (nh + 1)]
                    inst = e.matmul(pyb[:, 512 * nh:512 * (nh + 1)], lhsT, rhs,
                                    start=(ii == 0), stop=(ii == n_items - 1))
            return inst
        deps_b = [combT_h[tile], wpy]
        for item in bsb:
            deps_b += [item[2]]
            if item[0] == "mean":
                deps_b += [item[4]]
        hbias = kb.emit("tensor", bias_mms, deps=deps_b)
        gb, wgt, sgt = gt.next()
        hsum = kb.emit("vector", lambda e, gb=gb, pyb=pyb, tile=tile: e.tensor_tensor(
            gb[:, :], g_acc[:, D * tile:D * (tile + 1)], pyb[:, :], ALU.add),
            deps=[hbias, g_last[tile], wgt])
        py.used(spy, hsum)
        hsil = kb.emit("scalar", lambda e, gb=gb: e.activation(gb[:, :], gb[:, :], AF.Silu),
                       deps=[hsum])
        if dbg_g is not None:
            hdg = kb.dma("sync", dbg_g[128 * tile:128 * (tile + 1), :], gb[:, :], deps=[hsil])
            gt.used(sgt, hdg)
            out_h.append(hdg)
            continue
        xb, wx, sx = xr.next()
        hx = kb.dma("scalar", xb[:, :], x_in_ap[128 * tile:128 * (tile + 1), :], deps=[wx])
        sb_, wsr, ssx = sr.next()
        hs = kb.dma("scalar", sb_[:, :], s_dram[128 * tile:128 * (tile + 1), :], deps=[wsr])
        hm = kb.emit("vector", lambda e, gb=gb, sb_=sb_: e.tensor_tensor(
            gb[:, :], gb[:, :], sb_[:, :], ALU.mult), deps=[hsil, hs])
        ho = kb.emit("vector", lambda e, gb=gb, xb=xb: e.tensor_tensor(
            gb[:, :], gb[:, :], xb[:, :], ALU.add), deps=[hm, hx])
        hw = kb.dma("sync", x_out_ap[128 * tile:128 * (tile + 1), :], gb[:, :], deps=[ho])
        xr.used(sx, ho)
        sr.used(ssx, hm)
        gt.used(sgt, hw)
        out_h.append(hw)
        if bg:
            bg.pump(2)
    return out_h




def mamba_ctx(pr, tag="mb"):
    ctx = dict(
        W_in=pr.din("L0_attn_W_in", [1024, 4384]),
        conv_w=pr.din("L0_attn_conv_w", [4, 2304]),
        conv_b=pr.din("L0_attn_conv_b", [2304]),
        A_log=pr.din("L0_attn_A_log", [32]),
        Dvec=pr.din("L0_attn_D", [32]),
        dt_bias=pr.din("L0_attn_dt_bias", [32]),
        gamma=pr.din("L0_attn_gamma", [2048]),
        W_out=pr.din("L0_attn_W_out", [2048, 1024]),
        BT=pr.sbuf(f"{tag}_BT", [128, T], BF16),
        CT=pr.sbuf(f"{tag}_CT", [128, T], BF16),
        dtT=pr.sbuf(f"{tag}_dtT", [32, T], F32))
    return ctx


def emit_mamba_ab(pr, C, xnT, xnT_dep, z_dram, x_dram, ctx, tag="mb", bg=None):
    kb = pr
    W_in, conv_w, conv_b = ctx["W_in"], ctx["conv_w"], ctx["conv_b"]
    BT, CT, dtT = ctx["BT"], ctx["CT"], ctx["dtT"]
    DS = 2048
    NH = 32
    pr.push()
    wz = pr.sbuf(f"{tag}_wz", [128, 8 * DS], BF16)
    lwz = [kb.dma("gpsimd", wz[:, DS * k:DS * (k + 1)],
                  W_in[128 * k:128 * (k + 1), 0:DS]) for k in range(8)]
    zt = sbuf_ring(pr, f"{tag}_zt", 2, [128, DS], BF16)
    pz = psum_ring(pr, f"{tag}_pz", 4, [128, 512], F32)
    for t in range(NT):
        zb, wzt, szt = zt.next()
        evs = []
        for c4 in range(4):
            pzb, wpz, spz = pz.next()
            hm = kb.emit("tensor", lambda e, pzb=pzb, t=t, c4=c4: [
                e.matmul(pzb[:, :], xnT[:, T * k + 128 * t:T * k + 128 * (t + 1)],
                         wz[:, DS * k + 512 * c4:DS * k + 512 * (c4 + 1)],
                         start=(k == 0), stop=(k == 7)) for k in range(8)][-1],
                deps=lwz + [xnT_dep, wpz])
            he = kb.emit("scalar", lambda e, pzb=pzb, zb=zb, c4=c4: e.copy(
                zb[:, 512 * c4:512 * (c4 + 1)], pzb[:, :]), deps=[hm, wzt])
            pz.used(spz, he)
            evs.append(he)
        hd = kb.dma("scalar", z_dram[128 * t:128 * (t + 1), :], zb[:, :], deps=evs)
        zt.used(szt, hd)
        if bg:
            bg.pump(4)
    pr.pop()

    pr.push()
    x_tok = pr.sbuf(f"{tag}_xtok", [128, NT * DS], BF16)
    wst = sbuf_ring(pr, f"{tag}_ws", 4, [128, 1024], BF16)
    cxr = sbuf_ring(pr, f"{tag}_cx", 2, [128, T + 3], F32)
    cw = pr.sbuf(f"{tag}_cw", [128, 18 * 4], F32)
    cb = pr.sbuf(f"{tag}_cb", [128, 18], F32)
    hcw = [kb.dma("scalar", cw[:, 18 * k:18 * (k + 1)],
                  conv_w[k].rearrange("(f p) -> p f", p=128),
                  allow_slow_non_contiguous=True) for k in range(4)]
    hcb = kb.dma("scalar", cb[:, :], conv_b.rearrange("(f p) -> p f", p=128),
                 allow_slow_non_contiguous=True)
    hz3a = kb.emit("vector", lambda e: e.memset(cxr.bufs[0][:, 0:3], 0.0))
    hz3 = kb.emit("vector", lambda e: e.memset(cxr.bufs[1][:, 0:3], 0.0))
    pxb = psum_ring(pr, f"{tag}_pxb", 4, [128, 512], F32)
    ptr = psum_ring(pr, f"{tag}_ptr", 2, [128, 128], BF16)
    cvar = sbuf_ring(pr, f"{tag}_cva", 2, [128, T], F32)
    sxbr = sbuf_ring(pr, f"{tag}_sxb", 2, [128, T], BF16)
    prev_cx = [hz3]
    xtile_h = [[] for _ in range(NT)]
    bc_ready = None

    def mm_part(fc):
        wb, wwar, wslot = wst.next()
        lw = kb.dma("gpsimd", wb[:, :].rearrange("p (k c) -> p k c", c=128),
                    W_in[:, DS + 128 * fc:DS + 128 * (fc + 1)]
                    .rearrange("(k p) c -> p k c", p=128), deps=[wwar])
        evs = []
        hm = None
        cx, wcx, scx = cxr.next()
        for tq in range(4):
            pb, wp, sp = pxb.next()
            hm = kb.emit("tensor", lambda e, pb=pb, tq=tq, wb=wb: [
                e.matmul(pb[:, :], wb[:, 128 * k:128 * (k + 1)],
                         xnT[:, T * k + 512 * tq:T * k + 512 * (tq + 1)],
                         start=(k == 0), stop=(k == 7)) for k in range(8)][-1],
                deps=[lw, xnT_dep, wp])
            he = kb.emit("scalar", lambda e, pb=pb, tq=tq, cx=cx: e.copy(
                cx[:, 3 + 512 * tq:3 + 512 * (tq + 1)], pb[:, :]),
                deps=[hm] + (wcx if tq == 0 else []))
            pxb.used(sp, he)
            evs.append(he)
        wst.used(wslot, hm)
        return (fc, cx, scx, evs)

    def conv_part(stb):
        nonlocal bc_ready
        fc, cx, scx, evs = stb
        cva, wcv, scv = cvar.next()
        h1 = kb.emit("vector", lambda e, fc=fc, cva=cva, cx=cx: e.tensor_scalar(
            cva[:, :], cx[:, 0:T], cw[:, fc:fc + 1], cb[:, fc:fc + 1],
            ALU.mult, ALU.add), deps=evs + hcw + [hcb, hz3a, hz3] + wcv)
        for k in range(1, 4):
            h1 = kb.emit("vector", lambda e, fc=fc, k=k, cva=cva, cx=cx: e.scalar_tensor_tensor(
                cva[:, :], cx[:, k:k + T], cw[:, 18 * k + fc:18 * k + fc + 1],
                cva[:, :], ALU.mult, ALU.add), deps=[h1])
        cxr.used(scx, h1)
        sxb, wsx, ssx = sxbr.next()
        hs = kb.emit("scalar", lambda e, cva=cva, sxb=sxb: e.activation(
            sxb[:, :], cva[:, :], AF.Silu), deps=[h1] + wsx)
        cvar.used(scv, hs)
        if fc < 16:
            tevs = []
            for t in range(NT):
                pb, wp, sp = ptr.next()
                ht = kb.emit("tensor", lambda e, pb=pb, t=t, sxb=sxb: e.transpose(
                    pb[:, :], sxb[:, 128 * t:128 * (t + 1)], C.ident[:, :]),
                    deps=[hs, wp, C.ready])
                hev = kb.emit("scalar", lambda e, pb=pb, t=t, fc=fc: e.copy(
                    x_tok[:, DS * t + 128 * fc:DS * t + 128 * (fc + 1)], pb[:, :]),
                    deps=[ht])
                ptr.used(sp, hev)
                xtile_h[t].append(hev)
                tevs.append(hev)
            sxbr.used(ssx, tevs[-1])
        else:
            dst = BT if fc == 16 else CT
            hcp = kb.emit("vector", lambda e, dst=dst, sxb=sxb: e.tensor_copy(dst[:, :], sxb[:, :]),
                          deps=[hs])
            sxbr.used(ssx, hcp)
            bc_ready = hcp

    pend_b = None
    for fc in range(18):
        stb = mm_part(fc)
        if pend_b is not None:
            conv_part(pend_b)
        pend_b = stb
        if bg:
            bg.pump(4)
    conv_part(pend_b)
    for t in range(NT):
        kb.dma("scalar", x_dram[128 * t:128 * (t + 1), :],
               x_tok[:, DS * t:DS * (t + 1)], deps=[xtile_h[t][-1]])
    pdt = pr.psum(f"{tag}_pdt", [32, 512], F32)
    wdt = pr.sbuf(f"{tag}_wdt", [128, 8 * 32], BF16)
    ldt = kb.dma("gpsimd", wdt[:, :].rearrange("p (k c) -> p k c", c=32),
                 W_in[:, 4352:4384].rearrange("(k p) c -> p k c", p=128))
    prev_dt = []
    for tq in range(4):
        hmdt = kb.emit("tensor", lambda e, tq=tq: [
            e.matmul(pdt[:, :], wdt[:, 32 * k:32 * (k + 1)],
                     xnT[:, T * k + 512 * tq:T * k + 512 * (tq + 1)],
                     start=(k == 0), stop=(k == 7)) for k in range(8)][-1],
            deps=[ldt, xnT_dep] + prev_dt)
        hdt = kb.emit("vector", lambda e, tq=tq: e.tensor_copy(
            dtT[:, 512 * tq:512 * (tq + 1)], pdt[:, :]), deps=[hmdt])
        prev_dt = [hdt]
        if bg:
            bg.pump(2)
    dt_ready = prev_dt[0]
    pr.pop()

    ctx["bc_ready"] = bc_ready
    ctx["dt_ready"] = dt_ready


def emit_mamba_c(pr, C, s_dram, z_dram, x_dram, ctx, tag="mb", bg=None):
    kb = pr
    A_log, Dvec, dt_bias = ctx["A_log"], ctx["Dvec"], ctx["dt_bias"]
    gamma, W_out = ctx["gamma"], ctx["W_out"]
    BT, CT, dtT = ctx["BT"], ctx["CT"], ctx["dtT"]
    bc_ready, dt_ready = ctx["bc_ready"], ctx["dt_ready"]
    DS = 2048
    NH = 32
    pr.push()
    wout_sb = pr.sbuf(f"{tag}_wo", [128, 16 * 1024], BF16)
    lwo = [kb.dma("gpsimd", wout_sb[:, 1024 * kc:1024 * (kc + 1)],
                  W_out[128 * kc:128 * (kc + 1), :]) for kc in range(16)]
    sv = pr.sbuf(f"{tag}_sv", [32, 8], F32)
    hdb = kb.dma("sync", sv[0:32, 0:1], dt_bias.unsqueeze(1))
    hal = kb.dma("sync", sv[0:32, 1:2], A_log.unsqueeze(1))
    hea = kb.emit("scalar", lambda e: e.activation(sv[0:32, 3:4], sv[0:32, 1:2], AF.Exp),
                  deps=[hal])
    bbT = pr.sbuf(f"{tag}_bbT", [32, T], F32)
    osb = pr.sbuf(f"{tag}_osb", [64, T], F32)
    ebtok = pr.sbuf(f"{tag}_ebtok", [128, NT * 32], F32)
    lamall = pr.sbuf(f"{tag}_lamall", [128, NT * 32], F32)
    w2fall = pr.sbuf(f"{tag}_w2fa", [128, NT * 32], F32)
    lrow = pr.sbuf(f"{tag}_lrow", [1, NT * 32], F32)
    psA = pr.psum(f"{tag}_psA", [128, 2048], F32)
    psB = pr.psum(f"{tag}_psB", [128, 1024], F32)
    psT0 = pr.psum(f"{tag}_psT0", [128, 128], BF16)
    psT1 = pr.psum(f"{tag}_psT1", [128, 128], BF16)
    onesf = pr.sbuf(f"{tag}_onesf", [1, 128], F32)
    honf = kb.emit("vector", lambda e: e.memset(onesf[:, :], 1.0))
    onec = pr.sbuf(f"{tag}_onec", [128, 1], F32)
    honc = kb.emit("vector", lambda e: e.memset(onec[:, :], 1.0))
    pr.push()
    deltaT = pr.sbuf(f"{tag}_delta", [32, T], F32)
    ldmbT = pr.sbuf(f"{tag}_ldmb", [32, T], F32)
    aT = pr.sbuf(f"{tag}_aT", [32, T], F32)
    atok = pr.sbuf(f"{tag}_atok", [128, NT * 32], F32)
    u1 = pr.sbuf(f"{tag}_u1", [32, T], F32)
    u2 = pr.sbuf(f"{tag}_u2", [32, T], F32)
    q1 = kb.emit("vector", lambda e: e.tensor_scalar(
        u1[:, :], dtT[:, :], sv[0:32, 0:1], None, ALU.add), deps=[dt_ready, hdb])
    q2 = kb.emit("scalar", lambda e: e.activation(u2[:, :], u1[:, :], AF.Abs), deps=[q1])
    q3 = kb.emit("scalar", lambda e: e.activation(u2[:, :], u2[:, :], AF.Exp, scale=-1.0), deps=[q2])
    q4 = kb.emit("vector", lambda e: e.tensor_scalar_add(u2[:, :], u2[:, :], 1.0), deps=[q3])
    q5 = kb.emit("scalar", lambda e: e.activation(u2[:, :], u2[:, :], AF.Ln), deps=[q4])
    q6 = kb.emit("vector", lambda e: e.tensor_scalar_max(u1[:, :], u1[:, :], 0.0), deps=[q1, q2])
    q7 = kb.emit("vector", lambda e: e.tensor_add(deltaT[:, :], u1[:, :], u2[:, :]), deps=[q5, q6])
    q8 = kb.emit("scalar", lambda e: e.activation(u1[:, :], deltaT[:, :], AF.Ln), deps=[q7])
    q9 = kb.emit("vector", lambda e: e.tensor_scalar(
        aT[:, :], deltaT[:, :], sv[0:32, 3:4], -1.0, ALU.mult, ALU.mult), deps=[q7, hea])
    ones32 = pr.sbuf(f"{tag}_o32", [32, 1], F32)
    ho32 = kb.emit("vector", lambda e: e.memset(ones32[:, :], 1.0))
    hbb = None
    for c in range(NT):
        hbb = kb.emit("vector", lambda e, c=c: e.tensor_tensor_scan(
            bbT[:, 128 * c:128 * (c + 1)], ones32[:, :].broadcast_to([32, 128]),
            aT[:, 128 * c:128 * (c + 1)], 0.0, ALU.mult, ALU.add),
            deps=[q9, ho32])
    qA = kb.emit("vector", lambda e: e.tensor_sub(ldmbT[:, :], u1[:, :], bbT[:, :]), deps=[q8, hbb])
    hos1 = kb.emit("vector", lambda e: e.tensor_copy(osb[0:32, :], ldmbT[:, :]), deps=[qA])
    hos2 = kb.emit("vector", lambda e: e.memset(osb[32:64, :], 1.0))

    prep_evs = []
    prev_slot = [None, None]
    prev_lam = None
    for c in range(NT):
        cs = slice(128 * c, 128 * (c + 1))
        ts = slice(32 * c, 32 * (c + 1))
        for j, (srcT, dst, dep0) in enumerate(
                ((bbT, ebtok, hbb), (ldmbT, w2fall, qA), (aT, atok, q9))):
            bank = (c * 3 + j) % 2
            sl = slice(512 * bank, 512 * bank + 32)
            ht = kb.emit("tensor", lambda e, srcT=srcT, cs=cs, sl=sl: e.transpose(
                psA[:, sl], srcT[:, cs], C.identf[0:32, 0:32]),
                deps=[dep0, C.ready, prev_slot[bank]])
            hv2 = kb.emit("vector", lambda e, dst=dst, sl=sl, ts=ts: e.tensor_copy(
                dst[:, ts], psA[:, sl]), deps=[ht])
            prev_slot[bank] = hv2
            prep_evs.append(hv2)
        hl = kb.emit("tensor", lambda e, ts=ts, c=c: e.matmul(
            psB[0:1, 32 * (c % 8):32 * (c % 8) + 32], onec[:, :], atok[:, ts],
            start=True, stop=True), deps=[prep_evs[-1], honc, prev_lam])
        hr = kb.emit("vector", lambda e, ts=ts, c=c: e.tensor_copy(
            lrow[0:1, ts], psB[0:1, 32 * (c % 8):32 * (c % 8) + 32]), deps=[hl])
        prev_lam = hr
        prep_evs.append(hr)
        if bg:
            bg.pump(2)
    pr.pop()
    hlb = None
    lb_evs = []
    for i in range(NT * 32 // 512):
        hmb = kb.emit("tensor", lambda e, i=i: e.matmul(
            psA[:, 512:1024], onesf[:, :], lrow[0:1, 512 * i:512 * (i + 1)],
            start=True, stop=True), deps=prep_evs + [honf, hlb])
        hlb = kb.emit("vector", lambda e, i=i: e.tensor_copy(
            lamall[:, 512 * i:512 * (i + 1)], psA[:, 512:1024]), deps=[hmb])
        lb_evs.append(hlb)
    hw2a = kb.emit("vector", lambda e: e.tensor_add(w2fall[:, :], w2fall[:, :], lamall[:, :]),
                   deps=lb_evs + prep_evs)
    hw2b = kb.emit("scalar", lambda e: e.activation(w2fall[:, :], w2fall[:, :], AF.Exp), deps=[hw2a])
    hlam = kb.emit("scalar", lambda e: e.activation(lamall[:, :], lamall[:, :], AF.Exp),
                   deps=[hw2a] + lb_evs)
    hebt = kb.emit("scalar", lambda e: e.activation(ebtok[:, :], ebtok[:, :], AF.Exp), deps=prep_evs)

    grow = pr.sbuf(f"{tag}_grow", [1, DS], F32)
    hgr = kb.dma("sync", grow[:, :], gamma.unsqueeze(0))
    Drow = pr.sbuf(f"{tag}_Drow", [1, 32], F32)
    hDrw = kb.dma("sync", Drow[:, :], Dvec.unsqueeze(0))
    gbc = pr.sbuf(f"{tag}_gbc", [128, DS], BF16)
    Dbc = pr.sbuf(f"{tag}_Dbc", [128, 32], F32)
    hg1 = kb.emit("tensor", lambda e: [
        e.matmul(psA[:, 512 * i:512 * (i + 1)], onesf[:, :], grow[:, 512 * i:512 * (i + 1)],
                 start=True, stop=True) for i in range(4)][-1],
        deps=[hgr, honf, hlb] + prep_evs)
    hg2 = kb.emit("vector", lambda e: e.tensor_copy(gbc[:, :], psA[:, :]), deps=[hg1])
    hg3 = kb.emit("tensor", lambda e: e.matmul(psA[:, 0:32], onesf[:, :], Drow[:, :],
                                               start=True, stop=True), deps=[hDrw, honf, hg2])
    hg4 = kb.emit("vector", lambda e: e.tensor_copy(Dbc[:, :], psA[:, 0:32]), deps=[hg3])

    blockones = C.blockones_ap
    rhs_t = pr.sbuf(f"{tag}_rhs", [64, 4096], F32)
    hbo = kb.dma("sync", rhs_t[0:32, :], blockones)
    bones3 = pr.sbuf(f"{tag}_bo3", [32, 4096], BF16)
    hbo2 = kb.dma("gpsimd", bones3[:, :], blockones)

    hstf = pr.sbuf(f"{tag}_hst", [128, DS], F32)
    hstb = pr.sbuf(f"{tag}_hstbf", [128, DS], BF16)
    hh0 = kb.emit("vector", lambda e: e.memset(hstf[:, :], 0.0))
    hh1 = kb.emit("vector", lambda e: e.memset(hstb[:, :], 0.0))

    atbf = pr.sbuf(f"{tag}_atbf", [128, 128], BF16)
    btok = pr.sbuf(f"{tag}_btok", [128, 128], BF16)
    bwa = pr.sbuf(f"{tag}_bwa", [128, 4096], BF16)
    wsc = pr.sbuf(f"{tag}_wsc", [128, 1024], F32)
    watbf = pr.sbuf(f"{tag}_watbf", [128, 4096], BF16)
    ysum = pr.sbuf(f"{tag}_ysum", [128, DS], F32)
    yscr = pr.sbuf(f"{tag}_yscr", [128, DS], BF16)
    ssmT = pr.sbuf(f"{tag}_ssmT", [128, 16 * 128], BF16)
    sbf = pr.sbuf(f"{tag}_sbf", [128, 1024], BF16)
    zld = sbuf_ring(pr, f"{tag}_zld", 1, [128, DS], BF16)
    xld = sbuf_ring(pr, f"{tag}_xld", 2, [128, DS], BF16)
    rstat = pr.sbuf(f"{tag}_rstat", [128, 4], F32)

    pv = {"psA": [hg4], "psB": [prev_lam], "rhs": [hbo], "wsc": [],
          "watbf": [], "atbf": [], "btok": [], "bwa": [], "ysum": [], "yscr": [],
          "ssmbf": [], "ssmT": [], "sbf": [], "zsil": [], "rstat": [],
          "y2": [], "hstb": [hh1], "hstf": [hh0], "psT0": None, "psT1": None}

    for c in range(NT):
        cs = slice(128 * c, 128 * (c + 1))
        ts = slice(32 * c, 32 * (c + 1))
        xb, wxl, sxl = xld.next()
        h_xl = kb.dma("gpsimd", xb[:, :], x_dram[cs, :], deps=[wxl])
        h_at = kb.emit("tensor", lambda e, cs=cs: e.matmul(
            psB[:, 0:128], BT[:, cs], CT[:, cs], start=True, stop=True),
            deps=[bc_ready] + pv["psB"])
        h_ae = kb.emit("scalar", lambda e: e.copy(atbf[:, :], psB[:, 0:128]),
                       deps=[h_at] + pv["atbf"])
        h_bd = kb.emit("gpsimd", lambda e, cs=cs: e.tensor_tensor(
            rhs_t[32:64, :].rearrange("p (h t) -> p h t", t=128),
            bbT[:, cs].unsqueeze(1).broadcast_to([32, 32, 128]),
            bones3[:, :].rearrange("p (h t) -> p h t", t=128), ALU.mult),
            deps=[hbb, hbo2] + pv["rhs"])
        wat_evs = []
        h_wq_prev = h_ae
        last_wq = None
        for qq in range(4):
            qsl = slice(1024 * qq, 1024 * (qq + 1))
            h_wq = kb.emit("tensor", lambda e, cs=cs, qq=qq: [
                e.matmul(psB[:, 512 * i:512 * (i + 1)],
                         osb[0:64, cs],
                         rhs_t[0:64, 1024 * qq + 512 * i:1024 * qq + 512 * (i + 1)],
                         start=True, stop=True) for i in range(2)][-1],
                deps=[hos1, hos2, h_bd, h_wq_prev])
            last_wq = h_wq
            h_wm = kb.emit("vector", lambda e: e.tensor_tensor(
                wsc[:, :].rearrange("p (h t) -> p h t", t=128),
                psB[:, :].rearrange("p (h t) -> p h t", t=128),
                C.maskb[:, :].unsqueeze(1).broadcast_to([128, 8, 128]), ALU.add),
                deps=[h_wq] + pv["wsc"])
            h_we = kb.emit("scalar", lambda e: e.activation(wsc[:, :], wsc[:, :], AF.Exp),
                           deps=[h_wm])
            h_wt = kb.emit("vector", lambda e, qsl=qsl: e.tensor_tensor(
                watbf[:, qsl].rearrange("p (h t) -> p h t", t=128),
                wsc[:, :].rearrange("p (h t) -> p h t", t=128),
                atbf[:, :].unsqueeze(1).broadcast_to([128, 8, 128]), ALU.mult),
                deps=[h_we, h_ae] + (pv["watbf"] if qq == 0 else []))
            pv["wsc"] = [h_wt]
            h_wq_prev = h_wm
            wat_evs.append(h_wt)
        h_wm_last = h_wq_prev
        h_y1 = kb.emit("tensor", lambda e, xb=xb: [
            e.matmul(psA[:, 64 * h:64 * (h + 1)], watbf[:, 128 * h:128 * (h + 1)],
                     xb[:, 64 * h:64 * (h + 1)], start=True, stop=True)
            for h in range(NH)][-1],
            deps=wat_evs + [h_xl] + pv["psA"])
        yc_evs = []
        y2_mms = []
        for hh in range(2):
            hsl = slice(1024 * hh, 1024 * (hh + 1))
            h_y2 = kb.emit("tensor", lambda e, cs=cs, hsl=hsl: [
                e.matmul(psB[:, 512 * i:512 * (i + 1)], CT[:, cs],
                         hstb[:, hsl][:, 512 * i:512 * (i + 1)], start=True, stop=True)
                for i in range(2)][-1],
                deps=[bc_ready, h_wm_last] + pv["hstb"] + pv["psB"])
            y2_mms.append(h_y2)
            h_yc = kb.emit("vector", lambda e, ts=ts, hh=hh, hsl=hsl: e.tensor_tensor(
                ysum[:, hsl].rearrange("p (h q) -> p h q", q=64),
                psB[:, :].rearrange("p (h q) -> p h q", q=64),
                ebtok[:, ts][:, 16 * hh:16 * (hh + 1)].unsqueeze(2).broadcast_to([128, 16, 64]),
                ALU.mult), deps=[h_y2, hebt] + pv["ysum"])
            pv["psB"] = [h_yc]
            h_ya = kb.emit("vector", lambda e, hsl=hsl: e.tensor_tensor(
                ysum[:, hsl], ysum[:, hsl], psA[:, hsl], ALU.add), deps=[h_yc, h_y1])
            yc_evs.append(h_ya)
        pv["ysum"] = []
        pv["y2"] = y2_mms
        h_sk1 = kb.emit("vector", lambda e, xb=xb: e.tensor_tensor(
            yscr[:, :].rearrange("p (h q) -> p h q", q=64),
            xb[:, :].rearrange("p (h q) -> p h q", q=64),
            Dbc[:, :].unsqueeze(2).broadcast_to([128, 32, 64]), ALU.mult),
            deps=[h_xl, hg4] + pv["yscr"])
        h_sk2 = kb.emit("vector", lambda e: e.tensor_add(ysum[:, :], ysum[:, :], yscr[:, :]),
                        deps=[h_sk1] + yc_evs)
        h_r1 = kb.emit("scalar", lambda e: e.activation(yscr[:, :], ysum[:, :], AF.Square),
                       deps=[h_sk2])
        h_r2 = kb.emit("vector", lambda e: e.reduce_sum(rstat[:, 0:1], yscr[:, :], AX),
                       deps=[h_r1] + pv["rstat"])
        pv["yscr"] = [h_r2]
        h_r3 = kb.emit("vector", lambda e: e.tensor_scalar(
            rstat[:, 1:2], rstat[:, 0:1], 1.0 / DS, 1e-5, ALU.mult, ALU.add), deps=[h_r2])
        h_r4 = kb.emit("scalar", lambda e: e.sqrt(rstat[:, 2:3], rstat[:, 1:2]), deps=[h_r3])
        h_r5 = kb.emit("vector", lambda e: e.reciprocal(rstat[:, 3:4], rstat[:, 2:3]), deps=[h_r4])
        zb, wz2, sz2 = zld.next()
        h_zl = kb.dma("sync", zb[:, :], z_dram[cs, :], deps=[wz2])
        h_zs = kb.emit("scalar", lambda e, zb=zb: e.activation(zb[:, :], zb[:, :], AF.Silu),
                       deps=[h_zl] + pv["zsil"])
        h_n1 = kb.emit("vector", lambda e: e.tensor_scalar(
            ysum[:, :], ysum[:, :], rstat[:, 3:4], None, ALU.mult), deps=[h_r5, h_sk2])
        pv["rstat"] = [h_n1]
        h_n2 = kb.emit("vector", lambda e: e.tensor_tensor(ysum[:, :], ysum[:, :], gbc[:, :], ALU.mult),
                       deps=[h_n1, hg2])
        h_n3 = kb.emit("vector", lambda e, zb=zb: e.tensor_tensor(yscr[:, :], ysum[:, :], zb[:, :], ALU.mult),
                       deps=[h_n2, h_zs, h_r2] + pv["ssmbf"])
        pv["ysum"] = [h_n3]
        pv["zsil"] = [h_n3]
        zld.used(sz2, h_n3)
        tprev = [pv["psT0"], pv["psT1"]]
        psTs = [psT0, psT1]
        tevs = []
        for kcb in range(16):
            slot = kcb % 2
            pTt = psTs[slot]
            h_tt = kb.emit("tensor", lambda e, kcb=kcb, pTt=pTt: e.transpose(
                pTt[:, :], yscr[:, 128 * kcb:128 * (kcb + 1)], C.ident[:, :]),
                deps=[h_n3, C.ready, tprev[slot]])
            h_te = kb.emit("scalar", lambda e, kcb=kcb, pTt=pTt: e.copy(
                ssmT[:, 128 * kcb:128 * (kcb + 1)], pTt[:, :]),
                deps=[h_tt] + (pv["ssmT"] if kcb < 2 else []))
            tprev[slot] = h_te
            tevs.append(h_te)
        pv["ssmbf"] = [tevs[-1]]
        pv["yscr"] = [tevs[-1]]
        h_o1 = kb.emit("tensor", lambda e: [
            e.matmul(psB[:, 512 * nh:512 * (nh + 1)],
                     ssmT[:, 128 * kc:128 * (kc + 1)],
                     wout_sb[:, 1024 * kc + 512 * nh:1024 * kc + 512 * (nh + 1)],
                     start=(kc == 0), stop=(kc == 15))
            for nh in range(2) for kc in range(16)][-1],
            deps=tevs + lwo + pv["psB"])
        h_oe = kb.emit("scalar", lambda e: e.copy(sbf[:, :], psB[:, :]),
                       deps=[h_o1] + pv["sbf"])
        h_sd = kb.dma("sync", s_dram[cs, :], sbf[:, :], deps=[h_oe])
        pv["ssmT"] = [h_o1]
        pv["sbf"] = [h_sd]
        pv["psB"] = [h_oe]
        h_bt = kb.emit("tensor", lambda e, cs=cs: e.transpose(
            psT0[:, :], BT[:, cs], C.ident[:, :]),
            deps=[bc_ready, tprev[0]])
        h_be = kb.emit("scalar", lambda e: e.copy(btok[:, :], psT0[:, :]),
                       deps=[h_bt] + pv["btok"])
        pv["psT0"] = h_be
        pv["psT1"] = tprev[1]
        h_bw = kb.emit("vector", lambda e, ts=ts: e.tensor_tensor(
            bwa[:, :].rearrange("p (h n) -> p h n", n=128),
            btok[:, :].unsqueeze(1).broadcast_to([128, 32, 128]),
            w2fall[:, ts].unsqueeze(2).broadcast_to([128, 32, 128]), ALU.mult),
            deps=[h_be, hw2b] + pv["bwa"])
        h_dh = kb.emit("tensor", lambda e, xb=xb: [
            e.matmul(psA[:, 64 * h:64 * (h + 1)], bwa[:, 128 * h:128 * (h + 1)],
                     xb[:, 64 * h:64 * (h + 1)], start=True, stop=True)
            for h in range(NH)][-1],
            deps=[h_bw, h_xl] + yc_evs)
        xld.used(sxl, [h_dh, h_sk1, h_y1])
        pv["bwa"] = [h_dh]
        pv["btok"] = [h_bw]
        h_dc = kb.emit("vector", lambda e, ts=ts: e.tensor_tensor(
            hstf[:, :].rearrange("p (h q) -> p h q", q=64),
            hstf[:, :].rearrange("p (h q) -> p h q", q=64),
            lamall[:, ts].unsqueeze(2).broadcast_to([128, 32, 64]), ALU.mult),
            deps=[hlam] + pv["hstf"])
        h_da = kb.emit("vector", lambda e: e.tensor_add(hstf[:, :], hstf[:, :], psA[:, :]),
                       deps=[h_dc, h_dh])
        h_cb = kb.emit("vector", lambda e: e.tensor_copy(hstb[:, :], hstf[:, :]),
                       deps=[h_da] + y2_mms)
        pv["hstf"] = [h_cb]
        pv["hstb"] = [h_cb]
        pv["psA"] = [h_da]
        pv["atbf"] = [wat_evs[-1]]
        pv["watbf"] = [h_y1]
        pv["rhs"] = [last_wq]
        if bg:
            bg.pump(2)
    pr.pop()
    return None


def emit_mamba(pr, C, xnT, xnT_dep, s_dram, z_dram, x_dram, tag="mb", bg=None):
    ctx = mamba_ctx(pr, tag)
    emit_mamba_ab(pr, C, xnT, xnT_dep, z_dram, x_dram, ctx, tag, bg=bg)
    emit_mamba_c(pr, C, s_dram, z_dram, x_dram, ctx, tag, bg=bg)




def emit_gla(pr, C, xnT, xnT_dep, W, dims, s_dram, tag):
    kb = pr
    DK, DV, NHD = dims["DK"], dims["DV"], dims["HEADS"]
    HDK, HDV = DK // NHD, DV // NHD
    OWC = DK // 128
    OWH = HDK // 128
    VC = DV // 128
    scale = float(HDK) ** -0.5
    gnorm = dims["GATE_NORM"]
    CH = 128
    TB = 512
    NTB = T // TB
    CPB = TB // CH

    t1T = pr.sbuf(f"{tag}_t1T", [16, T], BF16)
    pr.push()
    wg1 = pr.sbuf(f"{tag}_wg1", [128, 8 * 16], BF16)
    lg1 = kb.dma("gpsimd", wg1[:, :].rearrange("p (k c) -> p k c", c=16),
                 W["Wgk1"].rearrange("(k p) c -> p k c", p=128))
    pg1 = pr.psum(f"{tag}_pg1", [16, 512], F32)
    prev_t1 = []
    for tq in range(4):
        hm = kb.emit("tensor", lambda e, tq=tq: [
            e.matmul(pg1[:, :], wg1[:, 16 * k:16 * (k + 1)],
                     xnT[:, T * k + 512 * tq:T * k + 512 * (tq + 1)],
                     start=(k == 0), stop=(k == 7)) for k in range(8)][-1],
            deps=[lg1, xnT_dep] + prev_t1)
        ht = kb.emit("vector", lambda e, tq=tq: e.tensor_copy(
            t1T[:, 512 * tq:512 * (tq + 1)], pg1[:, :]), deps=[hm])
        prev_t1 = [ht]
    t1_ready = prev_t1[0]
    pr.pop()
    pr.push()
    qT = pr.sbuf(f"{tag}_qT", [128, OWC * TB], BF16)
    kT = pr.sbuf(f"{tag}_kT", [128, OWC * TB], BF16)
    wqk = sbuf_ring(pr, f"{tag}_wqk", 2, [128, 8 * 128], BF16)
    wg2 = pr.sbuf(f"{tag}_wg2", [16, DK], BF16)
    lg2 = kb.dma("gpsimd", wg2[:, :], W["Wgk2"])
    bg2 = pr.sbuf(f"{tag}_bg2", [128, OWC], F32)
    lbg = kb.dma("sync", bg2[:, :], W["bgk2"].rearrange("(c p) -> p c", p=128),
                 allow_slow_non_contiguous=True)
    onesf = pr.sbuf(f"{tag}_onesf", [1, 128], F32)
    honf = kb.emit("vector", lambda e: e.memset(onesf[:, :], 1.0))
    ones_c = pr.sbuf(f"{tag}_onc", [128, 1], F32)
    honc = kb.emit("vector", lambda e: e.memset(ones_c[:, :], 1.0))
    wo_sb = pr.sbuf(f"{tag}_wo", [128, VC * 1024], BF16)
    lwo = [kb.dma("gpsimd", wo_sb[:, 1024 * vc:1024 * (vc + 1)],
                  W["Wo"][128 * vc:128 * (vc + 1), :]) for vc in range(VC)]
    rw_row = pr.sbuf(f"{tag}_rwrow", [1, HDV], F32)
    lrw = kb.dma("sync", rw_row[:, :], W["rms_w"].unsqueeze(0))
    rw_bc = pr.sbuf(f"{tag}_rwbc", [128, HDV], BF16)
    pPr = psum_ring(pr, f"{tag}_pPr", 2, [128, 512], F32)
    rw_evs = []
    for i in range((HDV + 511) // 512):
        w_ = min(512, HDV - 512 * i)
        pb_, wpp, spp = pPr.next()
        h1_ = kb.emit("tensor", lambda e, i=i, w_=w_, pb_=pb_: e.matmul(
            pb_[:, 0:w_], onesf[:, :], rw_row[:, 512 * i:512 * i + w_],
            start=True, stop=True), deps=[lrw, honf] + wpp)
        h2_ = kb.emit("vector", lambda e, i=i, w_=w_, pb_=pb_: e.tensor_copy(
            rw_bc[:, 512 * i:512 * i + w_], pb_[:, 0:w_]), deps=[h1_])
        pPr.used(spp, h2_)
        rw_evs.append(h2_)
    rw_ready = rw_evs[-1]

    S_sb = pr.sbuf(f"{tag}_S", [128, OWC * HDV], F32)
    S_bf = pr.sbuf(f"{tag}_Sbf", [128, OWC * HDV], BF16)
    hS0 = kb.emit("vector", lambda e: e.memset(S_sb[:, :], 0.0))
    hS0b = kb.emit("vector", lambda e: e.memset(S_bf[:, :], 0.0))

    vtb = pr.sbuf(f"{tag}_vtb", [128, 4 * DV], BF16)
    gtb = pr.sbuf(f"{tag}_gtb", [128, 4 * DV], BF16)
    gkr = pr.sbuf(f"{tag}_gkr", [128, OWC * TB], F32)
    wv = sbuf_ring(pr, f"{tag}_wv", 2, [128, 8 * 256], BF16)
    qgr = sbuf_ring(pr, f"{tag}_qg", 2, [128, OWC * 128], BF16)
    kdr = sbuf_ring(pr, f"{tag}_kd", 1, [128, OWC * 128], BF16)
    kgr = sbuf_ring(pr, f"{tag}_kg", 1, [128, OWC * 128], BF16)
    efr = sbuf_ring(pr, f"{tag}_efac", 2, [128, max(OWC * 128, TB)], F32)
    lmr = sbuf_ring(pr, f"{tag}_lamc", 2, [128, OWC], F32)
    atr = sbuf_ring(pr, f"{tag}_atm", 1, [128, 128 * NHD], BF16)
    oger = sbuf_ring(pr, f"{tag}_oge", 1, [128, DV], BF16)
    ogTr = sbuf_ring(pr, f"{tag}_ogT", 1, [128, VC * 128], BF16)
    kgtr = sbuf_ring(pr, f"{tag}_kgt", 2, [128, OWC * 128], BF16)
    sstat = pr.sbuf(f"{tag}_sstat", [128, 6 * NHD], F32)
    sagg = pr.sbuf(f"{tag}_sagg", [128, 2 * NHD], F32)
    srst = pr.sbuf(f"{tag}_srst", [128, 2 * NHD], F32)
    sbf = pr.sbuf(f"{tag}_sbf", [128, 1024], BF16)
    pA = pr.psum(f"{tag}_pA", [128, 128], F32)
    pOr = psum_ring(pr, f"{tag}_pO", 2, [128, HDV], F32)
    pW = pr.psum(f"{tag}_pW", [128, 512], F32)
    pT0 = pr.psum(f"{tag}_pT0", [128, 128], BF16)
    pT1 = pr.psum(f"{tag}_pT1", [128, 128], BF16)

    pv = {"vtb": [], "gtb": [], "gkr": [], "qkg": [], "efac": [], "lamc": [],
          "qkbuf": [], "vhalf": [],
          "atm": [], "oge": [], "osq": [], "ogb": [], "ogT": [], "kgt": [],
          "sstat": [], "sagg": [], "srst": [], "sbf": [], "pA": [],
          "pW": [], "pT0": None, "pT1": None, "S": [hS0], "Sbf": [hS0b]}

    for tb in range(NTB):
        t0 = TB * tb
        tqi = tb
        qk_evs = []
        for (dst, wname, key) in ((qT, "Wq", "qT"), (kT, "Wk", "kT")):
            for oc in range(OWC):
                wb, wwar, ws = wqk.next()
                lw = kb.dma("gpsimd", wb[:, :].rearrange("p (k c) -> p k c", c=128),
                            W[wname][:, 128 * oc:128 * (oc + 1)]
                            .rearrange("(k p) c -> p k c", p=128), deps=[wwar])
                pb_, wpp, spp = pPr.next()
                hm = kb.emit("tensor", lambda e, wb=wb, tqi=tqi, pb_=pb_: [
                    e.matmul(pb_[:, :], wb[:, 128 * k:128 * (k + 1)],
                             xnT[:, T * k + 512 * tqi:T * k + 512 * (tqi + 1)],
                             start=(k == 0), stop=(k == 7)) for k in range(8)][-1],
                    deps=[lw, xnT_dep] + wpp)
                he = kb.emit("scalar", lambda e, dst=dst, oc=oc, pb_=pb_: e.copy(
                    dst[:, TB * oc:TB * (oc + 1)], pb_[:, :]),
                    deps=[hm] + (pv["qkbuf"] if oc == 0 and dst is qT else []))
                wqk.used(ws, hm)
                pPr.used(spp, he)
                qk_evs.append(he)
        qk_done = qk_evs[-1]
        vread, gread = [], []
        for (dst, wname, rdkey) in ((vtb, "Wv", vread), (gtb, "Wg", gread)):
            evs = []
            for ncol in range(DV // 256):
                wb, wwar, ws = wv.next()
                lw = kb.dma("gpsimd", wb[:, :].rearrange("p (k c) -> p k c", c=256),
                            W[wname][:, 256 * ncol:256 * (ncol + 1)]
                            .rearrange("(k p) c -> p k c", p=128), deps=[wwar])
                hm_all = []
                for tt in range(4):
                    pb_, wpp, spp = pPr.next()
                    hm = kb.emit("tensor", lambda e, wb=wb, tt=tt, t0=t0, pb_=pb_: [
                        e.matmul(pb_[:, 0:256],
                                 xnT[:, T * k + t0 + 128 * tt:T * k + t0 + 128 * (tt + 1)],
                                 wb[:, 256 * k:256 * (k + 1)], start=(k == 0), stop=(k == 7))
                        for k in range(8)][-1],
                        deps=[lw, xnT_dep] + wpp)
                    he = kb.emit("scalar", lambda e, dst=dst, tt=tt, ncol=ncol, pb_=pb_: e.copy(
                        dst[:, DV * tt + 256 * ncol:DV * tt + 256 * (ncol + 1)], pb_[:, 0:256]),
                        deps=[hm] + (pv["vtb" if dst is vtb else "gtb"]
                                     if tt == 0 and ncol == 0 else []))
                    pPr.used(spp, he)
                    hm_all.append(hm)
                    evs.append(he)
                wv.used(ws, hm_all)
            if dst is vtb:
                v_ready = evs[-1]
            else:
                g_ready = evs[-1]
        hgs0 = kb.emit("scalar", lambda e: e.activation(gtb[:, :], gtb[:, :], AF.Silu),
                       deps=[g_ready])
        hgs = kb.emit("vector", lambda e: e.tensor_tensor(
            gtb[:, :].rearrange("p (b v) -> p b v", v=HDV),
            gtb[:, :].rearrange("p (b v) -> p b v", v=HDV),
            rw_bc[:, 0:HDV].unsqueeze(1).broadcast_to([128, 4 * NHD, HDV]), ALU.mult),
            deps=[hgs0, rw_ready])
        pv["gtb"] = []
        pv["vtb"] = []
        gk_evs = []
        for oc in range(OWC):
            pb_, wpp, spp = pPr.next()
            hm = kb.emit("tensor", lambda e, oc=oc, t0=t0, pb_=pb_: e.matmul(
                pb_[:, :], wg2[0:16, 128 * oc:128 * (oc + 1)],
                t1T[:, t0:t0 + TB], start=True, stop=True),
                deps=[lg2, t1_ready] + wpp)
            he = kb.emit("vector", lambda e, oc=oc, pb_=pb_: e.tensor_scalar(
                gkr[:, TB * oc:TB * (oc + 1)], pb_[:, :], bg2[:, oc:oc + 1], None,
                ALU.add), deps=[hm, lbg] + (pv["gkr"] if oc == 0 else []))
            pPr.used(spp, he)
            gk_evs.append(he)
        lg_evs = []
        for oc in range(OWC):
            sl = slice(TB * oc, TB * (oc + 1))
            hA = kb.emit("scalar", lambda e, sl=sl: e.activation(
                efac[:, 0:TB], gkr[:, sl], AF.Abs),
                deps=[gk_evs[oc]] + lg_evs[-1:] + pv["efac"])
            hB = kb.emit("scalar", lambda e: e.activation(
                efac[:, 0:TB], efac[:, 0:TB], AF.Exp, scale=-1.0), deps=[hA])
            hC = kb.emit("vector", lambda e: e.tensor_scalar_add(
                efac[:, 0:TB], efac[:, 0:TB], 1.0), deps=[hB])
            hD_ = kb.emit("scalar", lambda e: e.activation(
                efac[:, 0:TB], efac[:, 0:TB], AF.Ln), deps=[hC])
            hE = kb.emit("vector", lambda e, sl=sl: e.tensor_scalar_min(
                gkr[:, sl], gkr[:, sl], 0.0), deps=[hA])
            hF = kb.emit("vector", lambda e, sl=sl: e.tensor_sub(
                gkr[:, sl], gkr[:, sl], efac[:, 0:TB]), deps=[hD_, hE])
            hG = kb.emit("vector", lambda e, sl=sl: e.tensor_scalar_mul(
                gkr[:, sl], gkr[:, sl], 1.0 / gnorm), deps=[hF])
            lg_evs.append(hG)
        pv["efac"] = [lg_evs[-1]]
        sc_evs = []
        for oc in range(OWC):
            for ci in range(CPB):
                sl = slice(TB * oc + CH * ci, TB * oc + CH * (ci + 1))
                hSc = kb.emit("vector", lambda e, sl=sl: e.tensor_tensor_scan(
                    gkr[:, sl], ones_c[:, :].broadcast_to([128, CH]), gkr[:, sl],
                    0.0, ALU.mult, ALU.add), deps=[lg_evs[oc], honc])
                sc_evs.append(hSc)
        bc_ready = sc_evs[-1]
        pv["gkr"] = []

        for ci in range(CPB):
            c = CPB * tb + ci
            tt = ci
            qg, wqg, sqg = qgr.next()
            kd, wkd, skd = kdr.next()
            kg, wkg, skg = kgr.next()
            efac, wef, sef = efr.next()
            lamc, wlm, slm = lmr.next()
            atm, wat, sat = atr.next()
            oge, wog, sog = oger.next()
            osq = oge
            ogT, wot, sot = ogTr.next()
            kgt, wkt, skt = kgtr.next()
            gk3 = gkr[:, :].rearrange("p (o t) -> p o t", o=OWC)[:, :, CH * ci:CH * (ci + 1)]
            qT3 = qT[:, :].rearrange("p (o t) -> p o t", o=OWC)[:, :, CH * ci:CH * (ci + 1)]
            kT3 = kT[:, :].rearrange("p (o t) -> p o t", o=OWC)[:, :, CH * ci:CH * (ci + 1)]
            ef3 = efac[:, 0:OWC * CH].rearrange("p (o t) -> p o t", t=CH)
            qg3 = qg[:, :].rearrange("p (o t) -> p o t", t=CH)
            kd3 = kd[:, :].rearrange("p (o t) -> p o t", t=CH)
            kg3 = kg[:, :].rearrange("p (o t) -> p o t", t=CH)
            h1 = kb.emit("scalar", lambda e, gk3=gk3, ef3=ef3: e.activation(
                ef3, gk3, AF.Exp), deps=[bc_ready] + wef)
            h2 = kb.emit("vector", lambda e, ef3=ef3, qT3=qT3, qg3=qg3: e.scalar_tensor_tensor(
                qg3, ef3, scale, qT3, ALU.mult, ALU.mult), deps=[h1, qk_done] + wqg)
            h3 = kb.emit("scalar", lambda e, gk3=gk3, ef3=ef3: e.activation(
                ef3, gk3, AF.Exp, scale=-1.0), deps=[h2])
            h4 = kb.emit("vector", lambda e, ef3=ef3, kT3=kT3, kd3=kd3: e.tensor_tensor(
                kd3, ef3, kT3, ALU.mult), deps=[h3, qk_done] + wkd)
            h5s = []
            for oc in range(OWC):
                sl = slice(TB * oc + CH * ci, TB * oc + CH * (ci + 1))
                lastc = TB * oc + CH * ci + CH - 1
                h5 = kb.emit("scalar", lambda e, sl=sl, lastc=lastc, oc=oc, efac=efac: e.activation(
                    efac[:, CH * oc:CH * (oc + 1)], gkr[:, sl], AF.Exp, scale=-1.0,
                    bias=gkr[:, lastc:lastc + 1]), deps=[h4])
                h5s.append(h5)
            h6 = kb.emit("vector", lambda e, kT3=kT3, kg3=kg3, efac=efac: e.tensor_tensor(
                kg3, efac[:, 0:OWC * CH].rearrange("p (o t) -> p o t", t=CH),
                kT3, ALU.mult), deps=h5s + [qk_done] + wkg)
            hlam = kb.emit("scalar", lambda e, ci=ci, lamc=lamc: e.activation(
                lamc[:, :], gkr[:, CH * ci + CH - 1:CH * ci + CH + TB * (OWC - 1):TB] if OWC > 1
                else gkr[:, CH * ci + CH - 1:CH * ci + CH], AF.Exp),
                deps=[bc_ready] + wlm)
            och = []
            for h in range(NHD):
                hA1 = kb.emit("tensor", lambda e, h=h, qg=qg, kd=kd: [
                    e.matmul(pA[:, :], kd[:, CH * (OWH * h + j):CH * (OWH * h + j + 1)],
                             qg[:, CH * (OWH * h + j):CH * (OWH * h + j + 1)],
                             start=(j == 0), stop=(j == OWH - 1)) for j in range(OWH)][-1],
                    deps=[h2, h4] + pv["pA"])
                hA2 = kb.emit("vector", lambda e, h=h, atm=atm: e.tensor_tensor(
                    atm[:, CH * h:CH * (h + 1)], pA[:, :], C.u128[:, :], ALU.mult),
                    deps=[hA1, C.ready] + (wat if h == 0 else []))
                pv["pA"] = [hA2]
                och.append(hA2)
            hO1s = []
            ev_evs = []
            st_evs = []
            for h in range(NHD):
                pO, wpo, spo = pOr.next()

                def omm(e, h=h, tt=tt, pO=pO, atm=atm, qg=qg):
                    i1 = e.matmul(pO[:, :], atm[:, CH * h:CH * (h + 1)],
                                  vtb[:, DV * tt + HDV * h:DV * tt + HDV * (h + 1)],
                                  start=True, stop=False)
                    for j in range(OWH):
                        oc = OWH * h + j
                        i1 = e.matmul(pO[:, :], qg[:, CH * oc:CH * (oc + 1)],
                                      S_bf[:, HDV * oc:HDV * (oc + 1)],
                                      start=False, stop=(j == OWH - 1))
                    return i1
                hO1 = kb.emit("tensor", omm, deps=[och[h], h2, v_ready] + pv["Sbf"] + wpo)
                hO1s.append(hO1)
                hOe = kb.emit("scalar", lambda e, h=h, pO=pO, oge=oge: e.copy(
                    oge[:, HDV * h:HDV * (h + 1)], pO[:, :]),
                    deps=[hO1] + (wog if h == 0 else []))
                ev_evs.append(hOe)
                hst = kb.emit("vector", lambda e, h=h, oge=oge: e.bn_stats(
                    sstat[:, 6 * h:6 * (h + 1)], oge[:, HDV * h:HDV * (h + 1)]),
                    deps=[hOe] + (pv["sstat"] if h == 0 else []))
                hag = kb.emit("vector", lambda e, h=h: e.bn_aggr(
                    sagg[:, 2 * h:2 * (h + 1)], sstat[:, 6 * h:6 * (h + 1)]),
                    deps=[hst] + (pv["sagg"] if h == 0 else []))
                st_evs.append(hag)
                pOr.used(spo, [hOe])
            hr1 = kb.emit("vector", lambda e: e.tensor_tensor(
                srst[:, 0:NHD], sagg[:, 0:2 * NHD:2], sagg[:, 0:2 * NHD:2], ALU.mult),
                deps=st_evs + pv["srst"])
            hr2 = kb.emit("vector", lambda e: e.tensor_tensor(
                srst[:, NHD:2 * NHD], srst[:, 0:NHD], sagg[:, 1:2 * NHD:2], ALU.add),
                deps=[hr1] + st_evs)
            hr3 = kb.emit("vector", lambda e: e.tensor_scalar_add(
                srst[:, NHD:2 * NHD], srst[:, NHD:2 * NHD], 1e-5), deps=[hr2])
            hr4 = kb.emit("scalar", lambda e: e.activation(
                srst[:, NHD:2 * NHD], srst[:, NHD:2 * NHD], AF.Ln), deps=[hr3])
            hr5 = kb.emit("scalar", lambda e: e.activation(
                srst[:, NHD:2 * NHD], srst[:, NHD:2 * NHD], AF.Exp, scale=-0.5), deps=[hr4])
            pv["sstat"] = [hag]
            pv["sagg"] = [hr2]
            pv["srst"] = [hr5]
            hN = kb.emit("vector", lambda e, osq=osq, oge=oge: e.tensor_tensor(
                osq[:, :].rearrange("p (h v) -> p h v", v=HDV),
                oge[:, :].rearrange("p (h v) -> p h v", v=HDV),
                srst[:, NHD:2 * NHD].unsqueeze(2).broadcast_to([128, NHD, HDV]), ALU.mult),
                deps=[hr5] + ev_evs)
            hG1 = kb.emit("vector", lambda e, tt=tt, osq=osq: e.tensor_tensor(
                osq[:, :], osq[:, :], gtb[:, DV * tt:DV * (tt + 1)],
                ALU.mult), deps=[hN, hgs])
            tprev = [pv["pT0"], pv["pT1"]]
            tevs = []
            for vcb in range(VC):
                slot = vcb % 2
                pTt = pT0 if slot == 0 else pT1
                hT1 = kb.emit("tensor", lambda e, vcb=vcb, pTt=pTt, osq=osq: e.transpose(
                    pTt[:, :], osq[:, 128 * vcb:128 * (vcb + 1)], C.ident[:, :]),
                    deps=[hG1, C.ready, tprev[slot]])
                hT2 = kb.emit("scalar", lambda e, vcb=vcb, pTt=pTt, ogT=ogT: e.copy(
                    ogT[:, 128 * vcb:128 * (vcb + 1)], pTt[:, :]),
                    deps=[hT1] + (wot if vcb < 2 else []))
                tprev[slot] = hT2
                tevs.append(hT2)
            pv["pT0"], pv["pT1"] = tprev
            oger.used(sog, [tevs[-1], hG1])
            hW = kb.emit("tensor", lambda e, ogT=ogT: [
                e.matmul(pW[:, :], ogT[:, 128 * vcb:128 * (vcb + 1)],
                         wo_sb[:, 1024 * vcb:1024 * vcb + 512],
                         start=(vcb == 0), stop=(vcb == VC - 1))
                for vcb in range(VC)][-1],
                deps=tevs + lwo + pv["pW"])
            hWe = kb.emit("scalar", lambda e: e.copy(sbf[:, 0:512], pW[:, :]),
                          deps=[hW] + pv["sbf"])
            hWb = kb.emit("tensor", lambda e, ogT=ogT: [
                e.matmul(pW[:, :], ogT[:, 128 * vcb:128 * (vcb + 1)],
                         wo_sb[:, 1024 * vcb + 512:1024 * vcb + 1024],
                         start=(vcb == 0), stop=(vcb == VC - 1))
                for vcb in range(VC)][-1],
                deps=tevs + lwo + [hWe])
            hWe2 = kb.emit("scalar", lambda e: e.copy(sbf[:, 512:1024], pW[:, :]),
                           deps=[hWb])
            hWd = kb.dma("sync", s_dram[CH * c:CH * (c + 1), :], sbf[:, :], deps=[hWe, hWe2])
            ogTr.used(sot, hWb)
            pv["sbf"] = [hWd]
            pv["pW"] = [hWe2]
            ktevs = []
            tprev2 = [pv["pT0"], pv["pT1"]]
            for oc in range(OWC):
                slot = oc % 2
                pTt = pT0 if slot == 0 else pT1
                hK1 = kb.emit("tensor", lambda e, oc=oc, pTt=pTt, kg=kg: e.transpose(
                    pTt[:, :], kg[:, CH * oc:CH * (oc + 1)], C.ident[:, :]),
                    deps=[h6, C.ready, tprev2[slot]])
                hK2 = kb.emit("scalar", lambda e, oc=oc, pTt=pTt, kgt=kgt: e.copy(
                    kgt[:, 128 * oc:128 * (oc + 1)], pTt[:, :]),
                    deps=[hK1] + (wkt if oc < 2 else []))
                tprev2[slot] = hK2
                ktevs.append(hK2)
            pv["pT0"], pv["pT1"] = tprev2
            hSd = kb.emit("vector", lambda e, lamc=lamc: e.tensor_tensor(
                S_sb[:, :].rearrange("p (o v) -> p o v", o=OWC),
                S_sb[:, :].rearrange("p (o v) -> p o v", o=OWC),
                lamc[:, :].unsqueeze(2).broadcast_to([128, OWC, HDV]), ALU.mult),
                deps=[hlam] + pv["S"])
            lmr.used(slm, hSd)
            sdec = [hSd]
            hD3s = []
            for oc in range(OWC):
                h = oc // OWH
                pb_, wpp, spp = pPr.next()
                hD3 = kb.emit("tensor", lambda e, oc=oc, tt=tt, h=h, kgt=kgt, pb_=pb_: e.matmul(
                    pb_[:, 0:HDV], kgt[:, 128 * oc:128 * (oc + 1)],
                    vtb[:, DV * tt + HDV * h:DV * tt + HDV * (h + 1)],
                    start=True, stop=True),
                    deps=[ktevs[oc], v_ready] + wpp)
                hD3s.append(hD3)
                hD4 = kb.emit("vector", lambda e, oc=oc, pb_=pb_: e.tensor_add(
                    S_sb[:, HDV * oc:HDV * (oc + 1)], S_sb[:, HDV * oc:HDV * (oc + 1)],
                    pb_[:, 0:HDV]), deps=[hSd, hD3])
                pPr.used(spp, hD4)
                sdec.append(hD4)
            h_cbS = kb.emit("vector", lambda e: e.tensor_copy(S_bf[:, :], S_sb[:, :]),
                            deps=sdec + hO1s[-1:])
            pv["S"] = [h_cbS]
            pv["Sbf"] = [h_cbS]
            kgtr.used(skt, hD3s)
            qgr.used(sqg, [hD3s[-1], h2, hO1s[-1]])
            kdr.used(skd, [hA1, h4])
            kgr.used(skg, [hD3s[-1], h6])
            efr.used(sef, [h2, h4, h6])
            atr.used(sat, hO1s)
        pv["vtb"] = [hD3s[-1], hO1s[-1]]
        pv["gtb"] = [hG1]
        pv["qkbuf"] = [hD3s[-1], h6, h2, h4]
    pr.pop()
    return None



def emit_quantize(pr, C, w_dram, KC, HC, wq_sb, tag):
    kb = pr
    CW = min(HC, 1024)
    NCW = HC // CW
    pr.push()
    ring = sbuf_ring(pr, f"{tag}_ring", 6, [128, CW], F32)
    gcol = pr.sbuf(f"{tag}_gcol", [128, KC * NCW], F32)
    absr = sbuf_ring(pr, f"{tag}_abs", 2, [128, CW], F32)
    stt = pr.sbuf(f"{tag}_stt", [128, 2], F32)
    assert KC * NCW <= 128
    onec = pr.sbuf(f"{tag}_onec", [128, 1], F32)
    onesf = pr.sbuf(f"{tag}_onesf", [1, 128], F32)
    pg = pr.psum(f"{tag}_pg", [128, 8], F32)
    h1 = kb.emit("vector", lambda e: e.memset(onec[:, :], 1.0))
    h2 = kb.emit("vector", lambda e: e.memset(onesf[:, :], 1.0))
    red_evs = []
    for kc in range(KC):
        for j in range(NCW):
            wb, war, ws = ring.next()
            hl = kb.dma(("sync", "scalar", "gpsimd")[(kc * NCW + j) % 3], wb[:, :],
                        w_dram[128 * kc:128 * (kc + 1), CW * j:CW * (j + 1)], deps=[war])
            hr = kb.emit("vector", lambda e, wb=wb, kc=kc, j=j: e.reduce_sum(
                gcol[:, NCW * kc + j:NCW * kc + j + 1], wb[:, :], AX,
                apply_absolute_value=True), deps=[hl])
            ring.used(ws, hr)
            red_evs.append(hr)
    NKC = KC * NCW
    hm1 = kb.emit("tensor", lambda e: e.matmul(pg[0:NKC, 0:1], gcol[:, :], onec[:, :],
                                               start=True, stop=True),
                  deps=red_evs + [h1])
    hm2 = kb.emit("vector", lambda e: e.tensor_copy(stt[0:NKC, 0:1], pg[0:NKC, 0:1]), deps=[hm1])
    hm3 = kb.emit("tensor", lambda e: e.matmul(pg[0:1, 1:2], stt[0:NKC, 0:1], onec[0:NKC, :],
                                               start=True, stop=True), deps=[hm2])
    hm4 = kb.emit("vector", lambda e: e.tensor_scalar(
        stt[0:1, 1:2], pg[0:1, 1:2], 1.0 / (128 * KC * HC), 1e-6, ALU.mult, ALU.add),
        deps=[hm3])
    hm5 = kb.emit("vector", lambda e: e.tensor_scalar_mul(stt[0:1, 1:2], stt[0:1, 1:2], 0.5), deps=[hm4])
    hm6 = kb.emit("tensor", lambda e: e.matmul(pg[:, 2:3], onesf[:, :], stt[0:1, 1:2],
                                               start=True, stop=True), deps=[hm5, h2])
    thr = pr.sbuf(f"{tag}_thr", [128, 1], F32)
    hm7 = kb.emit("vector", lambda e: e.tensor_copy(thr[:, :], pg[:, 2:3]), deps=[hm6])
    q_evs = []
    for kc in range(KC):
      for j in range(NCW):
        wb, war, ws = ring.next()
        hl = kb.dma(("sync", "scalar", "gpsimd")[(kc * NCW + j) % 3], wb[:, :],
                    w_dram[128 * kc:128 * (kc + 1), CW * j:CW * (j + 1)], deps=[war])
        ab, awar, asl = absr.next()
        ha = kb.emit("scalar", lambda e, wb=wb, ab=ab: e.activation(ab[:, :], wb[:, :], AF.Abs),
                     deps=[hl] + awar)
        hg = kb.emit("vector", lambda e, ab=ab: e.tensor_scalar(
            ab[:, :], ab[:, :], thr[:, :], None, ALU.is_ge), deps=[ha, hm7])
        hs1 = kb.emit("vector", lambda e, wb=wb: e.tensor_single_scalar(
            wb[:, :], wb[:, :], 0.0, ALU.is_ge), deps=[hl, ha])
        hs2 = kb.emit("vector", lambda e, wb=wb: e.tensor_scalar(
            wb[:, :], wb[:, :], 2.0, -1.0, ALU.mult, ALU.add), deps=[hs1])
        hq = kb.emit("vector", lambda e, wb=wb, ab=ab, kc=kc, j=j: e.tensor_tensor(
            wq_sb[:, HC * kc + CW * j:HC * kc + CW * (j + 1)], ab[:, :], wb[:, :],
            ALU.mult), deps=[hg, hs2])
        ring.used(ws, [hq])
        absr.used(asl, hq)
        q_evs.append(hq)
    pr.pop()
    return q_evs[-1]


class QBG:

    CW = 512
    CW2 = 256

    def __init__(self, pr):
        self.pr, self.kb = pr, pr.kb
        kb = pr.kb
        wfc = pr.din("L1_ffn_W_fcT", [1024, 4096])
        wproj = pr.din("L1_ffn_W_projT", [4096, 1024])
        self.wq_dram = pr.dint("qbg_wq", [8, 128, 4096], BF16)
        self.wp_dram = pr.dint("qbg_wp", [32, 128, 1024], BF16)
        self.init_dep = []
        self.ring = self.outr = self.absr = None
        CW = self.CW
        self.gcol = pr.sbuf("qbg_gcol", [128, 128], F32)
        self.thr = pr.sbuf("qbg_thr", [128, 2], F32)
        self.onec = pr.sbuf("qbg_onec", [128, 1], F32)
        self.onesf = pr.sbuf("qbg_onesf", [1, 128], F32)
        self.misc = pr.sbuf("qbg_misc", [128, 4], F32)
        self.h_ones = [kb.emit("vector", lambda e: e.memset(self.onec[:, :], 1.0)),
                       kb.emit("vector", lambda e: e.memset(self.onesf[:, :], 1.0))]
        items = []
        for kc in range(8):
            for j in range(8):
                items.append((0, wfc[128 * kc:128 * (kc + 1), CW * j:CW * (j + 1)],
                              kc * 8 + j, None, None, CW))
        for kc in range(32):
            for j in range(2):
                items.append((1, wproj[128 * kc:128 * (kc + 1), CW * j:CW * (j + 1)],
                              64 + kc * 2 + j, None, None, CW))
        C2 = self.CW2
        for kc in range(8):
            for j in range(16):
                items.append((0, wfc[128 * kc:128 * (kc + 1), C2 * j:C2 * (j + 1)],
                              None, self.wq_dram[kc, :, C2 * j:C2 * (j + 1)],
                              (0, kc, j), C2))
        for kc in range(32):
            for j in range(4):
                items.append((1, wproj[128 * kc:128 * (kc + 1), C2 * j:C2 * (j + 1)],
                              None, self.wp_dram[kc, :, C2 * j:C2 * (j + 1)],
                              (1, kc, j), C2))
        self.items = items
        self.gate = 128
        self.di_cap = 128
        self.ci = 0
        self.di = 0
        self.pend = {}
        self.p1_evs = [[], []]
        self.out_h = {}
        self.thresh_done = None

    def open_arena(self, p2=False):
        pr = self.pr
        self.init_dep = list(pr.phase_dep)
        w = self.CW2 if p2 else self.CW
        self.LEAD = 6 if p2 else 4
        self.ring = sbuf_ring(pr, "qbg_in", 8 if p2 else 6, [128, w], F32)
        if p2:
            self.outr = sbuf_ring(pr, "qbg_out", 1, [128, w], BF16)
            self.absr = sbuf_ring(pr, "qbg_abs", 1, [128, w], F32)
            self.di_cap = len(self.items)

    def _issue_dma(self):
        m, src, gi, dst, key, w = self.items[self.di]
        wb, war, slot = self.ring.next()
        hl = self.kb.dma("sync", wb[:, 0:w], src, deps=war + self.init_dep)
        self.pend[self.di] = (wb, slot, hl)
        self.di += 1

    def _emit_compute(self, i):
        kb = self.kb
        m, src, gi, dst, key, w = self.items[i]
        wb, slot, hl = self.pend.pop(i)
        if dst is None:
            hr = kb.emit("vector", lambda e, wb=wb, gi=gi, w=w: e.reduce_sum(
                self.gcol[:, gi:gi + 1], wb[:, 0:w], AX,
                apply_absolute_value=True), deps=[hl])
            self.ring.used(slot, hr)
            self.p1_evs[m].append(hr)
            return
        ab, awar, aslot = self.absr.next()
        ha = kb.emit("scalar", lambda e, wb=wb, ab=ab, w=w: e.activation(
            ab[:, 0:w], wb[:, 0:w], AF.Abs), deps=[hl] + awar)
        hg = kb.emit("vector", lambda e, ab=ab, m=m, w=w: e.tensor_scalar(
            ab[:, 0:w], ab[:, 0:w], self.thr[:, m:m + 1], None, ALU.is_ge),
            deps=[ha] + self.thresh_done)
        hs1 = kb.emit("vector", lambda e, wb=wb, w=w: e.tensor_single_scalar(
            wb[:, 0:w], wb[:, 0:w], 0.0, ALU.is_ge), deps=[hl, ha])
        hs2 = kb.emit("vector", lambda e, wb=wb, w=w: e.tensor_scalar(
            wb[:, 0:w], wb[:, 0:w], 2.0, -1.0, ALU.mult, ALU.add), deps=[hs1])
        ob, owar, oslot = self.outr.next()
        hq = kb.emit("vector", lambda e, ob=ob, ab=ab, wb=wb, w=w: e.tensor_tensor(
            ob[:, 0:w], ab[:, 0:w], wb[:, 0:w], ALU.mult), deps=[hg, hs2] + owar)
        self.ring.used(slot, hq)
        self.absr.used(aslot, hq)
        hw = kb.dma("sync", dst, ob[:, 0:w], deps=[hq])
        self.outr.used(oslot, hw)
        self.out_h[key] = hw

    def pump(self, n=2):
        if self.ring is None:
            return
        for _ in range(n):
            while self.di < self.di_cap and self.di < self.ci + self.LEAD:
                self._issue_dma()
            if self.ci >= len(self.items):
                return
            if self.ci >= self.gate and (self.thresh_done is None
                                         or self.absr is None):
                return
            self._emit_compute(self.ci)
            self.ci += 1
        while self.di < self.di_cap and self.di < self.ci + self.LEAD:
            self._issue_dma()

    def drain_p1(self):
        while self.ci < self.gate:
            self.pump(8)

    def emit_thresh(self):
        kb, pr = self.kb, self.pr
        pg = pr.psum("qbg_pg", [128, 8], F32)
        phase = list(pr.phase_dep)
        N = 1024 * 4096
        hs = []
        for m in (0, 1):
            h1 = kb.emit("vector", lambda e, m=m: e.reduce_sum(
                self.misc[:, m:m + 1], self.gcol[:, 64 * m:64 * m + 64], AX),
                deps=self.p1_evs[m])
            h2 = kb.emit("tensor", lambda e, m=m: e.matmul(
                pg[0:1, m:m + 1], self.misc[:, m:m + 1], self.onec[:, :],
                start=True, stop=True), deps=[h1] + self.h_ones + phase)
            h3 = kb.emit("vector", lambda e, m=m: e.tensor_scalar(
                self.misc[0:1, 2 + m:3 + m], pg[0:1, m:m + 1],
                0.5 / N, 0.5e-6, ALU.mult, ALU.add), deps=[h2])
            h4 = kb.emit("tensor", lambda e, m=m: e.matmul(
                pg[:, 4 + m:5 + m], self.onesf[:, :], self.misc[0:1, 2 + m:3 + m],
                start=True, stop=True), deps=[h3] + self.h_ones + phase)
            h5 = kb.emit("vector", lambda e, m=m: e.tensor_copy(
                self.thr[:, m:m + 1], pg[:, 4 + m:5 + m]), deps=[h4])
            hs.append(h5)
        self.thresh_done = hs

    def drain(self):
        while True:
            before = (self.ci, self.di)
            self.pump(1000)
            if (self.ci, self.di) == before:
                break
        assert self.ci == len(self.items), "QBG not drained (thresh missing?)"


def emit_bitffn(pr, C, xnT, xnT_dep, W, x_in_ap, x_out_ap, s_dram, tag, qbg=None):
    kb = pr
    H = 4096
    wq_sb = pr.sbuf(f"{tag}_wq", [128, 8 * H], BF16)
    wp_sb = pr.sbuf(f"{tag}_wp", [128, 32 * 1024], BF16)
    if qbg is None:
        hq1 = emit_quantize(pr, C, W["W_fcT"], 8, H, wq_sb, tag + "q1")
        hq2 = emit_quantize(pr, C, W["W_projT"], 32, 1024, wq_sb=wp_sb, tag=tag + "q2")
        wq_ready = [[hq1] for _ in range(8)]
        wp_ready = [hq2]
    else:
        wq_ready = []
        for hc in range(8):
            wq_ready.append([pr.dma(
                "sync", wq_sb[:, H * kc + 512 * hc:H * kc + 512 * (hc + 1)],
                qbg.wq_dram[kc, :, 512 * hc:512 * (hc + 1)],
                deps=[qbg.out_h[(0, kc, 2 * hc)], qbg.out_h[(0, kc, 2 * hc + 1)]])
                for kc in range(8)])
        wp_ready = [pr.dma(
            "sync", wp_sb[:, 1024 * kc2:1024 * (kc2 + 1)], qbg.wp_dram[kc2],
            deps=[qbg.out_h[(1, kc2, j)] for j in range(4)])
            for kc2 in range(32)]
    pr.push()
    lnwc = pr.sbuf(f"{tag}_lnwc", [128, 32], F32)
    lnbc = pr.sbuf(f"{tag}_lnbc", [128, 32], F32)
    hlw = kb.dma("gpsimd", lnwc[:, :], W["ln_w"].rearrange("(c p) -> p c", p=128),
                 allow_slow_non_contiguous=True)
    hlb = kb.dma("gpsimd", lnbc[:, :], W["ln_b"].rearrange("(c p) -> p c", p=128),
                 allow_slow_non_contiguous=True)
    ln_ready = [hlw, hlb]
    hbr = sbuf_ring(pr, f"{tag}_hbuf", 2, [128, H], BF16)
    lnr = sbuf_ring(pr, f"{tag}_lnhT", 2, [128, 32 * 128], BF16)
    stat = pr.sbuf(f"{tag}_stat", [128, 8], F32)
    ph = psum_ring(pr, f"{tag}_ph", 2, [128, 512], F32)
    por = psum_ring(pr, f"{tag}_po", 2, [128, 1024], F32)
    pt0 = pr.psum(f"{tag}_pt0", [128, 128], BF16)
    pt1 = pr.psum(f"{tag}_pt1", [128, 128], BF16)
    xr = sbuf_ring(pr, f"{tag}_xr", 1, [128, D], F32)
    sr = sbuf_ring(pr, f"{tag}_sr", 1, [128, D], BF16)
    ot = sbuf_ring(pr, f"{tag}_ot", 1, [128, D], F32)
    pv = {"stat": [], "pt0": None, "pt1": None}
    out_h = []

    def h_phase(tt, hb, hb_war):
        hevs = []
        for hc in range(8):
            pb, wp_, sp = ph.next()
            hm = kb.emit("tensor", lambda e, pb=pb, hc=hc, tt=tt: [
                e.matmul(pb[:, :], xnT[:, T * kc + 128 * tt:T * kc + 128 * (tt + 1)],
                         wq_sb[:, H * kc + 512 * hc:H * kc + 512 * (hc + 1)],
                         start=(kc == 0), stop=(kc == 7)) for kc in range(8)][-1],
                deps=wq_ready[hc] + [xnT_dep, wp_])
            he = kb.emit("scalar", lambda e, pb=pb, hc=hc, hb=hb: e.activation(
                hb[:, 512 * hc:512 * (hc + 1)], pb[:, :], AF.Gelu),
                deps=[hm] + (hb_war if hc == 0 else []))
            ph.used(sp, he)
            hevs.append(he)
        return hevs

    def stats(tt, hb, hevs):
        lnhT, wln, sln = lnr.next()
        h1 = kb.emit("vector", lambda e, hb=hb: e.reduce_sum(stat[:, 0:1], hb[:, :], AX),
                     deps=hevs + pv["stat"])
        h2 = kb.emit("scalar", lambda e, hb=hb, lnhT=lnhT: e.activation(
            lnhT[:, :], hb[:, :], AF.Square), deps=hevs + wln)
        h3 = kb.emit("vector", lambda e, lnhT=lnhT: e.reduce_sum(stat[:, 1:2], lnhT[:, :], AX),
                     deps=[h2] + wln)
        h4 = kb.emit("vector", lambda e: e.tensor_scalar_mul(stat[:, 2:3], stat[:, 0:1], 1.0 / H), deps=[h1])
        h5 = kb.emit("vector", lambda e: e.tensor_tensor(stat[:, 3:4], stat[:, 2:3], stat[:, 2:3], ALU.mult), deps=[h4])
        h6 = kb.emit("vector", lambda e: e.tensor_scalar(
            stat[:, 4:5], stat[:, 1:2], 1.0 / H, stat[:, 3:4], ALU.mult, ALU.subtract), deps=[h3, h5])
        h7 = kb.emit("vector", lambda e: e.tensor_scalar_add(stat[:, 5:6], stat[:, 4:5], 1e-5), deps=[h6])
        h8 = kb.emit("scalar", lambda e: e.sqrt(stat[:, 6:7], stat[:, 5:6]), deps=[h7])
        h9 = kb.emit("vector", lambda e: e.reciprocal(stat[:, 7:8], stat[:, 6:7]), deps=[h8])
        hA = kb.emit("vector", lambda e, hb=hb: e.tensor_scalar(
            hb[:, :], hb[:, :], stat[:, 2:3], stat[:, 7:8], ALU.subtract, ALU.mult),
            deps=[h9, h2, h4])
        pv["stat"] = [hA]
        return (lnhT, sln, hA, h2)

    def rest(tt, hb, hbs, st):
        lnhT, sln, hA, h2 = st
        tprev = [pv["pt0"], pv["pt1"]]
        tevs = []
        tT1 = None
        for kc2 in range(32):
            slot = kc2 % 2
            ptx = pt0 if slot == 0 else pt1
            hT1 = kb.emit("tensor", lambda e, kc2=kc2, ptx=ptx, hb=hb: e.transpose(
                ptx[:, :], hb[:, 128 * kc2:128 * (kc2 + 1)], C.ident[:, :]),
                deps=[hA, C.ready, tprev[slot]])
            hT2 = kb.emit("vector", lambda e, kc2=kc2, ptx=ptx, lnhT=lnhT: e.tensor_scalar(
                lnhT[:, 128 * kc2:128 * (kc2 + 1)], ptx[:, :],
                lnwc[:, kc2:kc2 + 1], lnbc[:, kc2:kc2 + 1], ALU.mult, ALU.add),
                deps=[hT1] + ln_ready)
            tprev[slot] = hT2
            tT1 = hT1
            tevs.append(hT2)
        pv["pt0"], pv["pt1"] = tprev
        hbr.used(hbs, [hA, h2, tT1])
        po, wpo, spo = por.next()
        hO = kb.emit("tensor", lambda e, lnhT=lnhT, po=po: [
            e.matmul(po[:, 512 * nh:512 * (nh + 1)], lnhT[:, 128 * kc2:128 * (kc2 + 1)],
                     wp_sb[:, 1024 * kc2 + 512 * nh:1024 * kc2 + 512 * (nh + 1)],
                     start=(kc2 == 0), stop=(kc2 == 31))
            for nh in range(2) for kc2 in range(32)][-1],
            deps=tevs + wp_ready + wpo)
        lnr.used(sln, hO)
        ob, wo_, so = ot.next()
        hsil = kb.emit("scalar", lambda e, ob=ob, po=po: e.activation(ob[:, :], po[:, :], AF.Silu),
                       deps=[hO] + wo_)
        por.used(spo, hsil)
        xb, wx, sx = xr.next()
        hx = kb.dma("scalar", xb[:, :], x_in_ap[128 * tt:128 * (tt + 1), :], deps=[wx])
        sb_, wsr, ssx = sr.next()
        hs = kb.dma("gpsimd", sb_[:, :], s_dram[128 * tt:128 * (tt + 1), :], deps=[wsr])
        hm1 = kb.emit("vector", lambda e, ob=ob, sb_=sb_: e.tensor_tensor(
            ob[:, :], ob[:, :], sb_[:, :], ALU.mult), deps=[hsil, hs])
        hm2 = kb.emit("vector", lambda e, ob=ob, xb=xb: e.tensor_tensor(
            ob[:, :], ob[:, :], xb[:, :], ALU.add), deps=[hm1, hx])
        hw = kb.dma("gpsimd", x_out_ap[128 * tt:128 * (tt + 1), :], ob[:, :], deps=[hm2])
        xr.used(sx, hm2)
        sr.used(ssx, hm1)
        ot.used(so, hw)
        out_h.append(hw)

    prev = None
    for tt in range(NT):
        st = stats(*prev[:2], prev[3]) if prev is not None else None
        hb, hbw, hbs = hbr.next()
        hevs = h_phase(tt, hb, hbw)
        if prev is not None:
            rest(prev[0], prev[1], prev[2], st)
        prev = (tt, hb, hbs, hevs)
    st = stats(*prev[:2], prev[3])
    rest(prev[0], prev[1], prev[2], st)
    pr.pop()
    return out_h


def emit_layer1(pr, C, xnT, x_in, x_out, qbg=None):
    pr.push()
    ln_done = emit_ln(pr, C, x_in, xnT, tag="ln1")
    pr.pop()
    s_dram = pr.dint("sp_s1", [T, D], BF16)
    W = {k: pr.din("L1_attn_" + k,
                   dict(Wq=[1024, 1024], Wk=[1024, 1024], Wv=[1024, 2048],
                        Wg=[1024, 2048], Wgk1=[1024, 16], Wgk2=[16, 1024],
                        bgk2=[1024], Wo=[2048, 1024], rms_w=[512])[k])
         for k in ("Wq", "Wk", "Wv", "Wg", "Wgk1", "Wgk2", "bgk2", "Wo", "rms_w")}
    pr.push()
    emit_gla(pr, C, xnT, ln_done, W,
             dict(DK=1024, DV=2048, HEADS=4, GATE_NORM=16.0), s_dram, "g1")
    pr.pop()
    WF = dict(ln_w=pr.din("L1_ffn_ln_w", [4096]),
              ln_b=pr.din("L1_ffn_ln_b", [4096]))
    if qbg is None:
        WF["W_fcT"] = pr.din("L1_ffn_W_fcT", [1024, 4096])
        WF["W_projT"] = pr.din("L1_ffn_W_projT", [4096, 1024])
    pr.push()
    hs = emit_bitffn(pr, C, xnT, ln_done, WF, x_in, x_out, s_dram, "bf1", qbg=qbg)
    pr.pop()
    return hs


def l0_moe_cfg(pr):
    w1 = pr.din("L0_ffn_w1", [8, 1024, 2048])
    b1 = pr.din("L0_ffn_b1", [8, 2048])
    w2 = pr.din("L0_ffn_w2", [8, 2048, 1024])
    b2 = pr.din("L0_ffn_b2", [8, 1024])
    sw1 = pr.din("L0_ffn_sw1", [2, 1024, 2048])
    sb1 = pr.din("L0_ffn_sb1", [2, 2048])
    sw2 = pr.din("L0_ffn_sw2", [2, 2048, 1024])
    sb2 = pr.din("L0_ffn_sb2", [2, 1024])
    experts = [dict(w1=w1[e], w2=w2[e], b1=b1[e], comb=("col", e)) for e in range(8)]
    experts += [dict(w1=sw1[s], w2=sw2[s], b1=sb1[s], comb=("const", 0.5)) for s in range(2)]
    return dict(router_w=pr.din("L0_ffn_router_w", [1024, 8]),
                router_b=pr.din("L0_ffn_router_b", [8]),
                router_kind="softmax_top2", act=AF.Gelu, experts=experts,
                bias_items=[("comb", b2), ("mean", sb2, 0.5)])


def l2_moe_cfg(pr):
    ew1 = pr.din("L2_ffn_ew1", [8, 1024, 4096])
    eb1 = pr.din("L2_ffn_eb1", [8, 4096])
    ew2 = pr.din("L2_ffn_ew2", [8, 4096, 1024])
    eb2 = pr.din("L2_ffn_eb2", [8, 1024])
    sw1 = pr.din("L2_ffn_sw1", [1024, 4096])
    sb1 = pr.din("L2_ffn_sb1", [4096])
    sw2 = pr.din("L2_ffn_sw2", [4096, 1024])
    sb2 = pr.din("L2_ffn_sb2", [1024])
    experts = []
    for e in range(8):
        for hh in range(2):
            experts.append(dict(w1=ew1[e][:, 2048 * hh:2048 * (hh + 1)],
                                w2=ew2[e][2048 * hh:2048 * (hh + 1), :],
                                b1=eb1[e][2048 * hh:2048 * (hh + 1)],
                                comb=("col", e)))
    for hh in range(2):
        experts.append(dict(w1=sw1[:, 2048 * hh:2048 * (hh + 1)],
                            w2=sw2[2048 * hh:2048 * (hh + 1), :],
                            b1=sb1[2048 * hh:2048 * (hh + 1)],
                            comb=("const", 1.0)))
    return dict(router_w=pr.din("L2_ffn_router_wT", [1024, 8]),
                router_b=None,
                router_kind="sigmoid_top2", act=AF.Relu, experts=experts,
                bias_items=[("comb", eb2), ("mean", sb2.unsqueeze(0), 1.0)])


def emit_layer0(pr, C, xnT, x_in, x_out, bg=None):
    pr.push()
    ln_done = emit_ln(pr, C, x_in, xnT, tag="ln0")
    pr.pop()
    s_dram = pr.dint("sp_s0", [T, D], BF16)
    z_dram = pr.dint("sp_z0", [T, 2048], BF16)
    xsp_dram = pr.dint("sp_xc0", [T, 2048], BF16)
    pr.push()
    ctx = mamba_ctx(pr)
    pr.push()
    if bg:
        bg.open_arena(p2=False)
    emit_mamba_ab(pr, C, xnT, ln_done, z_dram, xsp_dram, ctx, bg=bg)
    if bg:
        bg.drain_p1()
    pr.pop()
    if bg:
        pr.push()
        bg.emit_thresh()
        pr.pop()
    emit_mamba_c(pr, C, s_dram, z_dram, xsp_dram, ctx)
    pr.pop()
    cfg = l0_moe_cfg(pr)
    pr.push()
    if bg:
        bg.open_arena(p2=True)
    hs = emit_moe(pr, C, xnT, ln_done, cfg, x_in, x_out, s_dram, "m0", bg=bg)
    if bg:
        bg.drain()
    pr.pop()
    return hs


def emit_rms_inplace(pr, C, xnT, tag):
    kb = pr
    pr.push()
    sq = pr.sbuf(f"{tag}_sq", [128, T], BF16)
    onec = pr.sbuf(f"{tag}_onec", [128, 1], BF16)
    h0 = kb.emit("vector", lambda e: e.memset(onec[:, :], 1.0))
    nwc = pr.sbuf(f"{tag}_nwc", [128, 8], F32)
    hnw = kb.dma("sync", nwc[:, :], pr.din("L2_ffn_norm_weight", [1024])
                 .rearrange("(c p) -> p c", p=128), allow_slow_non_contiguous=True)
    rrow = pr.sbuf(f"{tag}_rrow", [1, T], F32)
    rbc = pr.sbuf(f"{tag}_rbc", [128, T], BF16)
    onesf = pr.sbuf(f"{tag}_onesf", [1, 128], F32)
    h1 = kb.emit("vector", lambda e: e.memset(onesf[:, :], 1.0))
    pv = pr.psum(f"{tag}_pv", [1, T], F32)
    pb = pr.psum(f"{tag}_pb", [128, 512], F32)
    sq_evs = []
    mm_evs = []
    for kc in range(8):
        hs = kb.emit("scalar", lambda e, kc=kc: e.activation(
            sq[:, :], xnT[:, T * kc:T * (kc + 1)], AF.Square), deps=mm_evs[-1:])
        hm = kb.emit("tensor", lambda e, kc=kc: [
            e.matmul(pv[0:1, 512 * i:512 * (i + 1)], onec[:, :],
                     sq[:, 512 * i:512 * (i + 1)], start=(kc == 0), stop=(kc == 7))
            for i in range(4)][-1], deps=[hs, h0])
        sq_evs.append(hs)
        mm_evs.append(hm)
    h2 = kb.emit("vector", lambda e: e.tensor_scalar(
        rrow[0:1, :], pv[0:1, :], 1.0 / 1024.0, 1e-8, ALU.mult, ALU.add),
        deps=[mm_evs[-1]])
    h3 = kb.emit("scalar", lambda e: e.sqrt(rrow[0:1, :], rrow[0:1, :]), deps=[h2])
    h4 = kb.emit("vector", lambda e: e.reciprocal(rrow[0:1, :], rrow[0:1, :]), deps=[h3])
    bc_evs = []
    for i in range(4):
        hm = kb.emit("tensor", lambda e, i=i: e.matmul(
            pb[:, :], onesf[:, :], rrow[0:1, 512 * i:512 * (i + 1)],
            start=True, stop=True), deps=[h4, h1] + bc_evs[-1:])
        hv = kb.emit("vector", lambda e, i=i: e.tensor_copy(
            rbc[:, 512 * i:512 * (i + 1)], pb[:, :]), deps=[hm])
        bc_evs.append(hv)
    done = []
    for kc in range(8):
        ha = kb.emit("vector", lambda e, kc=kc: e.tensor_tensor(
            xnT[:, T * kc:T * (kc + 1)], xnT[:, T * kc:T * (kc + 1)], rbc[:, :],
            ALU.mult), deps=bc_evs + [sq_evs[-1]])
        hb = kb.emit("vector", lambda e, kc=kc: e.tensor_scalar(
            xnT[:, T * kc:T * (kc + 1)], xnT[:, T * kc:T * (kc + 1)],
            nwc[:, kc:kc + 1], None, ALU.mult), deps=[ha, hnw])
        done.append(hb)
    pr.pop()
    return done


def emit_layer2(pr, C, xnT, x_in, x_out):
    pr.push()
    ln_done = emit_ln(pr, C, x_in, xnT, tag="ln2")
    pr.pop()
    s_dram = pr.dint("sp_s2", [T, D], BF16)
    W = {k: pr.din("L2_attn_" + k,
                   dict(Wq=[1024, 512], Wk=[1024, 512], Wv=[1024, 1024],
                        Wg=[1024, 1024], Wgk1=[1024, 16], Wgk2=[16, 512],
                        bgk2=[512], Wo=[1024, 1024], gw=[256])[k])
         for k in ("Wq", "Wk", "Wv", "Wg", "Wgk1", "Wgk2", "bgk2", "Wo", "gw")}
    W["rms_w"] = W.pop("gw")
    pr.push()
    emit_gla(pr, C, xnT, ln_done, W,
             dict(DK=512, DV=1024, HEADS=4, GATE_NORM=16.0), s_dram, "g2")
    pr.pop()
    rms_done = emit_rms_inplace(pr, C, xnT, "r2")
    cfg = l2_moe_cfg(pr)
    pr.push()
    hs = emit_moe(pr, C, xnT, rms_done, cfg, x_in, x_out, s_dram, "m2")
    pr.pop()
    return hs




def emit_attn3(pr, C, xnT, xnT_dep, W, s_dram, tag="a3"):
    kb = pr
    NH, HD = 16, 64
    cos_d = pr.din("c_rope_cos", [T, D])
    sin_d = pr.din("c_rope_sin", [T, D])
    qTr = pr.sbuf(f"{tag}_qTr", [128, 8 * T], BF16)
    kTr = pr.sbuf(f"{tag}_kTr", [128, 8 * T], BF16)
    vtok = pr.sbuf(f"{tag}_vtok", [128, NT * D], BF16)
    pr.push()
    wsb = sbuf_ring(pr, f"{tag}_w", 2, [128, 8 * 512], BF16)
    pp = psum_ring(pr, f"{tag}_pp", 4, [128, 512], F32)
    ptr = psum_ring(pr, f"{tag}_ptr", 2, [128, 128], BF16)
    qf = pr.sbuf(f"{tag}_qf", [128, D], F32)
    rot = pr.sbuf(f"{tag}_rot", [128, D], F32)
    qb = pr.sbuf(f"{tag}_qb", [128, D], BF16)
    cs = sbuf_ring(pr, f"{tag}_cs", 2, [128, D], F32)
    sn = sbuf_ring(pr, f"{tag}_sn", 2, [128, D], F32)
    pv_qf = []
    pv_rot = []
    pv_qb = []
    tr_prev = [None, None]
    for (wname, dstT, do_rope) in (("WqT", qTr, True), ("WkT", kTr, True),
                                   ("WvT", vtok, False)):
        wslots = []
        for ncol in range(2):
            wb, war, ws = wsb.next()
            lw = kb.dma("gpsimd", wb[:, :].rearrange("p (k c) -> p k c", c=512),
                        W[wname][:, 512 * ncol:512 * (ncol + 1)]
                        .rearrange("(k p) c -> p k c", p=128), deps=[war])
            wslots.append((wb, lw, ws))
        wlast = None
        for tt in range(NT):
            hevs = []
            for ncol in range(2):
                wb, lw, ws = wslots[ncol]
                pb, wp, sp = pp.next()
                hm = kb.emit("tensor", lambda e, pb=pb, wb=wb, tt=tt: [
                    e.matmul(pb[:, :], xnT[:, T * k + 128 * tt:T * k + 128 * (tt + 1)],
                             wb[:, 512 * k:512 * (k + 1)], start=(k == 0), stop=(k == 7))
                    for k in range(8)][-1], deps=[lw, xnT_dep] + wp)
                wlast = hm
                if not do_rope:
                    he = kb.emit("scalar", lambda e, pb=pb, tt=tt, ncol=ncol: e.copy(
                        vtok[:, D * tt + 512 * ncol:D * tt + 512 * (ncol + 1)], pb[:, :]),
                        deps=[hm])
                else:
                    he = kb.emit("vector", lambda e, pb=pb, ncol=ncol: e.tensor_copy(
                        qf[:, 512 * ncol:512 * (ncol + 1)], pb[:, :]),
                        deps=[hm] + (pv_qf if ncol == 0 else []))
                pp.used(sp, he)
                hevs.append(he)
            if not do_rope:
                continue
            hc, wc_, sc_ = cs.next()
            hl1 = kb.dma("sync", hc[:, :], cos_d[128 * tt:128 * (tt + 1), :], deps=wc_)
            hsn, wsn, ssn = sn.next()
            hl2 = kb.dma("sync", hsn[:, :], sin_d[128 * tt:128 * (tt + 1), :], deps=wsn)
            h_r1 = kb.emit("vector", lambda e: e.tensor_scalar_mul(
                rot[:, 0:D:2], qf[:, 1:D:2], -1.0), deps=hevs + pv_rot)
            h_r2 = kb.emit("vector", lambda e: e.tensor_copy(
                rot[:, 1:D:2], qf[:, 0:D:2]), deps=hevs + pv_rot)
            h_r3 = kb.emit("vector", lambda e, hc=hc: e.tensor_tensor(
                qf[:, :], qf[:, :], hc[:, :], ALU.mult), deps=[hl1, h_r1, h_r2])
            h_r4 = kb.emit("vector", lambda e, hsn=hsn: e.tensor_tensor(
                rot[:, :], rot[:, :], hsn[:, :], ALU.mult), deps=[hl2, h_r1, h_r2])
            h_r5 = kb.emit("vector", lambda e: e.tensor_add(qf[:, :], qf[:, :], rot[:, :]),
                           deps=[h_r3, h_r4])
            cs.used(sc_, h_r3)
            sn.used(ssn, h_r4)
            pv_rot = [h_r5]
            h_cb = kb.emit("vector", lambda e: e.tensor_copy(qb[:, :], qf[:, :]),
                           deps=[h_r5] + pv_qb)
            pv_qf = [h_cb]
            tevs = []
            for kc in range(8):
                pb2, wp2, sp2 = ptr.next()
                hT1 = kb.emit("tensor", lambda e, kc=kc, pb2=pb2: e.transpose(
                    pb2[:, :], qb[:, 128 * kc:128 * (kc + 1)], C.ident[:, :]),
                    deps=[h_cb, C.ready] + wp2)
                hT2 = kb.emit("scalar", lambda e, kc=kc, pb2=pb2, dstT=dstT, tt=tt: e.copy(
                    dstT[:, T * kc + 128 * tt:T * kc + 128 * (tt + 1)], pb2[:, :]),
                    deps=[hT1])
                ptr.used(sp2, hT2)
                tevs.append(hT2)
            pv_qb = [tevs[-1]]
        wsb.used(wslots[0][2], wlast)
        wsb.used(wslots[1][2], wlast)
    pr.pop()
    qk_ready = kb.barrier()

    pr.push()
    PTb = pr.sbuf(f"{tag}_PT", [128, T], BF16)
    o_all = pr.sbuf(f"{tag}_oall", [128, D], BF16)
    sumc = pr.sbuf(f"{tag}_sum", [128, 2], F32)
    onec_bf = pr.sbuf(f"{tag}_onecb", [128, 1], BF16)
    h_ob = kb.emit("vector", lambda e: e.memset(onec_bf[:, :], 1.0))
    mm3 = pr.sbuf(f"{tag}_m3", [128, 1], F32)
    h_m3 = kb.emit("vector", lambda e: e.memset(mm3[:, :], -3.0))
    sbf = pr.sbuf(f"{tag}_sbf", [128, D], BF16)
    oT = pr.sbuf(f"{tag}_oT", [128, 8 * 128], BF16)
    wo_sb = pr.sbuf(f"{tag}_wo", [128, 8 * D], BF16)
    lwo = [kb.dma("gpsimd", wo_sb[:, D * kc:D * (kc + 1)],
                  W["WoT"][128 * kc:128 * (kc + 1), :]) for kc in range(8)]
    pS = psum_ring(pr, f"{tag}_pS", 3, [128, 128], F32)
    pSum = pr.psum(f"{tag}_pSum", [128, 8], F32)
    pOh = pr.psum(f"{tag}_pOh", [128, 64], F32)
    pT0 = pr.psum(f"{tag}_pT0", [128, 128], BF16)
    pT1 = pr.psum(f"{tag}_pT1", [128, 128], BF16)
    pW = pr.psum(f"{tag}_pW", [128, 512], F32)
    pv = {"PT": [], "oall": [], "sum": [], "pSum": [], "pOh": [], "pW": [],
          "pT0": None, "pT1": None, "oT": [], "sbf": []}
    for qt in range(NT):
        for h in range(NH):
            kc_h = h // 2
            pof = 64 * (h % 2)
            exp_evs = []
            for kt in range(qt + 1):
                pb, wp, sp = pS.next()
                hm = kb.emit("tensor", lambda e, pb=pb, h=h, qt=qt, kt=kt, kc_h=kc_h, pof=pof: e.matmul(
                    pb[:, :], kTr[pof:pof + 64, T * kc_h + 128 * kt:T * kc_h + 128 * (kt + 1)],
                    qTr[pof:pof + 64, T * kc_h + 128 * qt:T * kc_h + 128 * (qt + 1)],
                    start=True, stop=True), deps=qk_ready + wp)
                if kt == qt:
                    hmask = kb.emit("vector", lambda e, pb=pb: e.tensor_tensor(
                        pb[:, :], pb[:, :], C.maskb[:, :], ALU.add), deps=[hm, C.ready])
                else:
                    hmask = hm
                hx = kb.emit("scalar", lambda e, pb=pb, kt=kt: e.activation(
                    PTb[:, 128 * kt:128 * (kt + 1)], pb[:, :], AF.Exp,
                    bias=mm3[:, :], scale=0.125), deps=[hmask, h_m3] + (pv["PT"] if kt == 0 else []))
                pS.used(sp, hx)
                exp_evs.append(hx)
            hsum = kb.emit("tensor", lambda e, qt=qt: [
                e.matmul(pSum[:, 0:1], PTb[:, 128 * kt:128 * (kt + 1)], onec_bf[:, :],
                         start=(kt == 0), stop=(kt == qt)) for kt in range(qt + 1)][-1],
                deps=exp_evs + [h_ob] + pv["pSum"])
            ho = kb.emit("tensor", lambda e, qt=qt, h=h: [
                e.matmul(pOh[:, :], PTb[:, 128 * kt:128 * (kt + 1)],
                         vtok[:, D * kt + HD * h:D * kt + HD * (h + 1)],
                         start=(kt == 0), stop=(kt == qt)) for kt in range(qt + 1)][-1],
                deps=exp_evs + qk_ready + pv["pOh"])
            hrc = kb.emit("vector", lambda e: e.reciprocal(sumc[:, 0:1], pSum[:, 0:1]),
                          deps=[hsum] + pv["sum"])
            hon = kb.emit("vector", lambda e, h=h: e.tensor_scalar(
                o_all[:, HD * h:HD * (h + 1)], pOh[:, :], sumc[:, 0:1], None, ALU.mult),
                deps=[hrc, ho] + (pv["oall"] if h == 0 else []))
            pv["pSum"] = [hrc]
            pv["pOh"] = [hon]
            pv["sum"] = [hon]
            pv["PT"] = [hsum, ho]
            last_hon = hon
        tprev = [pv["pT0"], pv["pT1"]]
        tevs = []
        for kc in range(8):
            slot = kc % 2
            ptx = pT0 if slot == 0 else pT1
            hT1 = kb.emit("tensor", lambda e, kc=kc, ptx=ptx: e.transpose(
                ptx[:, :], o_all[:, 128 * kc:128 * (kc + 1)], C.ident[:, :]),
                deps=[last_hon, C.ready, tprev[slot]])
            hT2 = kb.emit("scalar", lambda e, kc=kc, ptx=ptx: e.copy(
                oT[:, 128 * kc:128 * (kc + 1)], ptx[:, :]),
                deps=[hT1] + (pv["oT"] if kc < 2 else []))
            tprev[slot] = hT2
            tevs.append(hT2)
        pv["pT0"], pv["pT1"] = tprev
        pv["oall"] = [tevs[-1]]
        hW = kb.emit("tensor", lambda e: [
            e.matmul(pW[:, :], oT[:, 128 * kc:128 * (kc + 1)],
                     wo_sb[:, D * kc + 512 * nh:D * kc + 512 * (nh + 1)],
                     start=(kc == 0), stop=(kc == 7))
            for nh in range(1) for kc in range(8)][-1],
            deps=tevs + lwo + pv["pW"])
        hW2 = kb.emit("tensor", lambda e: [
            e.matmul(pW[:, :], oT[:, 128 * kc:128 * (kc + 1)],
                     wo_sb[:, D * kc + 512:D * kc + 1024],
                     start=(kc == 0), stop=(kc == 7)) for kc in range(8)][-1],
            deps=[]) if False else None
        hWe = kb.emit("scalar", lambda e: e.copy(sbf[:, 0:512], pW[:, :]),
                      deps=[hW] + pv["sbf"])
        hWb = kb.emit("tensor", lambda e: [
            e.matmul(pW[:, :], oT[:, 128 * kc:128 * (kc + 1)],
                     wo_sb[:, D * kc + 512:D * kc + 1024],
                     start=(kc == 0), stop=(kc == 7)) for kc in range(8)][-1],
            deps=tevs + lwo + [hWe])
        hWe2 = kb.emit("scalar", lambda e: e.copy(sbf[:, 512:1024], pW[:, :]),
                       deps=[hWb])
        hSd = kb.dma("sync", s_dram[128 * qt:128 * (qt + 1), :], sbf[:, :], deps=[hWe, hWe2])
        pv["oT"] = [hWb]
        pv["sbf"] = [hSd]
        pv["pW"] = [hWe2]
    pr.pop()
    return None



def emit_lstm(pr, C, xnT, xnT_dep, W, a2a_out, cc_in, cc_out, h_a2a,
              x_in_ap, x_out_ap, s_dram, tag="l3f"):
    kb = pr
    NR = T // NCORES
    xsT = pr.sbuf(f"{tag}_xsT", [128, 8 * T], BF16)
    pr.push()
    xr = sbuf_ring(pr, f"{tag}_xr", 2, [128, D], BF16)
    ptr = psum_ring(pr, f"{tag}_ptr", 2, [128, 128], BF16)
    for tt in range(NT):
        xb, war, ws = xr.next()
        hl = kb.dma("sync", xb[:, :], a2a_out[128 * tt:128 * (tt + 1), :],
                    deps=[h_a2a] + war)
        tevs = []
        for kc in range(8):
            pb2, wp2, sp2 = ptr.next()
            hT1 = kb.emit("tensor", lambda e, kc=kc, pb2=pb2, xb=xb: e.transpose(
                pb2[:, :], xb[:, 128 * kc:128 * (kc + 1)], C.ident[:, :]),
                deps=[hl, C.ready] + wp2)
            hT2 = kb.emit("scalar", lambda e, kc=kc, pb2=pb2, tt=tt: e.copy(
                xsT[:, T * kc + 128 * tt:T * kc + 128 * (tt + 1)], pb2[:, :]),
                deps=[hT1])
            ptr.used(sp2, hT2)
            tevs.append(hT2)
        xr.used(ws, tevs[-1])
    pr.pop()
    xs_ready = kb.barrier()

    pr.push()
    wnames = ("wfT", "ufT", "wiT", "uiT", "wcT", "ucT")
    wsb = {}
    lws = []
    for wn in wnames:
        wsb[wn] = pr.sbuf(f"{tag}_{wn}", [128, 8 * 1024], BF16)
        lws += [kb.dma("gpsimd", wsb[wn][:, 1024 * kc:1024 * (kc + 1)],
                       W[wn][128 * kc:128 * (kc + 1), :]) for kc in range(8)]
    bcols = pr.sbuf(f"{tag}_bc", [128, 8 * 3], F32)
    lb = [kb.dma("sync", bcols[:, 8 * i:8 * (i + 1)],
                 W[bn].rearrange("(c p) -> p c", p=128),
                 allow_slow_non_contiguous=True)
          for i, bn in enumerate(("bf", "bi", "bc_"))]
    icc = pr.sbuf(f"{tag}_icc", [128, 8], F32)
    lic = kb.dma("sync", icc[:, :], W["in_c"].rearrange("(c p) -> p c", p=128),
                 allow_slow_non_contiguous=True)
    cT = pr.sbuf(f"{tag}_cT", [128, 8 * NR], F32)
    cTb = pr.sbuf(f"{tag}_cTb", [128, 8 * NR], BF16)
    fb = pr.sbuf(f"{tag}_fb", [128, 8 * NR], F32)
    ib = pr.sbuf(f"{tag}_ib", [128, 8 * NR], F32)
    gb = pr.sbuf(f"{tag}_gb", [128, 8 * NR], F32)
    pg = psum_ring(pr, f"{tag}_pg", 3, [128, NR], F32)
    init_evs = []
    for kc in range(8):
        h0 = kb.emit("vector", lambda e, kc=kc: e.memset(cT[:, NR * kc:NR * (kc + 1)], 0.0))
        h1 = kb.emit("vector", lambda e, kc=kc: e.tensor_scalar_add(
            cT[:, NR * kc:NR * (kc + 1)], cT[:, NR * kc:NR * (kc + 1)],
            icc[:, kc:kc + 1]), deps=[h0, lic])
        init_evs.append(h1)
    hcb = kb.emit("vector", lambda e: e.tensor_copy(cTb[:, :], cT[:, :]), deps=init_evs)
    pv = {"cT": init_evs, "cTb": [hcb], "fb": [], "ib": [], "gb": []}
    for t in range(NCORES):
        gate_evs = {"fb": [], "ib": [], "gb": []}
        for (gname, wname, uname, brow, actf) in (
                ("fb", "wfT", "ufT", 0, AF.Sigmoid), ("ib", "wiT", "uiT", 1, AF.Sigmoid),
                ("gb", "wcT", "ucT", 2, AF.Tanh)):
            dst = {"fb": fb, "ib": ib, "gb": gb}[gname]
            for oc in range(8):
                pb, wp, sp = pg.next()

                def gmm(e, pb=pb, wname=wname, uname=uname, oc=oc, t=t):
                    for kc in range(8):
                        e.matmul(pb[:, :],
                                 wsb[wname][:, 1024 * kc + 128 * oc:1024 * kc + 128 * (oc + 1)],
                                 xsT[:, T * kc + NR * t:T * kc + NR * (t + 1)],
                                 start=(kc == 0), stop=False)
                    for kc in range(8):
                        i1 = e.matmul(pb[:, :],
                                      wsb[uname][:, 1024 * kc + 128 * oc:1024 * kc + 128 * (oc + 1)],
                                      cTb[:, NR * kc:NR * (kc + 1)],
                                      start=False, stop=(kc == 7))
                    return i1
                hm = kb.emit("tensor", gmm, deps=lws + xs_ready + wp + pv["cTb"])
                he = kb.emit("scalar", lambda e, pb=pb, dst=dst, oc=oc, brow=brow, actf=actf: e.activation(
                    dst[:, NR * oc:NR * (oc + 1)], pb[:, :], actf,
                    bias=bcols[:, 8 * brow + oc:8 * brow + oc + 1]),
                    deps=[hm] + lb + (pv[gname] if oc == 0 else []))
                pg.used(sp, he)
                gate_evs[gname].append(he)
        hc1 = kb.emit("vector", lambda e: e.tensor_tensor(cT[:, :], fb[:, :], cT[:, :], ALU.mult),
                      deps=gate_evs["fb"] + pv["cT"])
        hc2 = kb.emit("vector", lambda e: e.tensor_tensor(ib[:, :], ib[:, :], gb[:, :], ALU.mult),
                      deps=gate_evs["ib"] + gate_evs["gb"])
        hc3 = kb.emit("vector", lambda e: e.tensor_add(cT[:, :], cT[:, :], ib[:, :]),
                      deps=[hc1, hc2])
        hcb = kb.emit("vector", lambda e: e.tensor_copy(cTb[:, :], cT[:, :]),
                      deps=[hc3] + [gate_evs["fb"][-1]])
        pv["cT"] = [hc3]
        pv["cTb"] = [hcb]
        pv["fb"] = [hc1]
        pv["ib"] = [hc3]
        pv["gb"] = [hc2]
    cfin_evs = []
    for kc in range(8):
        hd = kb.dma("sync", cc_in[128 * kc:128 * (kc + 1), :],
                    cT[:, NR * kc:NR * (kc + 1)], deps=pv["cT"])
        cfin_evs.append(hd)
    h_ag = kb.collective("AllGather", [cc_in], [cc_out], deps=cfin_evs)
    pr.pop()

    pr.push()
    woT = pr.sbuf(f"{tag}_woT", [128, 8 * 1024], BF16)
    uoT = pr.sbuf(f"{tag}_uoT", [128, 8 * 1024], BF16)
    lwo = [kb.dma("gpsimd", woT[:, 1024 * kc:1024 * (kc + 1)],
                  W["woT"][128 * kc:128 * (kc + 1), :]) for kc in range(8)]
    luo = [kb.dma("gpsimd", uoT[:, 1024 * kc:1024 * (kc + 1)],
                  W["uoT"][128 * kc:128 * (kc + 1), :]) for kc in range(8)]
    borow = pr.sbuf(f"{tag}_borow", [1, D], BF16)
    lbo = kb.dma("gpsimd", borow[:, :], W["bo"].unsqueeze(0))
    bo_bc = pr.sbuf(f"{tag}_bobc", [128, D], F32)
    pb0 = pr.psum(f"{tag}_pb0", [128, 512], F32)
    bc_evs = []
    for i in range(2):
        hm = kb.emit("tensor", lambda e, i=i: e.matmul(
            pb0[:, :], C.ones_col[:, :], borow[0:1, 512 * i:512 * (i + 1)],
            start=True, stop=True), deps=[lbo, C.ready] + bc_evs[-1:])
        hv = kb.emit("vector", lambda e, i=i: e.tensor_copy(
            bo_bc[:, 512 * i:512 * (i + 1)], pb0[:, :]), deps=[hm])
        bc_evs.append(hv)
    ccr = sbuf_ring(pr, f"{tag}_ccr", 2, [128, 8 * 128], BF16)
    po = psum_ring(pr, f"{tag}_po", 2, [128, 512], F32)
    gt = sbuf_ring(pr, f"{tag}_gt", 2, [128, D], F32)
    xr2 = sbuf_ring(pr, f"{tag}_xr2", 2, [128, D], F32)
    sr2 = sbuf_ring(pr, f"{tag}_sr2", 2, [128, D], BF16)
    out_h = []
    for tt in range(NT):
        cb, wcc, scc = ccr.next()
        lcc = [kb.dma("gpsimd", cb[:, 128 * kc:128 * (kc + 1)],
                      cc_out[1024 * (tt // 2) + 128 * kc:1024 * (tt // 2) + 128 * (kc + 1),
                             128 * (tt % 2):128 * (tt % 2 + 1)],
                      deps=[h_ag] + wcc) for kc in range(8)]
        gbuf, wg, sg = gt.next()
        evs = []
        for nh in range(2):
            pb, wp, sp = po.next()
            def fmm(e, pb=pb, cb=cb, tt=tt, nh=nh):
                for kc in range(8):
                    e.matmul(pb[:, :], xnT[:, T * kc + 128 * tt:T * kc + 128 * (tt + 1)],
                             woT[:, 1024 * kc + 512 * nh:1024 * kc + 512 * (nh + 1)],
                             start=(kc == 0), stop=False)
                for kc in range(8):
                    i1 = e.matmul(pb[:, :], cb[:, 128 * kc:128 * (kc + 1)],
                                  uoT[:, 1024 * kc + 512 * nh:1024 * kc + 512 * (nh + 1)],
                                  start=False, stop=(kc == 7))
                return i1
            hm = kb.emit("tensor", fmm, deps=lwo + luo + lcc + [xnT_dep] + wp)
            hv = kb.emit("vector", lambda e, pb=pb, gbuf=gbuf, nh=nh: e.tensor_tensor(
                gbuf[:, 512 * nh:512 * (nh + 1)], pb[:, :], bo_bc[:, 512 * nh:512 * (nh + 1)],
                ALU.add), deps=[hm] + bc_evs + (wg if nh == 0 else []))
            po.used(sp, hv)
            evs.append(hv)
        ht = kb.emit("scalar", lambda e, gbuf=gbuf: e.activation(gbuf[:, :], gbuf[:, :], AF.Tanh),
                     deps=evs)
        hsl = kb.emit("scalar", lambda e, gbuf=gbuf: e.activation(gbuf[:, :], gbuf[:, :], AF.Silu),
                      deps=[ht])
        ccr.used(scc, hm)
        xb, wx, sx = xr2.next()
        hx = kb.dma("sync", xb[:, :], x_in_ap[128 * tt:128 * (tt + 1), :], deps=[wx])
        sb_, wsr, ssx = sr2.next()
        hs = kb.dma("gpsimd", sb_[:, :], s_dram[128 * tt:128 * (tt + 1), :], deps=[wsr])
        hm1 = kb.emit("vector", lambda e, gbuf=gbuf, sb_=sb_: e.tensor_tensor(
            gbuf[:, :], gbuf[:, :], sb_[:, :], ALU.mult), deps=[hsl, hs])
        hm2 = kb.emit("vector", lambda e, gbuf=gbuf, xb=xb: e.tensor_tensor(
            gbuf[:, :], gbuf[:, :], xb[:, :], ALU.add), deps=[hm1, hx])
        hw = kb.dma("sync", x_out_ap[128 * tt:128 * (tt + 1), :], gbuf[:, :], deps=[hm2])
        gt.used(sg, hw)
        xr2.used(sx, hm2)
        sr2.used(ssx, hm1)
        out_h.append(hw)
    pr.pop()
    return out_h


def emit_layer3(pr, C, xnT, x_in, x_out):
    a2a_in = pr.dint("cc_a2a_in", [T, D], BF16)
    a2a_out = pr.dint("cc_a2a_out", [T, D], BF16)
    cc_in = pr.dint("cc_ag_in", [D, T // NCORES], F32)
    cc_out = pr.dint("cc_ag_out", [D * NCORES, T // NCORES], F32)
    pr.push()
    ln_done = emit_ln(pr, C, x_in, xnT, xn_dram=a2a_in, tag="ln3")
    pr.pop()
    h_a2a = pr.collective("AllToAll", [a2a_in], [a2a_out], deps=ln_done)
    s_dram = pr.dint("sp_s3", [T, D], BF16)
    W = {}
    for nm in ("Wq", "Wk", "Wv", "Wo"):
        W[nm + "T"] = pr.din(f"L3_attn_{nm}T", [1024, 1024])
    pr.push()
    emit_attn3(pr, C, xnT, ln_done, W, s_dram)
    pr.pop()
    WF = {}
    for nm in ("wfT", "ufT", "wiT", "uiT", "wcT", "ucT", "woT", "uoT"):
        WF[nm] = pr.din(f"L3_ffn_{nm}", [1024, 1024])
    for nm in ("bf", "bi", "bo", "in_c"):
        WF[nm] = pr.din(f"L3_ffn_{nm}", [1024])
    WF["bc_"] = pr.din("L3_ffn_bc", [1024])
    hs = emit_lstm(pr, C, xnT, ln_done, WF, a2a_out, cc_in, cc_out, h_a2a,
                   x_in, x_out, s_dram)
    return hs


def build_program(layers, dbg_outputs=()):
    nc = bass.Bass("TRN2", target_bir_lowering=False)
    es = ExitStack()
    with es:
        kb = KB(nc)
        pr = Prog(nc, es, kb)
        x_in = pr.din("x_in", [T, D], F32)
        x_out = pr.dout("x_out", [T, D], F32)
        C = Consts(pr)
        xnT = pr.sbuf("xnT", [128, 8 * T], BF16)

        if layers == ["gla1"]:
            pr.push()
            ln_done = emit_ln(pr, C, x_in, xnT, tag="ln1")
            pr.pop()
            s_dram = pr.dout("dbg_s", [T, D], BF16)
            W = {k: pr.din("L1_attn_" + k,
                           dict(Wq=[1024, 1024], Wk=[1024, 1024], Wv=[1024, 2048],
                                Wg=[1024, 2048], Wgk1=[1024, 16], Wgk2=[16, 1024],
                                bgk2=[1024], Wo=[2048, 1024], rms_w=[512])[k])
                 for k in ("Wq", "Wk", "Wv", "Wg", "Wgk1", "Wgk2", "bgk2", "Wo", "rms_w")}
            pr.push()
            emit_gla(pr, C, xnT, ln_done, W,
                     dict(DK=1024, DV=2048, HEADS=4, GATE_NORM=16.0), s_dram, "g1")
            pr.pop()
            xr0 = sbuf_ring(pr, "cp0", 2, [128, D], F32)
            for t in range(NT):
                xb, w, s = xr0.next()
                hl = pr.dma("sync", xb[:, :], x_in[128 * t:128 * (t + 1), :], deps=[w])
                hs = pr.dma("gpsimd", x_out[128 * t:128 * (t + 1), :], xb[:, :], deps=[hl])
                xr0.used(s, hs)
            layers = None
        if layers == [0, 1, 2, 3]:
            xa = pr.dint("x_mid_a", [T, D], F32)
            xb2 = pr.dint("x_mid_b", [T, D], F32)
            emit_layer0(pr, C, xnT, x_in, xa)
            emit_layer1(pr, C, xnT, xa, xb2)
            emit_layer2(pr, C, xnT, xb2, xa)
            emit_layer3(pr, C, xnT, xa, x_out)
            layers = None
        if layers == [0]:
            emit_layer0(pr, C, xnT, x_in, x_out)
            layers = None
        if layers == [1]:
            emit_layer1(pr, C, xnT, x_in, x_out)
            layers = None
        if layers == [2]:
            emit_layer2(pr, C, xnT, x_in, x_out)
            layers = None
        if layers == [3]:
            emit_layer3(pr, C, xnT, x_in, x_out)
            layers = None
        if layers == ["mamba0"]:
            pr.push()
            ln_done = emit_ln(pr, C, x_in, xnT)
            pr.pop()
            s_dram = pr.dout("dbg_s", [T, D], BF16)
            z_dram = pr.dint("sp_z", [T, 2048], BF16)
            xsp_dram = pr.dint("sp_x", [T, 2048], BF16)
            pr.push()
            emit_mamba(pr, C, xnT, ln_done, s_dram, z_dram, xsp_dram)
            pr.pop()
            xr0 = sbuf_ring(pr, "cp0", 2, [128, D], F32)
            for t in range(NT):
                xb, w, s = xr0.next()
                hl = pr.dma("sync", xb[:, :], x_in[128 * t:128 * (t + 1), :], deps=[w])
                hs = pr.dma("gpsimd", x_out[128 * t:128 * (t + 1), :], xb[:, :], deps=[hl])
                xr0.used(s, hs)
            layers = None
        if layers == ["moe0"]:
            pr.push()
            ln_done = emit_ln(pr, C, x_in, xnT)
            pr.pop()
            cfg = l0_moe_cfg(pr)
            dbg = pr.dout("dbg_g", [T, D], F32)
            pr.push()
            emit_moe(pr, C, xnT, ln_done, cfg, x_in, x_out, None, "m0", dbg_g=dbg)
            pr.pop()
            xr0 = sbuf_ring(pr, "cp0", 2, [128, D], F32)
            for t in range(NT):
                xb, w, s = xr0.next()
                hl = pr.dma("sync", xb[:, :], x_in[128 * t:128 * (t + 1), :], deps=[w])
                hs = pr.dma("gpsimd", x_out[128 * t:128 * (t + 1), :], xb[:, :], deps=[hl])
                xr0.used(s, hs)
            layers = None
        if layers == []:
            ln_done = emit_ln(pr, C, x_in, xnT)
            xr = sbuf_ring(pr, "cp", 2, [128, D], F32)
            for t in range(NT):
                xb, w, s = xr.next()
                hl = pr.dma("sync", xb[:, :], x_in[128 * t:128 * (t + 1), :], deps=[w])
                hs = pr.dma("gpsimd", x_out[128 * t:128 * (t + 1), :], xb[:, :], deps=[hl])
                xr.used(s, hs)
            if "xnT" in dbg_outputs:
                d = pr.dout("dbg_xnT", [128, 8 * T], BF16)
                pr.dma("sync", d, xnT[:, :], deps=ln_done)

        kb.emit("sync", lambda e: e.nop(), deps=kb.barrier())

        block = es.enter_context(nc.Block())
        kb.finalize(block, es)
    return nc


def _rope_tables():
    hd = 64
    inv = 1.0 / (10000.0 ** (np.arange(0, hd, 2, dtype=np.float32) / hd))
    pos = np.arange(T, dtype=np.float32)
    emb = np.repeat(pos[:, None] * inv[None, :], 2, axis=-1)
    cos = np.tile(np.cos(emb), (1, 16)).astype(np.float32)
    sin = np.tile(np.sin(emb), (1, 16)).astype(np.float32)
    return cos, sin


def _layer_inputs(L, inputs):
    f32 = lambda a: np.ascontiguousarray(a, dtype=np.float32)
    m = dict(Consts.host_inputs())
    if L == 0:
        for k in inputs:
            if k.startswith("L0_") and k != "L0_attn_hidden_states":
                m[k] = f32(inputs[k])
    elif L == 1:
        for k in inputs:
            if k.startswith("L1_attn_"):
                m[k] = f32(inputs[k])
        m["L1_ffn_W_fcT"] = f32(inputs["L1_ffn_W_fc"].T)
        m["L1_ffn_W_projT"] = f32(inputs["L1_ffn_W_proj"].T)
        m["L1_ffn_ln_w"] = f32(inputs["L1_ffn_ln_w"])
        m["L1_ffn_ln_b"] = f32(inputs["L1_ffn_ln_b"])
    elif L == 2:
        for k in inputs:
            if k.startswith("L2_attn_") or k.startswith("L2_ffn_"):
                m[k] = f32(inputs[k])
        m["L2_ffn_router_wT"] = f32(inputs["L2_ffn_router_w"].T)
        del m["L2_ffn_router_w"]
    else:
        cos, sin = _rope_tables()
        m["c_rope_cos"] = cos
        m["c_rope_sin"] = sin
        for nm in ("Wq", "Wk", "Wv", "Wo"):
            m[f"L3_attn_{nm}T"] = f32(inputs[f"L3_attn_{nm}"].T)
        for nm in ("wf", "uf", "wi", "ui", "wc", "uc", "wo", "uo"):
            m[f"L3_ffn_{nm}T"] = f32(inputs[f"L3_ffn_{nm}"].T)
        for nm in ("bf", "bi", "bc", "bo", "in_c"):
            m[f"L3_ffn_{nm}"] = f32(inputs[f"L3_ffn_{nm}"])
    return m


_prog_cache = {}


def _get_program(key):
    if key not in _prog_cache:
        _prog_cache[key] = build_program(list(key))
    return _prog_cache[key]


FUSED = True


def kernel(**inputs):
    x = np.ascontiguousarray(inputs["L0_attn_hidden_states"], dtype=np.float32)
    if FUSED:
        nc = _get_program((0, 1, 2, 3))
        wmap = {}
        for L in range(4):
            wmap.update(_layer_inputs(L, inputs))
        in_maps = []
        for b in range(NCORES):
            m = dict(wmap)
            m["x_in"] = x[b]
            in_maps.append(m)
        res = run_bass_kernel_spmd(nc, in_maps, core_ids=list(range(NCORES)))
        return np.stack([res.results[b]["x_out"] for b in range(NCORES)], axis=0)
    for L in range(4):
        nc = _get_program((L,))
        wmap = _layer_inputs(L, inputs)
        in_maps = []
        for b in range(NCORES):
            m = dict(wmap)
            m["x_in"] = x[b]
            in_maps.append(m)
        res = run_bass_kernel_spmd(nc, in_maps, core_ids=list(range(NCORES)))
        x = np.stack([res.results[b]["x_out"] for b in range(NCORES)], axis=0)
    return x
